# version 46
# baseline (speedup 1.0000x reference)
import numpy as np
import ml_dtypes
import concourse.bass as bass
import concourse.mybir as mybir
from concourse.bass_utils import run_bass_kernel_spmd
from contextlib import ExitStack

F32 = mybir.dt.float32
BF16 = mybir.dt.bfloat16
I8 = mybir.dt.int8
I32 = mybir.dt.int32
AF = mybir.ActivationFunctionType
ALU = mybir.AluOpType
AXX = mybir.AxisListType.X

D = 1024
DI = 2048
DS = 16
KC = D // 128
CC = DI // 128
EPS_LN = 1e-5
EPS_Q = 1e-6


class Stream:
    def __init__(self, name, step=1):
        self.name = name
        self.step = step
        self.n = 0

    def inc(self, k=None):
        k = self.step if k is None else k
        self.n += k
        return (self.name, k)

    def now(self):
        return (self.name, self.n)


class Plan:
    def __init__(self):
        self.items = []
        self.streams = {}

    def S(self, name, step=1):
        if name not in self.streams:
            self.streams[name] = Stream(name, step)
        return self.streams[name]

    def emit(self, eng, f, waits=(), incs=()):
        self.items.append((eng, tuple(waits), tuple(incs), f))

    def drain(self, eng):
        self.items.append((eng, (), (), "drain"))

    def run(self, engname, engobj, sems):
        for (eng, waits, incs, f) in self.items:
            if eng != engname:
                continue
            if f == "drain":
                engobj.drain()
                continue
            for (s, vv) in waits:
                if vv > 0:
                    engobj.wait_ge(sems[s], vv)
            if f is None:
                continue
            inst = f(engobj)
            for (s, k) in incs:
                inst.then_inc(sems[s], k)


def build_program(n_cores, T_core, T_tile, upto=5):
    assert T_core % T_tile == 0 and T_tile % 128 == 0
    n_tiles = T_core // T_tile
    n_sub = T_tile // 128
    nc = bass.Bass("TRN2", target_bir_lowering=False)

    dx = nc.dram_tensor("x", [T_core, D], F32, kind="ExternalInput").ap()
    dxh = nc.dram_tensor("xhalo", [3, D], F32, kind="ExternalInput").ap()
    dwin = nc.dram_tensor("w_in", [D, 2 * DI], F32, kind="ExternalInput").ap()
    dconvw = nc.dram_tensor("conv_w_p", [128, CC * 4], F32, kind="ExternalInput").ap()
    dconvb = nc.dram_tensor("conv_b_p", [128, CC], F32, kind="ExternalInput").ap()
    dwB = nc.dram_tensor("w_B", [DI, DS], F32, kind="ExternalInput").ap()
    dwC = nc.dram_tensor("w_C", [DS, DI], F32, kind="ExternalInput").ap()
    dwout = nc.dram_tensor("w_out", [DI, D], F32, kind="ExternalInput").ap()
    dwq = nc.dram_tensor("wq_t", [D, D], I8, kind="ExternalInput").ap()
    dwsb = nc.dram_tensor("w_scale_b", [128, D], F32, kind="ExternalInput").ap()
    dbb = nc.dram_tensor("bias_b", [128, D], F32, kind="ExternalInput").ap()
    dhm = nc.dram_tensor("halo_mask", [128, 3], F32, kind="ExternalInput").ap()
    dsel = nc.dram_tensor("sel_b", [128, DS], BF16, kind="ExternalInput").ap()
    ddecay = nc.dram_tensor("decay", [DS, T_core], F32, kind="ExternalInput").ap()
    didf = nc.dram_tensor("ident_f", [128, 128], F32, kind="ExternalInput").ap()
    didb = nc.dram_tensor("ident_b", [128, 128], BF16, kind="ExternalInput").ap()
    dout = nc.dram_tensor("out", [T_core, D], F32, kind="ExternalOutput").ap()
    dx2 = nc.dram_tensor("x2_sp", [T_core, D], F32).ap()
    dysh = nc.dram_tensor("ysp_h", [T_core // 128, 128, CC, 128],
                          BF16).ap()
    dysl = nc.dram_tensor("ysp_l", [T_core // 128, 128, CC, 128],
                          BF16).ap()
    dcci = nc.dram_tensor("cc_in", [DS, 1], F32)
    dcco = nc.dram_tensor("cc_out", [DS * n_cores, 1], F32)
    dhlc = nc.dram_tensor("hloc_sp", [DS, T_core], F32).ap()

    P = Plan()
    es = ExitStack()

    def sbt(name, shape, dt):
        return es.enter_context(nc.sbuf_tensor(name, shape, dt))

    def pst(name, shape, dt):
        return es.enter_context(nc.psum_tensor(name, shape, dt))

    with es:
        wq_b = sbt("wq_b", [128, KC * D], BF16)
        wsb = sbt("wsb", [128, D], F32)
        biasb = sbt("biasb", [128, D], F32)
        wxh = sbt("wxh", [128, KC * DI], BF16)
        wxl = sbt("wxl", [128, KC * DI], BF16)
        wbh = sbt("wbh", [128, CC * DS], BF16)
        wbl = sbt("wbl", [128, CC * DS], BF16)
        wch = sbt("wch", [DS, DI], BF16)
        wcl = sbt("wcl", [DS, DI], BF16)
        convw = sbt("convw", [128, CC * 4], F32)
        convb = sbt("convb", [128, CC], F32)
        selb = sbt("selb", [128, DS], BF16)
        identb = sbt("identb", [128, 128], BF16)
        halom = sbt("halom", [128, 3], F32)
        halobuf = sbt("halobuf", [128, CC * 3], F32)
        epsln = sbt("epsln", [128, 1], F32)
        u_t = sbt("u_t", [DS, T_tile], F32)
        d09 = sbt("d09", [DS, T_tile], F32)
        hl_tmp = [sbt(f"hl_tmp{i}", [DS, T_tile], F32) for i in range(2)]
        hh_t = sbt("hh_t", [DS, T_tile], BF16)
        hl_t = sbt("hl_t", [DS, T_tile], BF16)
        decay_t = d09
        htmp = u_t
        hin = sbt("hin", [DS, 1], F32)
        gbuf = sbt("gbuf", [128, 1], F32)
        ghl = sbt("ghl", [128, 2], BF16)

        bufx = [sbt(f"bufx{i}", [128, D], F32) for i in range(2)]
        bufx2 = [sbt(f"bufx2{i}", [128, D], F32) for i in range(2)]
        sqS = sbt("sqS", [128, D], F32)
        sq_guard = [("ss", 0)]
        ZS = []
        for zi in range(2):
            ZS.append(dict(
                scrA=sbt(f"scrA{zi}", [128, D], F32),
                scrB=sbt(f"scrB{zi}", [128, D], F32),
                scrGE=sbt(f"scrGE{zi}", [128, D], BF16),
                scrLE=sbt(f"scrLE{zi}", [128, D], BF16),
                qbf=sbt(f"qbf{zi}", [128, D], BF16),
                xqT=sbt(f"xqT{zi}", [128, KC * 128], BF16),
            ))
        xnTh = sbt("xnTh", [128, KC * T_tile], BF16)
        xnTl = sbt("xnTl", [128, KC * T_tile], BF16)
        xnTh2 = sbt("xnTh2", [128, KC * T_tile], BF16)
        xnTl2 = sbt("xnTl2", [128, KC * T_tile], BF16)
        ybh1 = xnTh2
        ybl1 = xnTl2
        wstage = ZS[0]["scrB"]
        wstage2 = ZS[1]["scrB"]
        wqi8 = ZS[0]["scrGE"][:, 0:D // 2].bitcast(I8)
        wcstage = wstage[0:DS, 0:D]
        wbstage = wstage[:, 0:CC * DS]
        xp = [sbt(f"xp{i}", [128, T_tile + 3], F32) for i in range(2)]
        xcf = sbt("xcf", [128, T_tile], F32)
        xcg = sbt("xcg", [128, T_tile], F32)
        xch = [sbt(f"xch{i}", [128, T_tile], BF16) for i in range(2)]
        xcl = [sbt(f"xcl{i}", [128, T_tile], BF16) for i in range(2)]
        gk = [xp[0][:, 0:T_tile], xp[1][:, 0:T_tile]]
        yk = xcf
        xcg_b = xcg[:].bitcast(BF16)
        ykh = [xcg_b[:, 0:T_tile], xch[0][:]]
        ykl = [xcg_b[:, T_tile:2 * T_tile], xcl[0][:]]
        for zi in range(2):
            ZS[zi]["stats"] = sbt(f"stats{zi}", [128, 16], F32)
            ZS[zi]["vtail"] = ("vv", 0)
            ZS[zi]["stail"] = ("ss", 0)
            ZS[zi]["sq_done"] = ("ss", 0)
        pb = [pst(f"pb{i}", [128, 512], F32) for i in range(8)]
        pu = pb[7][0:DS, :]

        sem_names = ["dhs", "dhb", "dw2",
                     "dw", "dcc", "ddc", "dhl",
                     "dxa0", "dxa1", "dsp0", "dsp1",
                     "dyh0", "dyh1", "dyl0", "dyl1",
                     "dyb0", "dyb1", "dot0", "dot1",
                     "vv", "ss", "tt", "gg"]
        sems = {n: es.enter_context(nc.semaphore(n)) for n in sem_names}

        sw = P.S("dw", 16)
        sw2 = P.S("dw2", 16)
        shs = P.S("dhs", 16)
        shb = P.S("dhb", 16)
        scc = P.S("dcc", 16)
        sdec = P.S("ddc", 16)
        shl = P.S("dhl", 16)
        sxp = [P.S("dxa0", 16), P.S("dxa1", 16)]
        sspp = [P.S("dsp0", 16), P.S("dsp1", 16)]
        syph = [P.S("dyh0", 16), P.S("dyh1", 16)]
        sypl = [P.S("dyl0", 16), P.S("dyl1", 16)]
        sybp = [P.S("dyb0", 16), P.S("dyb1", 16)]
        sybl2 = [P.S("dyl0", 16), P.S("dyl1", 16)]
        sotp = [P.S("dot0", 16), P.S("dot1", 16)]
        vev = P.S("vv")
        sev = P.S("ss")
        tmm = P.S("tt")
        gev = P.S("gg")

        def v(f, waits=(), incs=()):
            P.emit("vector", f, waits, incs)

        def a(f, waits=(), incs=()):
            P.emit("scalar", f, waits, incs)

        def t(f, waits=(), incs=()):
            P.emit("tensor", f, waits, incs)

        def g(f, waits=(), incs=()):
            P.emit("gpsimd", f, waits, incs)

        def sy(f, waits=(), incs=()):
            P.emit("sync", f, waits, incs)

        def dma(emitfn, f, stream, waits=()):
            emitfn(f, waits=[stream.now()] + list(waits), incs=[stream.inc()])

        def dma_nc(emitfn, f, stream, waits=()):
            emitfn(f, waits=list(waits), incs=[stream.inc()])

        v(lambda e: e.memset(epsln[:], EPS_LN), incs=[vev.inc()])
        v(lambda e: e.memset(d09[:], 0.9), incs=[vev.inc()])
        v(lambda e: e.memset(halobuf[:], 0.0), incs=[vev.inc()])
        v(lambda e: e.memset(gbuf[:], 0.0), incs=[vev.inc()])
        P.drain("vector")

        for (dst, src) in ((convw, dconvw), (convb, dconvb), (wsb, dwsb),
                           (biasb, dbb), (selb, dsel),
                           (identb, didb), (halom, dhm)):
            dma(sy, lambda e, dst=dst, src=src: e.dma_start(out=dst[:],
                                                            in_=src[:]), sw)
        dma(sy, lambda e: e.dma_start(
            out=wbstage.rearrange("p (c s) -> p c s", c=CC),
            in_=dwB[:].rearrange("(c p) s -> p c s", p=128)), sw)
        a(lambda e: e.activation(out=wbh[:], in_=wbstage, func=AF.Copy),
          waits=[sw.now()], incs=[sev.inc()])
        v(lambda e: e.tensor_sub(wbl[:], wbstage, wbh[:]),
          waits=[sev.now()], incs=[vev.inc()])
        for h in range(2):
            dma(sy, lambda e, h=h: e.dma_start(
                out=wcstage, in_=dwC[:, h * D:(h + 1) * D]), sw,
                [sev.now(), vev.now()])
            a(lambda e, h=h: e.activation(out=wch[:, h * D:(h + 1) * D],
                                          in_=wcstage, func=AF.Copy),
              waits=[sw.now()], incs=[sev.inc()])
            v(lambda e, h=h: e.tensor_sub(wcl[:, h * D:(h + 1) * D], wcstage,
                                          wch[:, h * D:(h + 1) * D]),
              waits=[sev.now()], incs=[vev.inc()])
        for k in range(KC):
            wts = [sev.now()] if k > 0 else []
            dma(sy, lambda e, k=k: e.dma_start(
                out=wqi8[:], in_=dwq[k * 128:(k + 1) * 128, :]), sw, wts)
            a(lambda e, k=k: e.activation(
                out=wq_b[:, k * D:(k + 1) * D], in_=wqi8[:], func=AF.Copy),
              waits=[sw.now()], incs=[sev.inc()])

        def load_split_big(chunks, wait_first=(), stages=None,
                           copy_first=()):
            if stages is None:
                stages = [wstage[:, 0:D], wstage2[:, 0:D],
                          ZS[0]["scrA"][:, 0:D], ZS[1]["scrA"][:, 0:D]]
            marks = [list(wait_first) for _ in stages]
            i = 0
            pieces = []
            for (off0, width0, src0) in chunks:
                nh = (width0 + D - 1) // D
                for h in range(nh):
                    pieces.append((off0 + h * D,
                                   min(D, width0 - h * D),
                                   src0[:, h * D:h * D + min(D, width0 - h * D)]))
            for (off, width, src) in pieces:
                stg = stages[i % len(stages)]
                swk = sw if i % 2 == 0 else sw2
                dma(sy, lambda e, off=off, width=width, src=src,
                    stg=stg: e.dma_start(
                    out=stg[:, :width] if False else stg[0:128, 0:width],
                    in_=src), swk, marks[i % len(stages)])
                a(lambda e, off=off, width=width, stg=stg: e.activation(
                    out=wxh[:, off:off + width], in_=stg[0:128, 0:width],
                    func=AF.Copy),
                  waits=[swk.now()] + (list(copy_first)
                                       if i < len(stages) else []),
                  incs=[sev.inc()])
                v(lambda e, off=off, width=width, stg=stg: e.tensor_sub(
                    wxl[:, off:off + width], stg[0:128, 0:width],
                    wxh[:, off:off + width]),
                  waits=[sev.now()], incs=[vev.inc()])
                marks[i % len(stages)] = [sev.now(), vev.now()]
                i += 1

        prep_done = [sev.now(), vev.now(), sw.now()]
        xnt2f = xnTh2[:].bitcast(F32)
        xnt2g = xnTl2[:].bitcast(F32)
        load_split_big([(k * DI, DI, dwin[k * 128:(k + 1) * 128, 0:DI])
                        for k in range(KC)],
                       wait_first=[sev.now(), vev.now()],
                       stages=[xnt2f[:, 0:D], xnt2f[:, D:2 * D],
                               xnt2g[:, 0:D], xnt2g[:, D:2 * D]])

        def vc(f, waits=()):
            v(f, waits=waits, incs=[vev.inc()])
            return vev.now()

        def ac(f, waits=()):
            a(f, waits=waits, incs=[sev.inc()])
            return sev.now()

        def run_gen(gn):
            for _ in gn:
                pass

        def duet(*gens_in):
            gens = list(gens_in)
            while gens:
                nxt = []
                for gn in gens:
                    try:
                        next(gn)
                        nxt.append(gn)
                    except StopIteration:
                        pass
                gens = nxt

        def gen_ln(xv, ntok, need_quant, waits0, Z, res):
            scrA = Z["scrA"]; stats = Z["stats"]
            w0 = list(waits0) + [Z["vtail"], Z["stail"]]
            sq_done = ac(lambda e: e.activation(out=sqS[0:ntok, :], in_=xv,
                                                func=AF.Square,
                                                accum_out=stats[0:ntok, 1:2]),
                         waits=w0 + [sq_guard[0]])
            sq_guard[0] = sq_done
            e1 = vc(lambda e: e.tensor_reduce(out=stats[0:ntok, 0:1], in_=xv,
                                              axis=AXX, op=ALU.add),
                    waits=w0)
            yield
            e2 = vc(lambda e: e.tensor_scalar(
                out=stats[0:ntok, 2:3], in0=stats[0:ntok, 0:1],
                scalar1=-1.0 / D, scalar2=None, op0=ALU.mult), waits=[e1])
            yield
            e3 = vc(lambda e: e.tensor_mul(stats[0:ntok, 3:4],
                                           stats[0:ntok, 2:3],
                                           stats[0:ntok, 2:3]), waits=[e2])
            yield
            e4 = vc(lambda e: e.tensor_scalar(
                out=stats[0:ntok, 4:5], in0=stats[0:ntok, 1:2],
                scalar1=1.0 / D, scalar2=stats[0:ntok, 3:4],
                op0=ALU.mult, op1=ALU.subtract), waits=[e3, sq_done])
            sq2 = ac(lambda e: e.activation(out=stats[0:ntok, 5:6],
                                            in_=stats[0:ntok, 4:5],
                                            func=AF.Sqrt,
                                            bias=epsln[0:ntok, :]),
                     waits=[e4])
            yield
            if need_quant:
                e7 = vc(lambda e: e.tensor_scalar(
                    out=scrA[0:ntok, :], in0=xv, scalar1=stats[0:ntok, 2:3],
                    scalar2=None, op0=ALU.add), waits=[e2])
                yield
            e6 = vc(lambda e: e.reciprocal(out=stats[0:ntok, 6:7],
                                           in_=stats[0:ntok, 5:6]),
                    waits=[sq2])
            yield
            if not need_quant:
                Z["vtail"] = e6
                Z["stail"] = sq2
                res["ln"] = e6
                return
            e8 = vc(lambda e: e.tensor_reduce(out=stats[0:ntok, 7:8],
                                              in_=scrA[0:ntok, :], axis=AXX,
                                              op=ALU.max,
                                              apply_absolute_value=True),
                    waits=[e7])
            yield
            e9 = vc(lambda e: e.tensor_mul(stats[0:ntok, 8:9],
                                           stats[0:ntok, 7:8],
                                           stats[0:ntok, 6:7]),
                    waits=[e8, e6])
            yield
            e10 = vc(lambda e: e.tensor_scalar(
                out=stats[0:ntok, 9:10], in0=stats[0:ntok, 8:9],
                scalar1=EPS_Q, scalar2=1.0 / 127.0,
                op0=ALU.max, op1=ALU.mult), waits=[e9])
            yield
            e13 = vc(lambda e: e.tensor_scalar(
                out=stats[0:ntok, 13:14], in0=stats[0:ntok, 9:10],
                scalar1=0.5, scalar2=None, op0=ALU.mult), waits=[e10])
            yield
            e11 = vc(lambda e: e.reciprocal(out=stats[0:ntok, 11:12],
                                            in_=stats[0:ntok, 9:10]),
                     waits=[e10])
            yield
            e12 = vc(lambda e: e.tensor_mul(stats[0:ntok, 12:13],
                                            stats[0:ntok, 6:7],
                                            stats[0:ntok, 11:12]),
                     waits=[e11])
            Z["vtail"] = e12
            Z["stail"] = sq2
            res["ln"] = e12

        def emit_ln(xv, ntok, need_quant, waits0, Z):
            res = {}
            run_gen(gen_ln(xv, ntok, need_quant, waits0, Z, res))
            return res["ln"]

        def gen_quant(ntok, Z, R, lnq, res):
            scrA = Z["scrA"]; scrB = Z["scrB"]
            scrGE = Z["scrGE"]; scrLE = Z["scrLE"]
            qbf = Z["qbf"]; stats = Z["stats"]
            q1 = ac(lambda e: e.activation(
                out=scrB[0:ntok, :], in_=scrA[0:ntok, :], func=AF.Copy,
                scale=stats[0:ntok, 12:13]), waits=[lnq])
            q2 = vc(lambda e: e.tensor_copy(out=scrA[0:ntok, :].bitcast(I32),
                                            in_=scrB[0:ntok, :]),
                    waits=[q1])
            yield
            q3 = vc(lambda e: e.tensor_copy(out=qbf[0:ntok, :],
                                            in_=scrA[0:ntok, :].bitcast(I32)),
                    waits=[q2, R["qbf"]])
            yield
            q4 = vc(lambda e: e.tensor_sub(scrB[0:ntok, :], scrB[0:ntok, :],
                                           qbf[0:ntok, :]), waits=[q3])
            yield
            qGE = vc(lambda e: e.tensor_scalar(
                out=scrGE[0:ntok, :], in0=scrB[0:ntok, :], scalar1=0.5,
                scalar2=None, op0=ALU.is_ge), waits=[q4, R["xn2"]])
            yield
            qLE = vc(lambda e: e.tensor_scalar(
                out=scrLE[0:ntok, :], in0=scrB[0:ntok, :], scalar1=-0.5,
                scalar2=None, op0=ALU.is_le), waits=[q4])
            yield
            q7 = vc(lambda e: e.tensor_sub(scrB[0:ntok, :], scrGE[0:ntok, :],
                                           scrLE[0:ntok, :]),
                    waits=[qGE, qLE])
            yield
            q8 = vc(lambda e: e.tensor_add(qbf[0:ntok, :], qbf[0:ntok, :],
                                           scrB[0:ntok, :]),
                    waits=[q7])
            Z["vtail"] = q8
            res["qd"] = q8

        bank_free = {i: ("vv", 0) for i in range(8)}
        RS = [{"xn2": ("tt", 0), "qbf": ("tt", 0), "xqT": ("tt", 0)}
              for _ in range(2)]

        def gen_ffn_part1(xbuf, ntok, x_wait, Z, R, res):
            r = {}
            yield from gen_ln(xbuf[0:ntok, :], ntok, True, x_wait, Z, r)
            yield from gen_quant(ntok, Z, R, r["ln"], res)

        def emit_ffn_part1(xbuf, ntok, x_wait, Z, R):
            res = {}
            run_gen(gen_ffn_part1(xbuf, ntok, x_wait, Z, R, res))
            return res["qd"]

        def emit_ffn_part2a(xbuf, ntok, qdone, Z, R, b_q=0, b_h=(1, 2)):
            qbf = Z["qbf"]; xqT = Z["xqT"]; stats = Z["stats"]
            scrB = Z["scrB"]
            pbq = pb[b_q][:].bitcast(BF16)
            stq = ntok if ntok % 2 == 0 else ntok + 1
            for c in range(KC):
                wts = [qdone, bank_free[b_q]] if c == 0 else []
                t(lambda e, c=c, ntok=ntok, stq=stq, pbq=pbq,
                  qbf=qbf: e.matmul(
                    pbq[:, c * stq:c * stq + ntok],
                    qbf[0:ntok, c * 128:(c + 1) * 128],
                    identb[0:ntok, 0:ntok],
                    is_transpose=True, start=(c == 0), stop=(c == KC - 1)),
                  waits=wts, incs=([tmm.inc()] if c == KC - 1 else []))
            trq_done = tmm.now()
            R["qbf"] = trq_done
            if stq == ntok:
                a(lambda e, ntok=ntok, pbq=pbq, xqT=xqT: e.activation(
                    out=xqT[:, 0:KC * ntok], in_=pbq[:, 0:KC * ntok],
                    func=AF.Copy),
                  waits=[trq_done, R["xqT"]], incs=[sev.inc()])
            else:
                a(lambda e, ntok=ntok, stq=stq, pbq=pbq,
                  xqT=xqT: e.activation(
                    out=xqT[:, 0:KC * ntok].rearrange(
                        "p (c q) -> p c q", q=ntok),
                    in_=pbq[:, 0:KC * stq].rearrange(
                        "p (c q) -> p c q", q=stq)[:, :, 0:ntok],
                    func=AF.Copy),
                  waits=[trq_done, R["xqT"]], incs=[sev.inc()])
            bank_free[b_q] = sev.now()
            xqt_done = sev.now()
            for k in range(KC):
                for half in range(2):
                    wts = []
                    if k == 0:
                        wts = [bank_free[b_h[half]]]
                        if half == 0:
                            wts.append(xqt_done)
                    t(lambda e, half=half, k=k, ntok=ntok, xqT=xqT: e.matmul(
                        pb[b_h[half]][0:ntok, :],
                        xqT[:, k * ntok:(k + 1) * ntok],
                        wq_b[:, k * D + half * 512: k * D + half * 512 + 512],
                        start=(k == 0), stop=(k == KC - 1)),
                      waits=wts, incs=([tmm.inc()]
                                       if (k == KC - 1 and half == 1) else []))
            ffn_done = tmm.now()
            R["xqT"] = ffn_done
            for half in range(2):
                a(lambda e, half=half, ntok=ntok, scrB=scrB,
                  stats=stats: e.activation(
                    out=scrB[0:ntok, half * 512:half * 512 + 512],
                    in_=pb[b_h[half]][0:ntok, :], func=AF.Copy,
                    scale=stats[0:ntok, 13:14]),
                  waits=([ffn_done] if half == 0 else []), incs=[sev.inc()])
                bank_free[b_h[half]] = sev.now()
            Z["stail"] = sev.now()
            return sev.now()

        def gen_ffn_part2b(xbuf, x2buf, ntok, evac_done, Z, x2_wait, res):
            scrB = Z["scrB"]
            p1 = vc(lambda e, ntok=ntok, scrB=scrB: e.tensor_mul(
                scrB[0:ntok, :], scrB[0:ntok, :], wsb[0:ntok, :]),
                waits=[evac_done, Z["vtail"]])
            yield
            p2 = vc(lambda e, ntok=ntok, scrB=scrB: e.tensor_add(
                scrB[0:ntok, :], scrB[0:ntok, :], xbuf[0:ntok, :]),
                waits=[p1])
            yield
            p3 = vc(lambda e, ntok=ntok, scrB=scrB: e.scalar_tensor_tensor(
                out=x2buf[0:ntok, :], in0=biasb[0:ntok, :], scalar=0.5,
                in1=scrB[0:ntok, :], op0=ALU.mult, op1=ALU.add),
                waits=[p2] + list(x2_wait))
            Z["vtail"] = p3
            res["x2"] = p3

        def emit_ffn_part2b(xbuf, x2buf, ntok, evac_done, Z, x2_wait=()):
            res = {}
            run_gen(gen_ffn_part2b(xbuf, x2buf, ntok, evac_done, Z,
                                   x2_wait, res))
            return res["x2"]

        def emit_ffn_part2(xbuf, x2buf, ntok, qdone, Z, R, b_q=0,
                           b_h=(1, 2), x2_wait=()):
            ed = emit_ffn_part2a(xbuf, ntok, qdone, Z, R, b_q, b_h)
            return emit_ffn_part2b(xbuf, x2buf, ntok, ed, Z, x2_wait)

        def emit_ffn_block(xbuf, x2buf, ntok, x_wait, Z, R, b_q=0,
                           b_h=(1, 2), x2_wait=()):
            qdone = emit_ffn_part1(xbuf, ntok, x_wait, Z, R)
            return emit_ffn_part2(xbuf, x2buf, ntok, qdone, Z, R, b_q, b_h,
                                  x2_wait)

        def gen_ln2_part1(x2buf, ntok, x2_wait, Z, R, res):
            scrA = Z["scrA"]; scrGE = Z["scrGE"]; scrLE = Z["scrLE"]
            stats = Z["stats"]
            r = {}
            yield from gen_ln(x2buf[0:ntok, :], ntok, False, x2_wait, Z, r)
            xn2_done = vc(
                lambda e, ntok=ntok, scrA=scrA, stats=stats: e.tensor_scalar(
                    out=scrA[0:ntok, :], in0=x2buf[0:ntok, :],
                    scalar1=stats[0:ntok, 2:3], scalar2=stats[0:ntok, 6:7],
                    op0=ALU.add, op1=ALU.mult),
                waits=[r["ln"]])
            yield
            res["xn2"] = xn2_done
            sp1 = ac(lambda e, ntok=ntok, scrGE=scrGE, scrA=scrA: e.activation(
                out=scrGE[0:ntok, :], in_=scrA[0:ntok, :], func=AF.Copy),
                waits=[xn2_done, R["xn2"]])
            sp2 = vc(lambda e, ntok=ntok, scrLE=scrLE, scrA=scrA,
                     scrGE=scrGE: e.tensor_sub(scrLE[0:ntok, :],
                                               scrA[0:ntok, :],
                                               scrGE[0:ntok, :]),
                     waits=[sp1])
            Z["vtail"] = sp2
            res["sd"] = sp2

        def emit_ln2_part1(x2buf, ntok, x2_wait, Z, R):
            res = {}
            run_gen(gen_ln2_part1(x2buf, ntok, x2_wait, Z, R, res))
            return res["xn2"], res["sd"]

        def emit_ln2_part2(ntok, col_off, split_done, xnh, xnl, Z, R,
                           banks=(3, 4), first_wait=()):
            scrGE = Z["scrGE"]; scrLE = Z["scrLE"]
            """bf16 transposes of scrGE/scrLE into xnh/xnl columns."""
            stq = ntok if ntok % 2 == 0 else ntok + 1
            for grp, (srcb, dstb) in enumerate(((scrGE, xnh), (scrLE, xnl))):
                bank = banks[grp]
                pbv = pb[bank][:].bitcast(BF16)
                for c in range(KC):
                    wts = ([split_done, bank_free[bank]] + list(first_wait)) \
                        if c == 0 else []
                    t(lambda e, c=c, ntok=ntok, stq=stq, pbv=pbv,
                      srcb=srcb: e.matmul(
                        pbv[:, c * stq:c * stq + ntok],
                        srcb[0:ntok, c * 128:(c + 1) * 128],
                        identb[0:ntok, 0:ntok],
                        is_transpose=True, start=(c == 0), stop=(c == KC - 1)),
                      waits=wts, incs=([tmm.inc()] if c == KC - 1 else []))
                tr_done = tmm.now()
                dst_ap = dstb[:].rearrange(
                    "p (c q) -> p c q", q=T_tile)[:, :,
                                                  col_off:col_off + ntok]
                src_ap = pbv[:, 0:KC * stq].rearrange(
                    "p (c q) -> p c q", q=stq)[:, :, 0:ntok]
                a(lambda e, dst_ap=dst_ap, src_ap=src_ap: e.activation(
                    out=dst_ap, in_=src_ap, func=AF.Copy),
                  waits=[tr_done], incs=[sev.inc()])
                bank_free[bank] = sev.now()
            R["xn2"] = tmm.now()

        def emit_ln2_transpose(x2buf, ntok, col_off, x2_wait, Z, R,
                               banks=(3, 4), xnh=None, xnl=None,
                               first_wait=()):
            if xnh is None:
                xnh, xnl = xnTh, xnTl
            xn2_done, split_done = emit_ln2_part1(x2buf, ntok, x2_wait, Z, R)
            emit_ln2_part2(ntok, col_off, split_done, xnh, xnl, Z, R, banks,
                           first_wait)
            return xn2_done


        def emit_win_chunk_mm(cchunk, ntok, bank, ready_wait,
                              xnh=None, xnl=None):
            if xnh is None:
                xnh, xnl = xnTh, xnTl
            n0 = cchunk * 128
            first = True
            for k in range(KC):
                for (wbuf, xbuf_) in ((wxh, xnh), (wxh, xnl), (wxl, xnh)):
                    is_first = first
                    first = False
                    is_last = (k == KC - 1 and wbuf is wxl)
                    wts = (list(ready_wait) + [bank_free[bank]]) \
                        if is_first else []
                    t(lambda e, k=k, wbuf=wbuf, xbuf_=xbuf_, n0=n0,
                      is_first=is_first, is_last=is_last, ntok=ntok: e.matmul(
                          pb[bank][:, 0:ntok],
                          wbuf[:, k * DI + n0:k * DI + n0 + 128],
                          xbuf_[:, k * T_tile:k * T_tile + ntok],
                          start=is_first, stop=is_last),
                      waits=wts, incs=([tmm.inc()] if is_last else []))
            return tmm.now()

        if upto >= 1:
            dma(sy, lambda e: e.dma_start(out=bufx[0][0:3, :], in_=dxh[:]),
                sxp[0])
            halo_wait = [sxp[0].now()] + prep_done
            halo_x2 = emit_ffn_block(bufx[0], bufx2[0], 3, halo_wait,
                                     ZS[0], RS[0])
            halo_xn2 = emit_ln2_transpose(bufx2[0], 3, 0, [halo_x2],
                                          ZS[0], RS[0])
            halo_xn = [vev.now(), sev.now()]
            hbuf_mark = [None] * CC
            for c in range(CC):
                bank = 5 + (c % 2)
                mm_done = emit_win_chunk_mm(c, 3, bank, halo_xn if c < 2 else [])
                hbuf_mark[c] = vc(lambda e, c=c, bank=bank: e.tensor_mul(
                    halobuf[:, c * 3:(c + 1) * 3], pb[bank][:, 0:3], halom[:]),
                    waits=[mm_done])
                bank_free[bank] = hbuf_mark[c]
        else:
            halo_x2 = ("vv", 0)
            halo_xn2 = ("vv", 0)
            hbuf_mark = [("vv", 0)] * CC
        x_free = [halo_x2, ("vv", 0)]
        x2_free = [[halo_xn2, ("dsp0", 0)], [("vv", 0), ("dsp1", 0)]]
        xn_used_halo = tmm.now()
        xp_free = [[("vv", 0)], [("vv", 0)]]
        xc_free = [("tt", 0)] * 2
        scan_done = ("vv", 0)
        hl_free = [("dhs", 0), ("dhs", 0)]

        xnhb = [xnTh, xnTh2]
        xnlb = [xnTl, xnTl2]
        xn_used2 = [xn_used_halo, ("tt", 0)]
        xn_ready2 = [[("vv", 0)], [("vv", 0)]]
        cbuf_free = {"xcf": [("ss", 0)], "xcg": ("ss", 0), "g": ("vv", 0)}

        def gen_subA_p1(j, s, res):
            gsub = j * n_sub + s
            par = gsub % 2
            trow = j * T_tile + s * 128
            dma(sy, lambda e, trow=trow, par=par: e.dma_start(
                out=bufx[par][:], in_=dx[trow:trow + 128, :]),
                sxp[par], [x_free[par]])
            yield from gen_ffn_part1(bufx[par], 128, [sxp[par].now()],
                                     ZS[s % 2], RS[s % 2], res)

        def subA_p1(j, s):
            res = {}
            run_gen(gen_subA_p1(j, s, res))
            return res["qd"]

        def subA_p2a(j, s, qd):
            return emit_ffn_part2a(bufx[(j * n_sub + s) % 2], 128, qd,
                                   ZS[s % 2], RS[s % 2])

        def gen_subA_p2b(j, s, ed, res):
            gsub = j * n_sub + s
            par = gsub % 2
            trow = j * T_tile + s * 128
            r = {}
            yield from gen_ffn_part2b(bufx[par], bufx2[par], 128, ed,
                                      ZS[s % 2], x2_free[par], r)
            x2done = r["x2"]
            x_free[par] = x2done
            dma(g, lambda e, trow=trow, par=par: e.dma_start(
                out=dx2[trow:trow + 128, :], in_=bufx2[par][:]),
                sspp[par], [x2done])
            r2 = {}
            yield from gen_ln2_part1(bufx2[par], 128, [x2done],
                                     ZS[s % 2], RS[s % 2], r2)
            x2_free[par] = [r2["xn2"], sspp[par].now()]
            res["sd"] = r2["sd"]

        def subA_p2b(j, s, ed):
            res = {}
            run_gen(gen_subA_p2b(j, s, ed, res))
            return res["sd"]

        def subA_p3(j, s, sd):
            jp = j % 2
            fw = [xn_used2[jp]] if s == 0 else []
            emit_ln2_part2(128, s * 128, sd, xnhb[jp], xnlb[jp],
                           ZS[s % 2], RS[s % 2], first_wait=fw)
            if s == n_sub - 1:
                xn_ready2[jp] = [vev.now(), sev.now()]

        def emit_u_mm(centry):
            c, par, xch_done, xcl_done = centry
            for ui, (wb_, xc_) in enumerate(((wbh, xch), (wbh, xcl),
                                             (wbl, xch))):
                is_first_g = (c == 0 and ui == 0)
                is_last_g = (c == CC - 1 and ui == 2)
                wts = []
                if ui == 0:
                    wts = [xch_done, xcl_done]
                    if is_first_g:
                        wts.append(bank_free[7])
                t(lambda e, c=c, wb_=wb_, xc_=xc_, par=par,
                  is_first_g=is_first_g, is_last_g=is_last_g: e.matmul(
                      pu[:, 0:T_tile],
                      wb_[:, c * DS:(c + 1) * DS],
                      xc_[par][:, :],
                      start=is_first_g, stop=is_last_g),
                  waits=wts, incs=[tmm.inc()] if ui == 2 else [])
            xc_free[par] = tmm.now()

        def chunkA_stage1(j, c, pump=lambda: None):
            jp = j % 2
            bank = 3 + (c % 4)
            par = c % 2
            mm_done = emit_win_chunk_mm(c, T_tile, bank,
                                        xn_ready2[jp] if c < 2 else [],
                                        xnhb[jp], xnlb[jp])
            pump()
            halo_ld = ac(lambda e, c=c, par=par: e.activation(
                out=xp[par][:, 0:3], in_=halobuf[:, c * 3:(c + 1) * 3],
                func=AF.Copy),
                waits=list(xp_free[par]) + [hbuf_mark[c]])
            evx = ac(lambda e, par=par, bank=bank: e.activation(
                out=xp[par][:, 3:3 + T_tile], in_=pb[bank][:, 0:T_tile],
                func=AF.Copy),
                waits=[mm_done])
            bank_free[bank] = evx
            hbuf_mark[c] = ac(lambda e, c=c, par=par: e.activation(
                out=halobuf[:, c * 3:(c + 1) * 3],
                in_=xp[par][:, T_tile:T_tile + 3], func=AF.Copy),
                waits=[evx])
            return dict(evx=evx, halo_ld=halo_ld, par=par)

        def chunkA_stage2(j, c, st, pump=lambda: None):
            par = st["par"]
            par4 = c % 2
            xp_ready = [st["evx"], st["halo_ld"]]
            ct0 = vc(lambda e, c=c, par=par: e.tensor_scalar(
                out=xcf[:, :], in0=xp[par][:, 0:T_tile],
                scalar1=convw[:, 4 * c:4 * c + 1], scalar2=None,
                op0=ALU.mult),
                waits=list(xp_ready) + list(cbuf_free["xcf"]))
            pump()
            ct1 = vc(lambda e, c=c, par=par: e.scalar_tensor_tensor(
                out=xcg[:, :], in0=xp[par][:, 1:1 + T_tile],
                scalar=convw[:, 4 * c + 1:4 * c + 2], in1=xcf[:, :],
                op0=ALU.mult, op1=ALU.add),
                waits=[ct0, cbuf_free["xcg"]])
            pump()
            ct2 = vc(lambda e, c=c, par=par: e.scalar_tensor_tensor(
                out=xcf[:, :], in0=xp[par][:, 2:2 + T_tile],
                scalar=convw[:, 4 * c + 2:4 * c + 3], in1=xcg[:, :],
                op0=ALU.mult, op1=ALU.add),
                waits=[ct1])
            pump()
            conv_done = vc(lambda e, c=c, par=par: e.scalar_tensor_tensor(
                out=xcg[:, :], in0=xp[par][:, 3:3 + T_tile],
                scalar=convw[:, 4 * c + 3:4 * c + 4], in1=xcf[:, :],
                op0=ALU.mult, op1=ALU.add),
                waits=[ct2])
            xp_free[par] = [conv_done]
            sl = ac(lambda e, c=c: e.activation(
                out=xcf[:, :], in_=xcg[:, :], func=AF.Silu,
                bias=convb[:, c:c + 1]),
                waits=[conv_done])
            cbuf_free["xcg"] = sl
            pump()
            xch_done = ac(lambda e, par4=par4: e.activation(
                out=xch[par4][:, :], in_=xcf[:, :], func=AF.Copy),
                waits=[sl, xc_free[par4]])
            pump()
            v(lambda e, par4=par4: e.tensor_sub(
                xcl[par4][:, :], xcf[:, :], xch[par4][:, :]),
              waits=[xch_done], incs=[vev.inc()])
            xcl_done = vev.now()
            pump()
            cbuf_free["xcf"] = [xch_done, xcl_done]
            u_pend.append((c, par4, xch_done, xcl_done))

        if upto >= 2:
            u_pend = []
            subA_p1_q = {}
            subA_p2_sd = {}
            subA_all_first = True
            for s0 in range(0, n_sub, 2):
                if s0 + 1 < n_sub:
                    r0, r1 = {}, {}
                    duet(gen_subA_p1(0, s0, r0),
                         gen_subA_p1(0, s0 + 1, r1))
                    ed0 = subA_p2a(0, s0, r0["qd"])
                    ed1 = subA_p2a(0, s0 + 1, r1["qd"])
                    h0, h1 = {}, {}
                    duet(gen_subA_p2b(0, s0, ed0, h0),
                         gen_subA_p2b(0, s0 + 1, ed1, h1))
                    subA_p3(0, s0, h0["sd"])
                    subA_p3(0, s0 + 1, h1["sd"])
                else:
                    qd = subA_p1(0, s0)
                    ed = subA_p2a(0, s0, qd)
                    sd = subA_p2b(0, s0, ed)
                    subA_p3(0, s0, sd)
            if n_sub == 2:
                events = [(0, "p1", 0), (0, "p1", 1), (9, "p2a", 0),
                          (10, "p2a", 1), (10, "p2b", 0), (10, "p2b", 1),
                          (14, "p3", 0), (15, "p3", 1)]
            elif n_sub == 4:
                events = [(0, "p1", 0), (0, "p1", 1),
                          (4, "p2a", 0), (4, "p2b", 0),
                          (5, "p2a", 1), (5, "p2b", 1),
                          (8, "p3", 0), (8, "p1", 2),
                          (9, "p3", 1), (9, "p1", 3),
                          (11, "p2a", 2), (11, "p2b", 2),
                          (12, "p2a", 3), (12, "p2b", 3),
                          (14, "p3", 2), (14, "p3", 3)]
            else:
                events = []
                for s in range(n_sub):
                    base = (CC // n_sub) * s
                    events += [(base, "p1", s), (base + 1, "p2a", s),
                               (base + 2, "p2b", s), (base + 3, "p3", s)]
            active = []
            rr = [0]

            def pumpA():
                done = 0
                while active and done < 2:
                    i = rr[0] % len(active)
                    try:
                        next(active[i])
                        rr[0] = i + 1
                        done += 1
                    except StopIteration:
                        active.pop(i)

            def exhaust(gn):
                try:
                    while True:
                        next(gn)
                except StopIteration:
                    pass
                if gn in active:
                    active.remove(gn)

            st_g = {}
            st_res = {}
            st_ed = {}
            for j in range(n_tiles):
                jp = j % 2
                t0 = j * T_tile
                st1 = None
                for c in range(CC):
                    while len(u_pend) >= 2:
                        emit_u_mm(u_pend.pop(0))
                    if st1 is None:
                        st1 = chunkA_stage1(j, 0, pumpA)
                    st1_next = (chunkA_stage1(j, c + 1, pumpA)
                                if c + 1 < CC else None)
                    chunkA_stage2(j, c, st1, pumpA)
                    st1 = st1_next
                    if j + 1 < n_tiles:
                        for (cc_, stage, s) in events:
                            if cc_ != c:
                                continue
                            if stage == "p1":
                                st_res[s] = {}
                                st_g[s] = gen_subA_p1(j + 1, s, st_res[s])
                                active.append(st_g[s])
                            elif stage == "p2a":
                                exhaust(st_g[s])
                                st_ed[s] = subA_p2a(j + 1, s,
                                                    st_res[s]["qd"])
                            elif stage == "p2b":
                                st_res[s] = {}
                                st_g[s] = gen_subA_p2b(j + 1, s, st_ed[s],
                                                       st_res[s])
                                active.append(st_g[s])
                            elif stage == "p3":
                                exhaust(st_g[s])
                                subA_p3(j + 1, s, st_res[s]["sd"])
                while u_pend:
                    emit_u_mm(u_pend.pop(0))
                xn_used2[jp] = tmm.now()
                u_done = tmm.now()
                u_ev = vc(lambda e: e.tensor_scalar(out=u_t[:, :],
                                                    in0=pu[:, 0:T_tile],
                                                    scalar1=0.1, scalar2=None,
                                                    op0=ALU.mult),
                          waits=[u_done, scan_done])
                bank_free[7] = u_ev
                hlp = j % 2
                if j == 0:
                    scan_done = vc(lambda e: e.tensor_tensor_scan(
                        out=hl_tmp[0][:, :], data0=d09[:, :],
                        data1=u_t[:, :], initial=0.0,
                        op0=ALU.mult, op1=ALU.add),
                        waits=[u_ev])
                else:
                    scan_done = vc(lambda e, hlp=hlp: e.tensor_tensor_scan(
                        out=hl_tmp[hlp][:, :], data0=d09[:, :],
                        data1=u_t[:, :],
                        initial=hl_tmp[1 - hlp][:, T_tile - 1:T_tile],
                        op0=ALU.mult, op1=ALU.add),
                        waits=[u_ev, hl_free[hlp]])
                dma(g, lambda e, t0=t0, hlp=hlp: e.dma_start(
                    out=dhlc[:, t0:t0 + T_tile], in_=hl_tmp[hlp][:, :]),
                    shs, [scan_done])
                hl_free[hlp] = shs.now()


        a_last_mm = tmm.now()

        if upto >= 3:
            dma(g, lambda e: e.dma_start(out=dcci[:, :],
                                         in_=hl_tmp[(n_tiles - 1) % 2][
                                             :, T_tile - 1:T_tile]),
                scc, [scan_done])
            g(lambda e: e.collective_compute(
                "AllGather", ALU.bypass,
                replica_groups=[list(range(n_cores))],
                ins=[dcci.ap().opt()], outs=[dcco.ap().opt()]),
              waits=[scc.now()], incs=[gev.inc()])
            dma(g, lambda e: e.dma_start(out=gbuf[0:DS * n_cores, :],
                                         in_=dcco.ap()),
                scc, [gev.now()])
            gh1 = vc(lambda e: e.tensor_copy(out=ghl[:, 0:1], in_=gbuf[:, :]),
                     waits=[scc.now()])
            ghl_done = vc(lambda e: e.tensor_sub(ghl[:, 1:2], gbuf[:, :],
                                                 ghl[:, 0:1]),
                          waits=[gh1])
            t(lambda e: e.matmul(pb[0][0:DS, 0:1], selb[:, :], ghl[:, 0:1],
                                 start=True, stop=False),
              waits=[ghl_done, bank_free[0]])
            t(lambda e: e.matmul(pb[0][0:DS, 0:1], selb[:, :], ghl[:, 1:2],
                                 start=False, stop=True),
              incs=[tmm.inc()])
            v(lambda e: e.tensor_copy(out=hin[:, :], in_=pb[0][0:DS, 0:1]),
              waits=[tmm.now()], incs=[vev.inc()])
            bank_free[0] = vev.now()
            hin_done = vev.now()
            decay_used = ("vv", 0)

        if upto >= 4:
            load_split_big([(k * DI, DI,
                             dwin[k * 128:(k + 1) * 128, DI:2 * DI])
                            for k in range(KC)], wait_first=[a_last_mm],
                           stages=[xnt2f[:, 0:D], xnt2f[:, D:2 * D],
                                   xnt2g[:, 0:D], xnt2g[:, D:2 * D]])
            b_w_done = [sev.now(), vev.now()]
            hh_used = ("tt", 0)
            decay_used = scan_done
            spill_a_done = [sspp[0].now(), sspp[1].now()]
            gk_free = [xp_free[0][0], xp_free[1][0]]
            ykh_free = [("dyh0", 0), ("dyh1", 0)]
            ykl_free = [("dyl0", 0), ("dyl1", 0)]
            yk_used = [("ss", 0)]

            def gen_subB_p1(j, s, res):
                gsub = j * n_sub + s
                par = gsub % 2
                trow = j * T_tile + s * 128
                dma(sy, lambda e, trow=trow, par=par: e.dma_start(
                    out=bufx2[par][:], in_=dx2[trow:trow + 128, :]),
                    sxp[par], list(spill_a_done) + list(x2_free[par]))
                extra = []
                r2 = {}
                yield from gen_ln2_part1(
                    bufx2[par], 128, [sxp[par].now()] + list(extra),
                    ZS[s % 2], RS[s % 2], r2)
                x2_free[par] = [r2["xn2"]]
                res["sd"] = r2["sd"]

            def subB_p1(j, s):
                res = {}
                run_gen(gen_subB_p1(j, s, res))
                return res["sd"]

            def subB_p3(j, s, sd):
                jp = j % 2
                fw = [xn_used2[jp]] if s == 0 else []
                emit_ln2_part2(128, s * 128, sd, xnhb[jp], xnlb[jp],
                               ZS[s % 2], RS[s % 2], first_wait=fw)
                if s == n_sub - 1:
                    xn_ready2[jp] = [vev.now(), sev.now()]

            def chunkB(j, c, hh_done):
                jp = j % 2
                bank = 5 + (c % 2)
                ybank = 1 + (c % 2)
                par = c % 2
                t0 = j * T_tile
                zw = list(xn_ready2[jp]) if c < 2 else []
                if j == 0 and c < 2:
                    zw = zw + b_w_done
                z_done = emit_win_chunk_mm(c, T_tile, bank, zw,
                                           xnhb[jp], xnlb[jp])
                for ui, (wc_, hb_) in enumerate(((wch, hh_t), (wch, hl_t),
                                                 (wcl, hh_t))):
                    wts = []
                    if ui == 0:
                        wts = [bank_free[ybank]]
                        if c < 2:
                            wts.append(hh_done)
                    t(lambda e, c=c, wc_=wc_, hb_=hb_, ybank=ybank,
                      ui=ui: e.matmul(
                          pb[ybank][:, 0:T_tile],
                          wc_[0:DS, c * 128:(c + 1) * 128],
                          hb_[0:DS, 0:T_tile],
                          start=(ui == 0), stop=(ui == 2)),
                      waits=wts, incs=([tmm.inc()] if ui == 2 else []))
                yp_done = tmm.now()
                a(lambda e, bank=bank, par=par: e.activation(
                    out=gk[par][:, :], in_=pb[bank][:, 0:T_tile],
                    func=AF.Silu),
                  waits=[z_done, gk_free[par]], incs=[sev.inc()])
                bank_free[bank] = sev.now()
                g_done = sev.now()
                v(lambda e, ybank=ybank, par=par: e.tensor_mul(
                    yk[:, :], pb[ybank][:, 0:T_tile], gk[par][:, :]),
                  waits=[yp_done, g_done] + list(yk_used), incs=[vev.inc()])
                bank_free[ybank] = vev.now()
                gk_free[par] = vev.now()
                yk_done = vev.now()
                a(lambda e, par=par: e.activation(
                    out=ykh[par][:, :], in_=yk[:, :], func=AF.Copy),
                  waits=[yk_done, ykh_free[par]], incs=[sev.inc()])
                ykh_done = sev.now()
                v(lambda e, par=par: e.tensor_sub(
                    ykl[par][:, :], yk[:, :], ykh[par][:, :]),
                  waits=[ykh_done, ykl_free[par]], incs=[vev.inc()])
                ykl_done = vev.now()
                yk_used[:] = [sev.now(), ykl_done]
                gs0 = t0 // 128
                dma(g, lambda e, c=c, gs0=gs0, par=par: e.dma_start(
                    out=dysh[gs0:gs0 + n_sub, :, c, :].rearrange(
                        "b p n -> p b n"),
                    in_=ykh[par][:, :].rearrange("p (b n) -> p b n",
                                                 b=n_sub)),
                    syph[par], [ykh_done])
                ykh_free[par] = syph[par].now()
                dma(g, lambda e, c=c, gs0=gs0, par=par: e.dma_start(
                    out=dysl[gs0:gs0 + n_sub, :, c, :].rearrange(
                        "b p n -> p b n"),
                    in_=ykl[par][:, :].rearrange("p (b n) -> p b n",
                                                 b=n_sub)),
                    sypl[par], [ykl_done])
                ykl_free[par] = sypl[par].now()

            subB_sd = {}
            for s0 in range(0, n_sub, 2):
                if s0 + 1 < n_sub:
                    r0, r1 = {}, {}
                    duet(gen_subB_p1(0, s0, r0),
                         gen_subB_p1(0, s0 + 1, r1))
                    subB_p3(0, s0, r0["sd"])
                    subB_p3(0, s0 + 1, r1["sd"])
                else:
                    sd = subB_p1(0, s0)
                    subB_p3(0, s0, sd)
            if n_sub == 4:
                p1_events = {1: (0, 1), 7: (2, 3)}
                p3_sched = {4: 0, 5: 1, 11: 2, 12: 3}
            else:
                p1_events = {1: tuple(range(n_sub))}
                p3_sched = ({6: 0, 8: 1} if n_sub == 2 else {6: 0})
            hstage = hl_tmp[0]
            hstage_used = hl_free[0] if n_tiles % 2 == 0 else hl_free[1]
            for j in range(n_tiles):
                jp = j % 2
                t0 = j * T_tile
                dma(sy, lambda e, t0=t0: e.dma_start(
                    out=decay_t[:, :], in_=ddecay[:, t0:t0 + T_tile]),
                    sdec, [decay_used])
                dma(sy, lambda e, t0=t0: e.dma_start(
                    out=hstage[:, :], in_=dhlc[:, t0:t0 + T_tile]),
                    shb, [hstage_used, hl_free[0], hl_free[1]])
                decay_used = vc(lambda e, t0=t0: e.scalar_tensor_tensor(
                    out=htmp[:, :], in0=decay_t[:, :], scalar=hin[:, :],
                    in1=hstage[:, :], op0=ALU.mult, op1=ALU.add),
                    waits=[sdec.now(), shb.now(), hin_done, hh_used])
                hstage_used = decay_used
                hcp = vc(lambda e: e.tensor_copy(out=hh_t[:, :],
                                                 in_=htmp[:, :]),
                         waits=[decay_used])
                hh_done = vc(lambda e: e.tensor_sub(hl_t[:, :], htmp[:, :],
                                                    hh_t[:, :]),
                             waits=[hcp])
                for c in range(CC):
                    chunkB(j, c, hh_done)
                    if j + 1 < n_tiles:
                        if c in p1_events:
                            ss = p1_events[c]
                            ress = {s: {} for s in ss}
                            duet(*[gen_subB_p1(j + 1, s, ress[s])
                                   for s in ss])
                            for s in ss:
                                subB_sd[s] = ress[s]["sd"]
                        if c in p3_sched:
                            s = p3_sched[c]
                            subB_p3(j + 1, s, subB_sd[s])
                xn_used2[jp] = tmm.now()
                hh_used = tmm.now()

            b_last_mm = tmm.now()
            spill_b_done = [syph[0].now(), syph[1].now(),
                            sypl[0].now(), sypl[1].now()]

        if upto >= 5:
            xntf = xnTh[:].bitcast(F32)
            xntg = xnTl[:].bitcast(F32)
            load_split_big([(k * D, D, dwout[k * 128:(k + 1) * 128, :])
                            for k in range(CC)],
                           wait_first=[xn_used2[0]],
                           stages=[xntf[:, 0:D], xntf[:, D:2 * D],
                                   xntg[:, 0:D], xntg[:, D:2 * D]],
                           copy_first=[b_last_mm])
            c_w_done = [sev.now(), vev.now()]
            ybh = [xnTh[:, 0:CC * 128], ybh1[:, 0:CC * 128]]
            ybl = [xnTl[:, 0:CC * 128], ybl1[:, 0:CC * 128]]
            HC = CC // 2
            yb_free_f = [b_last_mm, ("vv", 0)]
            yb_free_b = [b_last_mm, ("vv", 0)]
            out_free = [("dot0", 0), ("dot1", 0)]

            def emitC_yb_loads(gsub):
                par = gsub % 2
                trow = gsub * 128
                gsi = trow // 128
                cw = c_w_done if gsub < 2 else []
                dma(sy, lambda e, gsi=gsi, par=par: e.dma_start(
                    out=ybh[par][:, 0:HC * 128],
                    in_=dysh[gsi, :, 0:HC, :].rearrange("p c n -> p (c n)")),
                    sybp[par], list(spill_b_done) + [yb_free_f[par]] + cw)
                dma(g, lambda e, gsi=gsi, par=par: e.dma_start(
                    out=ybl[par][:, 0:HC * 128],
                    in_=dysl[gsi, :, 0:HC, :].rearrange("p c n -> p (c n)")),
                    sybl2[par], list(spill_b_done) + [yb_free_f[par]] + cw)
                ywf = [sybp[par].now(), sybl2[par].now()]
                dma(sy, lambda e, gsi=gsi, par=par: e.dma_start(
                    out=ybh[par][:, HC * 128:CC * 128],
                    in_=dysh[gsi, :, HC:CC, :].rearrange("p c n -> p (c n)")),
                    sybp[par], [yb_free_b[par]] + cw)
                dma(g, lambda e, gsi=gsi, par=par: e.dma_start(
                    out=ybl[par][:, HC * 128:CC * 128],
                    in_=dysl[gsi, :, HC:CC, :].rearrange("p c n -> p (c n)")),
                    sybl2[par], [yb_free_b[par]] + cw)
                ywb = [sybp[par].now(), sybl2[par].now()]
                return dict(gsub=gsub, par=par, trow=trow, ywf=ywf, ywb=ywb)

            def emitC_x2load(pd):
                par = pd["par"]
                trow = pd["trow"]
                dma(sy, lambda e, trow=trow, par=par: e.dma_start(
                    out=bufx2[par][:], in_=dx2[trow:trow + 128, :]),
                    sxp[par], list(x2_free[par]))
                pd["x2w"] = sxp[par].now()

            def emitC_wout(pd, first_sub):
                gsub = pd["gsub"]
                par = pd["par"]
                ywf = pd["ywf"]
                ywb = pd["ywb"]
                bp = (1, 2) if gsub % 2 == 0 else (5, 6)
                started = [False, False]
                back_waited = False
                for c in range(CC):
                    groups = (((ybh, wxh), (ybh, wxl)), ((ybl, wxh),))
                    for grp in groups:
                        for (yb_, wv) in grp:
                            for half in range(2):
                                is_first = not started[half]
                                started[half] = True
                                is_last = (c == CC - 1 and yb_ is ybl)
                                is_fend = (c == HC - 1 and yb_ is ybl)
                                wts = []
                                if is_first:
                                    wts = list(ywf) + [bank_free[bp[half]]]
                                    if first_sub and half == 0:
                                        wts += c_w_done
                                if c == HC and not back_waited:
                                    wts = wts + list(ywb)
                                    back_waited = True
                                t(lambda e, c=c, yb_=yb_, wv=wv, par=par,
                                  half=half, bp=bp, is_first=is_first,
                                  is_last=is_last: e.matmul(
                                      pb[bp[half]][:, :],
                                      yb_[par][:, c * 128:(c + 1) * 128],
                                      wv[:, c * D + half * 512:
                                         c * D + half * 512 + 512],
                                      start=is_first, stop=is_last),
                                  waits=wts,
                                  incs=([tmm.inc()]
                                        if ((is_last or is_fend)
                                            and half == 1) else []))
                    if c == HC - 1:
                        yb_free_f[par] = tmm.now()
                wo_done = tmm.now()
                yb_free_b[par] = wo_done
                pd["wo_done"] = wo_done
                pd["bp"] = bp

            def gen_x3_p1(pd):
                par = pd["par"]
                bp = pd["bp"]
                for half in range(2):
                    v(lambda e, half=half, par=par, bp=bp: e.tensor_add(
                        bufx[par][:, half * 512:half * 512 + 512],
                        pb[bp[half]][:, :],
                        bufx2[par][:, half * 512:half * 512 + 512]),
                      waits=([pd["wo_done"], pd["x2w"], x_free[par]]
                             if half == 0 else []),
                      incs=[vev.inc()])
                    bank_free[bp[half]] = vev.now()
                    if half == 0:
                        yield
                x3_done = vev.now()
                x2_free[par] = [x3_done]
                yield
                r = {}
                yield from gen_ffn_part1(bufx[par], 128, [x3_done],
                                         ZS[par], RS[par], r)
                pd["qd"] = r["qd"]

            def emitC_part2a(pd):
                pd["ed"] = emit_ffn_part2a(bufx[pd["par"]], 128, pd["qd"],
                                           ZS[pd["par"]], RS[pd["par"]],
                                           b_q=0, b_h=(3, 4))

            def gen_part2b_store(pd):
                par = pd["par"]
                trow = pd["trow"]
                r = {}
                yield from gen_ffn_part2b(bufx[par], bufx[par], 128,
                                          pd["ed"], ZS[par], (), r)
                x4_done = r["x2"]
                dma_nc(sy, lambda e, trow=trow, par=par: e.dma_start(
                    out=dout[trow:trow + 128, :], in_=bufx[par][:]),
                    sotp[par], [x4_done])
                x_free[par] = sotp[par].now()

            prev = None
            first_sub = True
            for gp in range(T_core // 256):
                if prev is not None:
                    duet(gen_x3_p1(prev[0]), gen_x3_p1(prev[1]))
                pds = []
                for gsub in (2 * gp, 2 * gp + 1):
                    pds.append(emitC_yb_loads(gsub))
                for pd in pds:
                    emitC_x2load(pd)
                for pd in pds:
                    emitC_wout(pd, first_sub)
                    first_sub = False
                if prev is not None:
                    emitC_part2a(prev[0])
                    emitC_part2a(prev[1])
                    duet(gen_part2b_store(prev[0]), gen_part2b_store(prev[1]))
                prev = pds
            duet(gen_x3_p1(prev[0]), gen_x3_p1(prev[1]))
            emitC_part2a(prev[0])
            emitC_part2a(prev[1])
            duet(gen_part2b_store(prev[0]), gen_part2b_store(prev[1]))

        if upto >= 5:
            g(None, waits=[sotp[0].now(), sotp[1].now()])
        else:
            g(None, waits=[("vv", P.streams["vv"].n), ("ss", P.streams["ss"].n),
                           ("tt", P.streams["tt"].n), ("gg", P.streams["gg"].n),
                           ("dsp0", P.streams["dsp0"].n), ("dsp1", P.streams["dsp1"].n),
                           ("dyh0", P.streams["dyh0"].n), ("dyh1", P.streams["dyh1"].n),
                           ("dyl0", P.streams["dyl0"].n), ("dyl1", P.streams["dyl1"].n),
                           ("dcc", P.streams["dcc"].n)])

        with nc.Block() as block:
            @block.sync
            def _(eng):
                P.run("sync", eng, sems)

            @block.tensor
            def _(eng):
                P.run("tensor", eng, sems)

            @block.vector
            def _(eng):
                P.run("vector", eng, sems)

            @block.scalar
            def _(eng):
                P.run("scalar", eng, sems)

            @block.gpsimd
            def _(eng):
                P.run("gpsimd", eng, sems)

    return nc


def make_core_inputs(x_flat, n_cores, T_core, weights):
    (w_in, conv_w, conv_b, w_B, w_C, w_out, wq, wsc, bias) = weights
    halves_per_seq = 2
    wq_t = np.ascontiguousarray(wq.T)
    conv_w_p = np.ascontiguousarray(
        conv_w[:, 0, :].reshape(CC, 128, 4).transpose(1, 0, 2).reshape(128, CC * 4))
    conv_b_p = np.ascontiguousarray(conv_b.reshape(CC, 128).T)
    wsb = np.ascontiguousarray(np.broadcast_to(wsc[None, :], (128, D)))
    bias_b = np.ascontiguousarray(np.broadcast_to(bias[None, :], (128, D)))
    identf = np.eye(128, dtype=np.float32)
    identb = np.eye(128, dtype=ml_dtypes.bfloat16)
    tt = np.arange(1, T_core + 1, dtype=np.float64)
    decay = np.broadcast_to((0.9 ** tt).astype(np.float32)[None, :],
                            (DS, T_core)).copy()
    in_maps = []
    for i in range(n_cores):
        r0 = i * T_core
        xc = np.ascontiguousarray(x_flat[r0:r0 + T_core])
        is_mid = (i % halves_per_seq) != 0
        if is_mid:
            xhalo = np.ascontiguousarray(x_flat[r0 - 3:r0])
            hm = np.ones((128, 3), dtype=np.float32)
            sel = np.zeros((128, DS), dtype=ml_dtypes.bfloat16)
            prev = i - 1
            for q in range(DS):
                sel[prev * DS + q, q] = 1.0
        else:
            xhalo = np.zeros((3, D), dtype=np.float32)
            hm = np.zeros((128, 3), dtype=np.float32)
            sel = np.zeros((128, DS), dtype=ml_dtypes.bfloat16)
        in_maps.append({
            "x": xc, "xhalo": xhalo, "w_in": w_in,
            "conv_w_p": conv_w_p, "conv_b_p": conv_b_p,
            "w_B": w_B, "w_C": w_C, "w_out": w_out, "wq_t": wq_t,
            "w_scale_b": wsb, "bias_b": bias_b, "halo_mask": hm,
            "sel_b": sel, "decay": decay, "ident_f": identf,
            "ident_b": identb,
        })
    return in_maps


def kernel(L0_attn_x, L0_attn_w_in, L0_attn_conv_w, L0_attn_conv_b,
           L0_attn_w_B, L0_attn_w_C, L0_attn_w_out, L0_ffn_weight_q,
           L0_ffn_weight_scale, L0_ffn_bias):
    n_cores = 8
    T_core = 4096
    T_tile = 512
    B, S, _ = L0_attn_x.shape
    x_flat = np.ascontiguousarray(
        np.asarray(L0_attn_x, dtype=np.float32).reshape(B * S, D))
    weights = (
        np.ascontiguousarray(np.asarray(L0_attn_w_in, dtype=np.float32)),
        np.ascontiguousarray(np.asarray(L0_attn_conv_w, dtype=np.float32)),
        np.ascontiguousarray(np.asarray(L0_attn_conv_b, dtype=np.float32)),
        np.ascontiguousarray(np.asarray(L0_attn_w_B, dtype=np.float32)),
        np.ascontiguousarray(np.asarray(L0_attn_w_C, dtype=np.float32)),
        np.ascontiguousarray(np.asarray(L0_attn_w_out, dtype=np.float32)),
        np.ascontiguousarray(np.asarray(L0_ffn_weight_q, dtype=np.int8)),
        np.ascontiguousarray(np.asarray(L0_ffn_weight_scale, dtype=np.float32)),
        np.ascontiguousarray(np.asarray(L0_ffn_bias, dtype=np.float32)),
    )
    nc = build_program(n_cores, T_core, T_tile)
    in_maps = make_core_inputs(x_flat, n_cores, T_core, weights)
    res = run_bass_kernel_spmd(nc, in_maps, core_ids=list(range(n_cores)))
    out = np.concatenate([np.asarray(r["out"]) for r in res.results], axis=0)
    return out.reshape(B, S, D).astype(np.float32)


# version 47
# speedup vs baseline: 1.0150x; 1.0150x over previous
import numpy as np
import ml_dtypes
import concourse.bass as bass
import concourse.mybir as mybir
from concourse.bass_utils import run_bass_kernel_spmd
from contextlib import ExitStack

F32 = mybir.dt.float32
BF16 = mybir.dt.bfloat16
I8 = mybir.dt.int8
I32 = mybir.dt.int32
AF = mybir.ActivationFunctionType
ALU = mybir.AluOpType
AXX = mybir.AxisListType.X

D = 1024
DI = 2048
DS = 16
KC = D // 128
CC = DI // 128
EPS_LN = 1e-5
EPS_Q = 1e-6


class Stream:
    def __init__(self, name, step=1):
        self.name = name
        self.step = step
        self.n = 0

    def inc(self, k=None):
        k = self.step if k is None else k
        self.n += k
        return (self.name, k)

    def now(self):
        return (self.name, self.n)


class Plan:
    def __init__(self):
        self.items = []
        self.streams = {}

    def S(self, name, step=1):
        if name not in self.streams:
            self.streams[name] = Stream(name, step)
        return self.streams[name]

    def emit(self, eng, f, waits=(), incs=()):
        self.items.append((eng, tuple(waits), tuple(incs), f))

    def drain(self, eng):
        self.items.append((eng, (), (), "drain"))

    def run(self, engname, engobj, sems):
        for (eng, waits, incs, f) in self.items:
            if eng != engname:
                continue
            if f == "drain":
                engobj.drain()
                continue
            for (s, vv) in waits:
                if vv > 0:
                    engobj.wait_ge(sems[s], vv)
            if f is None:
                continue
            inst = f(engobj)
            for (s, k) in incs:
                inst.then_inc(sems[s], k)


def build_program(n_cores, T_core, T_tile, upto=5):
    assert T_core % T_tile == 0 and T_tile % 128 == 0
    n_tiles = T_core // T_tile
    n_sub = T_tile // 128
    nc = bass.Bass("TRN2", target_bir_lowering=False)

    dx = nc.dram_tensor("x", [T_core, D], F32, kind="ExternalInput").ap()
    dxh = nc.dram_tensor("xhalo", [3, D], F32, kind="ExternalInput").ap()
    dwin = nc.dram_tensor("w_in", [D, 2 * DI], F32, kind="ExternalInput").ap()
    dconvw = nc.dram_tensor("conv_w_p", [128, CC * 4], F32, kind="ExternalInput").ap()
    dconvb = nc.dram_tensor("conv_b_p", [128, CC], F32, kind="ExternalInput").ap()
    dwB = nc.dram_tensor("w_B", [DI, DS], F32, kind="ExternalInput").ap()
    dwC = nc.dram_tensor("w_C", [DS, DI], F32, kind="ExternalInput").ap()
    dwout = nc.dram_tensor("w_out", [DI, D], F32, kind="ExternalInput").ap()
    dwq = nc.dram_tensor("wq_t", [D, D], I8, kind="ExternalInput").ap()
    dwsb = nc.dram_tensor("w_scale_b", [128, D], F32, kind="ExternalInput").ap()
    dbb = nc.dram_tensor("bias_b", [128, D], F32, kind="ExternalInput").ap()
    dhm = nc.dram_tensor("halo_mask", [128, 3], F32, kind="ExternalInput").ap()
    dsel = nc.dram_tensor("sel_b", [128, DS], BF16, kind="ExternalInput").ap()
    ddecay = nc.dram_tensor("decay", [DS, T_core], F32, kind="ExternalInput").ap()
    didf = nc.dram_tensor("ident_f", [128, 128], F32, kind="ExternalInput").ap()
    didb = nc.dram_tensor("ident_b", [128, 128], BF16, kind="ExternalInput").ap()
    dout = nc.dram_tensor("out", [T_core, D], F32, kind="ExternalOutput").ap()
    dx2 = nc.dram_tensor("x2_sp", [T_core, D], F32).ap()
    dysh = nc.dram_tensor("ysp_h", [T_core // 128, 128, CC, 128],
                          BF16).ap()
    dysl = nc.dram_tensor("ysp_l", [T_core // 128, 128, CC, 128],
                          BF16).ap()
    dcci = nc.dram_tensor("cc_in", [DS, 1], F32)
    dcco = nc.dram_tensor("cc_out", [DS * n_cores, 1], F32)
    dhlc = nc.dram_tensor("hloc_sp", [DS, T_core], F32).ap()

    P = Plan()
    es = ExitStack()

    def sbt(name, shape, dt):
        return es.enter_context(nc.sbuf_tensor(name, shape, dt))

    def pst(name, shape, dt):
        return es.enter_context(nc.psum_tensor(name, shape, dt))

    with es:
        wq_b = sbt("wq_b", [128, KC * D], BF16)
        wsb = sbt("wsb", [128, D], F32)
        biasb = sbt("biasb", [128, D], F32)
        wxh = sbt("wxh", [128, KC * DI], BF16)
        wxl = sbt("wxl", [128, KC * DI], BF16)
        wbh = sbt("wbh", [128, CC * DS], BF16)
        wbl = sbt("wbl", [128, CC * DS], BF16)
        wch = sbt("wch", [DS, DI], BF16)
        wcl = sbt("wcl", [DS, DI], BF16)
        convw = sbt("convw", [128, CC * 4], F32)
        convb = sbt("convb", [128, CC], F32)
        selb = sbt("selb", [128, DS], BF16)
        identb = sbt("identb", [128, 128], BF16)
        halom = sbt("halom", [128, 3], F32)
        halobuf = sbt("halobuf", [128, CC * 3], F32)
        epsln = sbt("epsln", [128, 1], F32)
        u_t = sbt("u_t", [DS, T_tile], F32)
        d09 = sbt("d09", [DS, T_tile], F32)
        hl_tmp = [sbt(f"hl_tmp{i}", [DS, T_tile], F32) for i in range(2)]
        hh_t = sbt("hh_t", [DS, T_tile], BF16)
        hl_t = sbt("hl_t", [DS, T_tile], BF16)
        decay_t = d09
        htmp = u_t
        hin = sbt("hin", [DS, 1], F32)
        gbuf = sbt("gbuf", [128, 1], F32)
        ghl = sbt("ghl", [128, 2], BF16)

        bufx = [sbt(f"bufx{i}", [128, D], F32) for i in range(2)]
        bufx2 = [sbt(f"bufx2{i}", [128, D], F32) for i in range(2)]
        sqS = sbt("sqS", [128, D], F32)
        sq_guard = [("ss", 0)]
        ZS = []
        for zi in range(2):
            ZS.append(dict(
                scrA=sbt(f"scrA{zi}", [128, D], F32),
                scrB=sbt(f"scrB{zi}", [128, D], F32),
                scrGE=sbt(f"scrGE{zi}", [128, D], BF16),
                scrLE=sbt(f"scrLE{zi}", [128, D], BF16),
                qbf=sbt(f"qbf{zi}", [128, D], BF16),
                xqT=sbt(f"xqT{zi}", [128, KC * 128], BF16),
            ))
        xnTh = sbt("xnTh", [128, KC * T_tile], BF16)
        xnTl = sbt("xnTl", [128, KC * T_tile], BF16)
        xnTh2 = sbt("xnTh2", [128, KC * T_tile], BF16)
        xnTl2 = sbt("xnTl2", [128, KC * T_tile], BF16)
        ybh1 = xnTh2
        ybl1 = xnTl2
        wstage = ZS[0]["scrB"]
        wstage2 = ZS[1]["scrB"]
        wqi8 = ZS[0]["scrGE"][:, 0:D // 2].bitcast(I8)
        wcstage = wstage[0:DS, 0:D]
        wbstage = wstage[:, 0:CC * DS]
        xp = [sbt(f"xp{i}", [128, T_tile + 3], F32) for i in range(2)]
        xcf = sbt("xcf", [128, T_tile], F32)
        xcg = sbt("xcg", [128, T_tile], F32)
        xch = [sbt(f"xch{i}", [128, T_tile], BF16) for i in range(2)]
        xcl = [sbt(f"xcl{i}", [128, T_tile], BF16) for i in range(2)]
        gk = [xp[0][:, 0:T_tile], xp[1][:, 0:T_tile]]
        yk = xcf
        xcg_b = xcg[:].bitcast(BF16)
        ykh = [xcg_b[:, 0:T_tile], xch[0][:]]
        ykl = [xcg_b[:, T_tile:2 * T_tile], xcl[0][:]]
        for zi in range(2):
            ZS[zi]["stats"] = sbt(f"stats{zi}", [128, 16], F32)
            ZS[zi]["vtail"] = ("vv", 0)
            ZS[zi]["stail"] = ("ss", 0)
            ZS[zi]["sq_done"] = ("ss", 0)
        pb = [pst(f"pb{i}", [128, 512], F32) for i in range(8)]
        pu = pb[7][0:DS, :]

        sem_names = ["dhs", "dhb", "dw2",
                     "dw", "dcc", "ddc", "dhl",
                     "dxa0", "dxa1", "dsp0", "dsp1",
                     "dyh0", "dyh1", "dyl0", "dyl1",
                     "dyb0", "dyb1", "dot0", "dot1",
                     "vv", "ss", "tt", "gg"]
        sems = {n: es.enter_context(nc.semaphore(n)) for n in sem_names}

        sw = P.S("dw", 16)
        sw2 = P.S("dw2", 16)
        shs = P.S("dhs", 16)
        shb = P.S("dhb", 16)
        scc = P.S("dcc", 16)
        sdec = P.S("ddc", 16)
        shl = P.S("dhl", 16)
        sxp = [P.S("dxa0", 16), P.S("dxa1", 16)]
        sspp = [P.S("dsp0", 16), P.S("dsp1", 16)]
        syph = [P.S("dyh0", 16), P.S("dyh1", 16)]
        sypl = [P.S("dyl0", 16), P.S("dyl1", 16)]
        sybp = [P.S("dyb0", 16), P.S("dyb1", 16)]
        sybl2 = [P.S("dyl0", 16), P.S("dyl1", 16)]
        sotp = [P.S("dot0", 16), P.S("dot1", 16)]
        vev = P.S("vv")
        sev = P.S("ss")
        tmm = P.S("tt")
        gev = P.S("gg")

        def v(f, waits=(), incs=()):
            P.emit("vector", f, waits, incs)

        def a(f, waits=(), incs=()):
            P.emit("scalar", f, waits, incs)

        def t(f, waits=(), incs=()):
            P.emit("tensor", f, waits, incs)

        def g(f, waits=(), incs=()):
            P.emit("gpsimd", f, waits, incs)

        def sy(f, waits=(), incs=()):
            P.emit("sync", f, waits, incs)

        def dma(emitfn, f, stream, waits=()):
            emitfn(f, waits=[stream.now()] + list(waits), incs=[stream.inc()])

        def dma_nc(emitfn, f, stream, waits=()):
            emitfn(f, waits=list(waits), incs=[stream.inc()])

        v(lambda e: e.memset(epsln[:], EPS_LN), incs=[vev.inc()])
        v(lambda e: e.memset(d09[:], 0.9), incs=[vev.inc()])
        v(lambda e: e.memset(halobuf[:], 0.0), incs=[vev.inc()])
        v(lambda e: e.memset(gbuf[:], 0.0), incs=[vev.inc()])
        P.drain("vector")

        for (dst, src) in ((convw, dconvw), (convb, dconvb), (wsb, dwsb),
                           (biasb, dbb), (selb, dsel),
                           (identb, didb), (halom, dhm)):
            dma(sy, lambda e, dst=dst, src=src: e.dma_start(out=dst[:],
                                                            in_=src[:]), sw)
        dma(sy, lambda e: e.dma_start(
            out=wbstage.rearrange("p (c s) -> p c s", c=CC),
            in_=dwB[:].rearrange("(c p) s -> p c s", p=128)), sw)
        a(lambda e: e.activation(out=wbh[:], in_=wbstage, func=AF.Copy),
          waits=[sw.now()], incs=[sev.inc()])
        v(lambda e: e.tensor_sub(wbl[:], wbstage, wbh[:]),
          waits=[sev.now()], incs=[vev.inc()])
        for h in range(2):
            dma(sy, lambda e, h=h: e.dma_start(
                out=wcstage, in_=dwC[:, h * D:(h + 1) * D]), sw,
                [sev.now(), vev.now()])
            a(lambda e, h=h: e.activation(out=wch[:, h * D:(h + 1) * D],
                                          in_=wcstage, func=AF.Copy),
              waits=[sw.now()], incs=[sev.inc()])
            v(lambda e, h=h: e.tensor_sub(wcl[:, h * D:(h + 1) * D], wcstage,
                                          wch[:, h * D:(h + 1) * D]),
              waits=[sev.now()], incs=[vev.inc()])
        for k in range(KC):
            wts = [sev.now()] if k > 0 else []
            dma(sy, lambda e, k=k: e.dma_start(
                out=wqi8[:], in_=dwq[k * 128:(k + 1) * 128, :]), sw, wts)
            a(lambda e, k=k: e.activation(
                out=wq_b[:, k * D:(k + 1) * D], in_=wqi8[:], func=AF.Copy),
              waits=[sw.now()], incs=[sev.inc()])

        def load_split_big(chunks, wait_first=(), stages=None,
                           copy_first=()):
            if stages is None:
                stages = [wstage[:, 0:D], wstage2[:, 0:D],
                          ZS[0]["scrA"][:, 0:D], ZS[1]["scrA"][:, 0:D]]
            marks = [list(wait_first) for _ in stages]
            i = 0
            pieces = []
            for (off0, width0, src0) in chunks:
                nh = (width0 + D - 1) // D
                for h in range(nh):
                    pieces.append((off0 + h * D,
                                   min(D, width0 - h * D),
                                   src0[:, h * D:h * D + min(D, width0 - h * D)]))
            for (off, width, src) in pieces:
                stg = stages[i % len(stages)]
                swk = sw if i % 2 == 0 else sw2
                dma(sy, lambda e, off=off, width=width, src=src,
                    stg=stg: e.dma_start(
                    out=stg[:, :width] if False else stg[0:128, 0:width],
                    in_=src), swk, marks[i % len(stages)])
                a(lambda e, off=off, width=width, stg=stg: e.activation(
                    out=wxh[:, off:off + width], in_=stg[0:128, 0:width],
                    func=AF.Copy),
                  waits=[swk.now()] + (list(copy_first)
                                       if i < len(stages) else []),
                  incs=[sev.inc()])
                v(lambda e, off=off, width=width, stg=stg: e.tensor_sub(
                    wxl[:, off:off + width], stg[0:128, 0:width],
                    wxh[:, off:off + width]),
                  waits=[sev.now()], incs=[vev.inc()])
                marks[i % len(stages)] = [sev.now(), vev.now()]
                i += 1

        prep_done = [sev.now(), vev.now(), sw.now()]
        xnt2f = xnTh2[:].bitcast(F32)
        xnt2g = xnTl2[:].bitcast(F32)
        load_split_big([(k * DI, DI, dwin[k * 128:(k + 1) * 128, 0:DI])
                        for k in range(KC)],
                       wait_first=[sev.now(), vev.now()],
                       stages=[xnt2f[:, 0:D], xnt2f[:, D:2 * D],
                               xnt2g[:, 0:D], xnt2g[:, D:2 * D]])

        def vc(f, waits=()):
            v(f, waits=waits, incs=[vev.inc()])
            return vev.now()

        def ac(f, waits=()):
            a(f, waits=waits, incs=[sev.inc()])
            return sev.now()

        def run_gen(gn):
            for _ in gn:
                pass

        def duet(*gens_in):
            gens = list(gens_in)
            while gens:
                nxt = []
                for gn in gens:
                    try:
                        next(gn)
                        nxt.append(gn)
                    except StopIteration:
                        pass
                gens = nxt

        def gen_ln(xv, ntok, need_quant, waits0, Z, res):
            scrA = Z["scrA"]; stats = Z["stats"]
            w0 = list(waits0) + [Z["vtail"], Z["stail"]]
            sq_done = ac(lambda e: e.activation(out=sqS[0:ntok, :], in_=xv,
                                                func=AF.Square,
                                                accum_out=stats[0:ntok, 1:2]),
                         waits=w0 + [sq_guard[0]])
            sq_guard[0] = sq_done
            e1 = vc(lambda e: e.tensor_reduce(out=stats[0:ntok, 0:1], in_=xv,
                                              axis=AXX, op=ALU.add),
                    waits=w0)
            yield
            e2 = vc(lambda e: e.tensor_scalar(
                out=stats[0:ntok, 2:3], in0=stats[0:ntok, 0:1],
                scalar1=-1.0 / D, scalar2=None, op0=ALU.mult), waits=[e1])
            yield
            e3 = vc(lambda e: e.tensor_mul(stats[0:ntok, 3:4],
                                           stats[0:ntok, 2:3],
                                           stats[0:ntok, 2:3]), waits=[e2])
            yield
            e4 = vc(lambda e: e.tensor_scalar(
                out=stats[0:ntok, 4:5], in0=stats[0:ntok, 1:2],
                scalar1=1.0 / D, scalar2=stats[0:ntok, 3:4],
                op0=ALU.mult, op1=ALU.subtract), waits=[e3, sq_done])
            sq2 = ac(lambda e: e.activation(out=stats[0:ntok, 5:6],
                                            in_=stats[0:ntok, 4:5],
                                            func=AF.Sqrt,
                                            bias=epsln[0:ntok, :]),
                     waits=[e4])
            yield
            if need_quant:
                e7 = vc(lambda e: e.tensor_scalar(
                    out=scrA[0:ntok, :], in0=xv, scalar1=stats[0:ntok, 2:3],
                    scalar2=None, op0=ALU.add), waits=[e2])
                yield
            e6 = vc(lambda e: e.reciprocal(out=stats[0:ntok, 6:7],
                                           in_=stats[0:ntok, 5:6]),
                    waits=[sq2])
            yield
            if not need_quant:
                Z["vtail"] = e6
                Z["stail"] = sq2
                res["ln"] = e6
                return
            e8 = vc(lambda e: e.tensor_reduce(out=stats[0:ntok, 7:8],
                                              in_=scrA[0:ntok, :], axis=AXX,
                                              op=ALU.max,
                                              apply_absolute_value=True),
                    waits=[e7])
            yield
            e9 = vc(lambda e: e.tensor_mul(stats[0:ntok, 8:9],
                                           stats[0:ntok, 7:8],
                                           stats[0:ntok, 6:7]),
                    waits=[e8, e6])
            yield
            e10 = vc(lambda e: e.tensor_scalar(
                out=stats[0:ntok, 9:10], in0=stats[0:ntok, 8:9],
                scalar1=EPS_Q, scalar2=1.0 / 127.0,
                op0=ALU.max, op1=ALU.mult), waits=[e9])
            yield
            e13 = vc(lambda e: e.tensor_scalar(
                out=stats[0:ntok, 13:14], in0=stats[0:ntok, 9:10],
                scalar1=0.5, scalar2=None, op0=ALU.mult), waits=[e10])
            yield
            e11 = vc(lambda e: e.reciprocal(out=stats[0:ntok, 11:12],
                                            in_=stats[0:ntok, 9:10]),
                     waits=[e10])
            yield
            e12 = vc(lambda e: e.tensor_mul(stats[0:ntok, 12:13],
                                            stats[0:ntok, 6:7],
                                            stats[0:ntok, 11:12]),
                     waits=[e11])
            Z["vtail"] = e12
            Z["stail"] = sq2
            res["ln"] = e12

        def emit_ln(xv, ntok, need_quant, waits0, Z):
            res = {}
            run_gen(gen_ln(xv, ntok, need_quant, waits0, Z, res))
            return res["ln"]

        def gen_quant(ntok, Z, R, lnq, res):
            scrA = Z["scrA"]; scrB = Z["scrB"]
            scrGE = Z["scrGE"]; scrLE = Z["scrLE"]
            qbf = Z["qbf"]; stats = Z["stats"]
            q1 = ac(lambda e: e.activation(
                out=scrB[0:ntok, :], in_=scrA[0:ntok, :], func=AF.Copy,
                scale=stats[0:ntok, 12:13]), waits=[lnq])
            q2 = vc(lambda e: e.tensor_copy(out=scrA[0:ntok, :].bitcast(I32),
                                            in_=scrB[0:ntok, :]),
                    waits=[q1])
            yield
            q3 = vc(lambda e: e.tensor_copy(out=qbf[0:ntok, :],
                                            in_=scrA[0:ntok, :].bitcast(I32)),
                    waits=[q2, R["qbf"]])
            yield
            q4 = vc(lambda e: e.tensor_sub(scrB[0:ntok, :], scrB[0:ntok, :],
                                           qbf[0:ntok, :]), waits=[q3])
            yield
            qGE = vc(lambda e: e.tensor_scalar(
                out=scrGE[0:ntok, :], in0=scrB[0:ntok, :], scalar1=0.5,
                scalar2=None, op0=ALU.is_ge), waits=[q4, R["xn2"]])
            yield
            qLE = vc(lambda e: e.tensor_scalar(
                out=scrLE[0:ntok, :], in0=scrB[0:ntok, :], scalar1=-0.5,
                scalar2=None, op0=ALU.is_le), waits=[q4])
            yield
            q7 = vc(lambda e: e.tensor_sub(scrB[0:ntok, :], scrGE[0:ntok, :],
                                           scrLE[0:ntok, :]),
                    waits=[qGE, qLE])
            yield
            q8 = vc(lambda e: e.tensor_add(qbf[0:ntok, :], qbf[0:ntok, :],
                                           scrB[0:ntok, :]),
                    waits=[q7])
            Z["vtail"] = q8
            res["qd"] = q8

        bank_free = {i: ("vv", 0) for i in range(8)}
        RS = [{"xn2": ("tt", 0), "qbf": ("tt", 0), "xqT": ("tt", 0)}
              for _ in range(2)]

        def gen_ffn_part1(xbuf, ntok, x_wait, Z, R, res):
            r = {}
            yield from gen_ln(xbuf[0:ntok, :], ntok, True, x_wait, Z, r)
            yield from gen_quant(ntok, Z, R, r["ln"], res)

        def emit_ffn_part1(xbuf, ntok, x_wait, Z, R):
            res = {}
            run_gen(gen_ffn_part1(xbuf, ntok, x_wait, Z, R, res))
            return res["qd"]

        def emit_ffn_part2a(xbuf, ntok, qdone, Z, R, b_q=0, b_h=(1, 2)):
            qbf = Z["qbf"]; xqT = Z["xqT"]; stats = Z["stats"]
            scrB = Z["scrB"]
            pbq = pb[b_q][:].bitcast(BF16)
            stq = ntok if ntok % 2 == 0 else ntok + 1
            for c in range(KC):
                wts = [qdone, bank_free[b_q]] if c == 0 else []
                t(lambda e, c=c, ntok=ntok, stq=stq, pbq=pbq,
                  qbf=qbf: e.matmul(
                    pbq[:, c * stq:c * stq + ntok],
                    qbf[0:ntok, c * 128:(c + 1) * 128],
                    identb[0:ntok, 0:ntok],
                    is_transpose=True, start=(c == 0), stop=(c == KC - 1)),
                  waits=wts, incs=([tmm.inc()] if c == KC - 1 else []))
            trq_done = tmm.now()
            R["qbf"] = trq_done
            if stq == ntok:
                a(lambda e, ntok=ntok, pbq=pbq, xqT=xqT: e.activation(
                    out=xqT[:, 0:KC * ntok], in_=pbq[:, 0:KC * ntok],
                    func=AF.Copy),
                  waits=[trq_done, R["xqT"]], incs=[sev.inc()])
            else:
                a(lambda e, ntok=ntok, stq=stq, pbq=pbq,
                  xqT=xqT: e.activation(
                    out=xqT[:, 0:KC * ntok].rearrange(
                        "p (c q) -> p c q", q=ntok),
                    in_=pbq[:, 0:KC * stq].rearrange(
                        "p (c q) -> p c q", q=stq)[:, :, 0:ntok],
                    func=AF.Copy),
                  waits=[trq_done, R["xqT"]], incs=[sev.inc()])
            bank_free[b_q] = sev.now()
            xqt_done = sev.now()
            for k in range(KC):
                for half in range(2):
                    wts = []
                    if k == 0:
                        wts = [bank_free[b_h[half]]]
                        if half == 0:
                            wts.append(xqt_done)
                    t(lambda e, half=half, k=k, ntok=ntok, xqT=xqT: e.matmul(
                        pb[b_h[half]][0:ntok, :],
                        xqT[:, k * ntok:(k + 1) * ntok],
                        wq_b[:, k * D + half * 512: k * D + half * 512 + 512],
                        start=(k == 0), stop=(k == KC - 1)),
                      waits=wts, incs=([tmm.inc()]
                                       if (k == KC - 1 and half == 1) else []))
            ffn_done = tmm.now()
            R["xqT"] = ffn_done
            for half in range(2):
                a(lambda e, half=half, ntok=ntok, scrB=scrB,
                  stats=stats: e.activation(
                    out=scrB[0:ntok, half * 512:half * 512 + 512],
                    in_=pb[b_h[half]][0:ntok, :], func=AF.Copy,
                    scale=stats[0:ntok, 13:14]),
                  waits=([ffn_done] if half == 0 else []), incs=[sev.inc()])
                bank_free[b_h[half]] = sev.now()
            Z["stail"] = sev.now()
            return sev.now()

        def gen_ffn_part2b(xbuf, x2buf, ntok, evac_done, Z, x2_wait, res):
            scrB = Z["scrB"]
            p1 = vc(lambda e, ntok=ntok, scrB=scrB: e.tensor_mul(
                scrB[0:ntok, :], scrB[0:ntok, :], wsb[0:ntok, :]),
                waits=[evac_done, Z["vtail"]])
            yield
            p2 = vc(lambda e, ntok=ntok, scrB=scrB: e.tensor_add(
                scrB[0:ntok, :], scrB[0:ntok, :], xbuf[0:ntok, :]),
                waits=[p1])
            yield
            p3 = vc(lambda e, ntok=ntok, scrB=scrB: e.scalar_tensor_tensor(
                out=x2buf[0:ntok, :], in0=biasb[0:ntok, :], scalar=0.5,
                in1=scrB[0:ntok, :], op0=ALU.mult, op1=ALU.add),
                waits=[p2] + list(x2_wait))
            Z["vtail"] = p3
            res["x2"] = p3

        def emit_ffn_part2b(xbuf, x2buf, ntok, evac_done, Z, x2_wait=()):
            res = {}
            run_gen(gen_ffn_part2b(xbuf, x2buf, ntok, evac_done, Z,
                                   x2_wait, res))
            return res["x2"]

        def emit_ffn_part2(xbuf, x2buf, ntok, qdone, Z, R, b_q=0,
                           b_h=(1, 2), x2_wait=()):
            ed = emit_ffn_part2a(xbuf, ntok, qdone, Z, R, b_q, b_h)
            return emit_ffn_part2b(xbuf, x2buf, ntok, ed, Z, x2_wait)

        def emit_ffn_block(xbuf, x2buf, ntok, x_wait, Z, R, b_q=0,
                           b_h=(1, 2), x2_wait=()):
            qdone = emit_ffn_part1(xbuf, ntok, x_wait, Z, R)
            return emit_ffn_part2(xbuf, x2buf, ntok, qdone, Z, R, b_q, b_h,
                                  x2_wait)

        def gen_ln2_part1(x2buf, ntok, x2_wait, Z, R, res):
            scrA = Z["scrA"]; scrGE = Z["scrGE"]; scrLE = Z["scrLE"]
            stats = Z["stats"]
            r = {}
            yield from gen_ln(x2buf[0:ntok, :], ntok, False, x2_wait, Z, r)
            xn2_done = vc(
                lambda e, ntok=ntok, scrA=scrA, stats=stats: e.tensor_scalar(
                    out=scrA[0:ntok, :], in0=x2buf[0:ntok, :],
                    scalar1=stats[0:ntok, 2:3], scalar2=stats[0:ntok, 6:7],
                    op0=ALU.add, op1=ALU.mult),
                waits=[r["ln"]])
            yield
            res["xn2"] = xn2_done
            sp1 = ac(lambda e, ntok=ntok, scrGE=scrGE, scrA=scrA: e.activation(
                out=scrGE[0:ntok, :], in_=scrA[0:ntok, :], func=AF.Copy),
                waits=[xn2_done, R["xn2"]])
            sp2 = vc(lambda e, ntok=ntok, scrLE=scrLE, scrA=scrA,
                     scrGE=scrGE: e.tensor_sub(scrLE[0:ntok, :],
                                               scrA[0:ntok, :],
                                               scrGE[0:ntok, :]),
                     waits=[sp1])
            Z["vtail"] = sp2
            res["sd"] = sp2

        def emit_ln2_part1(x2buf, ntok, x2_wait, Z, R):
            res = {}
            run_gen(gen_ln2_part1(x2buf, ntok, x2_wait, Z, R, res))
            return res["xn2"], res["sd"]

        def emit_ln2_part2(ntok, col_off, split_done, xnh, xnl, Z, R,
                           banks=(3, 4), first_wait=()):
            scrGE = Z["scrGE"]; scrLE = Z["scrLE"]
            """bf16 transposes of scrGE/scrLE into xnh/xnl columns."""
            stq = ntok if ntok % 2 == 0 else ntok + 1
            for grp, (srcb, dstb) in enumerate(((scrGE, xnh), (scrLE, xnl))):
                bank = banks[grp]
                pbv = pb[bank][:].bitcast(BF16)
                for c in range(KC):
                    wts = ([split_done, bank_free[bank]] + list(first_wait)) \
                        if c == 0 else []
                    t(lambda e, c=c, ntok=ntok, stq=stq, pbv=pbv,
                      srcb=srcb: e.matmul(
                        pbv[:, c * stq:c * stq + ntok],
                        srcb[0:ntok, c * 128:(c + 1) * 128],
                        identb[0:ntok, 0:ntok],
                        is_transpose=True, start=(c == 0), stop=(c == KC - 1)),
                      waits=wts, incs=([tmm.inc()] if c == KC - 1 else []))
                tr_done = tmm.now()
                dst_ap = dstb[:].rearrange(
                    "p (c q) -> p c q", q=T_tile)[:, :,
                                                  col_off:col_off + ntok]
                src_ap = pbv[:, 0:KC * stq].rearrange(
                    "p (c q) -> p c q", q=stq)[:, :, 0:ntok]
                a(lambda e, dst_ap=dst_ap, src_ap=src_ap: e.activation(
                    out=dst_ap, in_=src_ap, func=AF.Copy),
                  waits=[tr_done], incs=[sev.inc()])
                bank_free[bank] = sev.now()
            R["xn2"] = tmm.now()

        def emit_ln2_transpose(x2buf, ntok, col_off, x2_wait, Z, R,
                               banks=(3, 4), xnh=None, xnl=None,
                               first_wait=()):
            if xnh is None:
                xnh, xnl = xnTh, xnTl
            xn2_done, split_done = emit_ln2_part1(x2buf, ntok, x2_wait, Z, R)
            emit_ln2_part2(ntok, col_off, split_done, xnh, xnl, Z, R, banks,
                           first_wait)
            return xn2_done


        def emit_win_chunk_mm(cchunk, ntok, bank, ready_wait,
                              xnh=None, xnl=None):
            if xnh is None:
                xnh, xnl = xnTh, xnTl
            n0 = cchunk * 128
            first = True
            for k in range(KC):
                for (wbuf, xbuf_) in ((wxh, xnh), (wxh, xnl), (wxl, xnh)):
                    is_first = first
                    first = False
                    is_last = (k == KC - 1 and wbuf is wxl)
                    wts = (list(ready_wait) + [bank_free[bank]]) \
                        if is_first else []
                    t(lambda e, k=k, wbuf=wbuf, xbuf_=xbuf_, n0=n0,
                      is_first=is_first, is_last=is_last, ntok=ntok: e.matmul(
                          pb[bank][:, 0:ntok],
                          wbuf[:, k * DI + n0:k * DI + n0 + 128],
                          xbuf_[:, k * T_tile:k * T_tile + ntok],
                          start=is_first, stop=is_last),
                      waits=wts, incs=([tmm.inc()] if is_last else []))
            return tmm.now()

        if upto >= 1:
            dma(sy, lambda e: e.dma_start(out=bufx[0][0:3, :], in_=dxh[:]),
                sxp[0])
            halo_wait = [sxp[0].now()] + prep_done
            halo_x2 = emit_ffn_block(bufx[0], bufx2[0], 3, halo_wait,
                                     ZS[0], RS[0])
            halo_xn2 = emit_ln2_transpose(bufx2[0], 3, 0, [halo_x2],
                                          ZS[0], RS[0])
            halo_xn = [vev.now(), sev.now()]
            hbuf_mark = [None] * CC
            for c in range(CC):
                bank = 5 + (c % 2)
                mm_done = emit_win_chunk_mm(c, 3, bank, halo_xn if c < 2 else [])
                hbuf_mark[c] = vc(lambda e, c=c, bank=bank: e.tensor_mul(
                    halobuf[:, c * 3:(c + 1) * 3], pb[bank][:, 0:3], halom[:]),
                    waits=[mm_done])
                bank_free[bank] = hbuf_mark[c]
        else:
            halo_x2 = ("vv", 0)
            halo_xn2 = ("vv", 0)
            hbuf_mark = [("vv", 0)] * CC
        x_free = [halo_x2, ("vv", 0)]
        x2_free = [[halo_xn2, ("dsp0", 0)], [("vv", 0), ("dsp1", 0)]]
        xn_used_halo = tmm.now()
        xp_free = [[("vv", 0)], [("vv", 0)]]
        xc_free = [("tt", 0)] * 2
        scan_done = ("vv", 0)
        hl_free = [("dhs", 0), ("dhs", 0)]

        xnhb = [xnTh, xnTh2]
        xnlb = [xnTl, xnTl2]
        xn_used2 = [xn_used_halo, ("tt", 0)]
        xn_ready2 = [[("vv", 0)], [("vv", 0)]]
        cbuf_free = {"xcf": [("ss", 0)], "xcg": ("ss", 0), "g": ("vv", 0)}

        def gen_subA_p1(j, s, res):
            gsub = j * n_sub + s
            par = gsub % 2
            trow = j * T_tile + s * 128
            dma(sy, lambda e, trow=trow, par=par: e.dma_start(
                out=bufx[par][:], in_=dx[trow:trow + 128, :]),
                sxp[par], [x_free[par]])
            yield from gen_ffn_part1(bufx[par], 128, [sxp[par].now()],
                                     ZS[s % 2], RS[s % 2], res)

        def subA_p1(j, s):
            res = {}
            run_gen(gen_subA_p1(j, s, res))
            return res["qd"]

        def subA_p2a(j, s, qd):
            return emit_ffn_part2a(bufx[(j * n_sub + s) % 2], 128, qd,
                                   ZS[s % 2], RS[s % 2])

        def gen_subA_p2b(j, s, ed, res):
            gsub = j * n_sub + s
            par = gsub % 2
            trow = j * T_tile + s * 128
            r = {}
            yield from gen_ffn_part2b(bufx[par], bufx2[par], 128, ed,
                                      ZS[s % 2], x2_free[par], r)
            x2done = r["x2"]
            x_free[par] = x2done
            dma(g, lambda e, trow=trow, par=par: e.dma_start(
                out=dx2[trow:trow + 128, :], in_=bufx2[par][:]),
                sspp[par], [x2done])
            r2 = {}
            yield from gen_ln2_part1(bufx2[par], 128, [x2done],
                                     ZS[s % 2], RS[s % 2], r2)
            x2_free[par] = [r2["xn2"], sspp[par].now()]
            res["sd"] = r2["sd"]

        def subA_p2b(j, s, ed):
            res = {}
            run_gen(gen_subA_p2b(j, s, ed, res))
            return res["sd"]

        def subA_p3(j, s, sd):
            jp = j % 2
            fw = [xn_used2[jp]] if s == 0 else []
            emit_ln2_part2(128, s * 128, sd, xnhb[jp], xnlb[jp],
                           ZS[s % 2], RS[s % 2], first_wait=fw)
            if s == n_sub - 1:
                xn_ready2[jp] = [vev.now(), sev.now()]

        def emit_u_mm(centry):
            c, par, xch_done, xcl_done = centry
            for ui, (wb_, xc_) in enumerate(((wbh, xch), (wbh, xcl),
                                             (wbl, xch))):
                is_first_g = (c == 0 and ui == 0)
                is_last_g = (c == CC - 1 and ui == 2)
                wts = []
                if ui == 0:
                    wts = [xch_done, xcl_done]
                    if is_first_g:
                        wts.append(bank_free[7])
                t(lambda e, c=c, wb_=wb_, xc_=xc_, par=par,
                  is_first_g=is_first_g, is_last_g=is_last_g: e.matmul(
                      pu[:, 0:T_tile],
                      wb_[:, c * DS:(c + 1) * DS],
                      xc_[par][:, :],
                      start=is_first_g, stop=is_last_g),
                  waits=wts, incs=[tmm.inc()] if ui == 2 else [])
            xc_free[par] = tmm.now()

        def chunkA_stage1(j, c):
            jp = j % 2
            bank = 3 + (c % 4)
            par = c % 2
            mm_done = emit_win_chunk_mm(c, T_tile, bank,
                                        xn_ready2[jp] if c < 2 else [],
                                        xnhb[jp], xnlb[jp])
            halo_ld = ac(lambda e, c=c, par=par: e.activation(
                out=xp[par][:, 0:3], in_=halobuf[:, c * 3:(c + 1) * 3],
                func=AF.Copy),
                waits=list(xp_free[par]) + [hbuf_mark[c]])
            evx = ac(lambda e, par=par, bank=bank: e.activation(
                out=xp[par][:, 3:3 + T_tile], in_=pb[bank][:, 0:T_tile],
                func=AF.Copy),
                waits=[mm_done])
            bank_free[bank] = evx
            hbuf_mark[c] = ac(lambda e, c=c, par=par: e.activation(
                out=halobuf[:, c * 3:(c + 1) * 3],
                in_=xp[par][:, T_tile:T_tile + 3], func=AF.Copy),
                waits=[evx])
            return dict(evx=evx, halo_ld=halo_ld, par=par)

        def chunkA_stage2(j, c, st, pump=lambda: None):
            par = st["par"]
            par4 = c % 2
            xp_ready = [st["evx"], st["halo_ld"]]
            ct0 = vc(lambda e, c=c, par=par: e.tensor_scalar(
                out=xcf[:, :], in0=xp[par][:, 0:T_tile],
                scalar1=convw[:, 4 * c:4 * c + 1], scalar2=None,
                op0=ALU.mult),
                waits=list(xp_ready) + list(cbuf_free["xcf"]))
            pump()
            ct1 = vc(lambda e, c=c, par=par: e.scalar_tensor_tensor(
                out=xcg[:, :], in0=xp[par][:, 1:1 + T_tile],
                scalar=convw[:, 4 * c + 1:4 * c + 2], in1=xcf[:, :],
                op0=ALU.mult, op1=ALU.add),
                waits=[ct0, cbuf_free["xcg"]])
            pump()
            ct2 = vc(lambda e, c=c, par=par: e.scalar_tensor_tensor(
                out=xcf[:, :], in0=xp[par][:, 2:2 + T_tile],
                scalar=convw[:, 4 * c + 2:4 * c + 3], in1=xcg[:, :],
                op0=ALU.mult, op1=ALU.add),
                waits=[ct1])
            pump()
            conv_done = vc(lambda e, c=c, par=par: e.scalar_tensor_tensor(
                out=xcg[:, :], in0=xp[par][:, 3:3 + T_tile],
                scalar=convw[:, 4 * c + 3:4 * c + 4], in1=xcf[:, :],
                op0=ALU.mult, op1=ALU.add),
                waits=[ct2])
            xp_free[par] = [conv_done]
            sl = ac(lambda e, c=c: e.activation(
                out=xcf[:, :], in_=xcg[:, :], func=AF.Silu,
                bias=convb[:, c:c + 1]),
                waits=[conv_done])
            cbuf_free["xcg"] = sl
            pump()
            xch_done = ac(lambda e, par4=par4: e.activation(
                out=xch[par4][:, :], in_=xcf[:, :], func=AF.Copy),
                waits=[sl, xc_free[par4]])
            pump()
            v(lambda e, par4=par4: e.tensor_sub(
                xcl[par4][:, :], xcf[:, :], xch[par4][:, :]),
              waits=[xch_done], incs=[vev.inc()])
            xcl_done = vev.now()
            pump()
            cbuf_free["xcf"] = [xch_done, xcl_done]
            u_pend.append((c, par4, xch_done, xcl_done))

        if upto >= 2:
            u_pend = []
            subA_p1_q = {}
            subA_p2_sd = {}
            subA_all_first = True
            for s0 in range(0, n_sub, 2):
                if s0 + 1 < n_sub:
                    r0, r1 = {}, {}
                    duet(gen_subA_p1(0, s0, r0),
                         gen_subA_p1(0, s0 + 1, r1))
                    ed0 = subA_p2a(0, s0, r0["qd"])
                    ed1 = subA_p2a(0, s0 + 1, r1["qd"])
                    h0, h1 = {}, {}
                    duet(gen_subA_p2b(0, s0, ed0, h0),
                         gen_subA_p2b(0, s0 + 1, ed1, h1))
                    subA_p3(0, s0, h0["sd"])
                    subA_p3(0, s0 + 1, h1["sd"])
                else:
                    qd = subA_p1(0, s0)
                    ed = subA_p2a(0, s0, qd)
                    sd = subA_p2b(0, s0, ed)
                    subA_p3(0, s0, sd)
            if n_sub == 2:
                events = [(0, "p1", 0), (0, "p1", 1), (9, "p2a", 0),
                          (10, "p2a", 1), (10, "p2b", 0), (10, "p2b", 1),
                          (14, "p3", 0), (15, "p3", 1)]
            elif n_sub == 4:
                events = [(0, "p1", 0), (0, "p1", 1),
                          (4, "p2a", 0), (4, "p2b", 0),
                          (5, "p2a", 1), (5, "p2b", 1),
                          (8, "p3", 0), (8, "p1", 2),
                          (9, "p3", 1), (9, "p1", 3),
                          (11, "p2a", 2), (11, "p2b", 2),
                          (12, "p2a", 3), (12, "p2b", 3),
                          (14, "p3", 2), (14, "p3", 3)]
            else:
                events = []
                for s in range(n_sub):
                    base = (CC // n_sub) * s
                    events += [(base, "p1", s), (base + 1, "p2a", s),
                               (base + 2, "p2b", s), (base + 3, "p3", s)]
            active = []
            rr = [0]

            def pumpA():
                done = 0
                while active and done < 2:
                    i = rr[0] % len(active)
                    try:
                        next(active[i])
                        rr[0] = i + 1
                        done += 1
                    except StopIteration:
                        active.pop(i)

            def exhaust(gn):
                try:
                    while True:
                        next(gn)
                except StopIteration:
                    pass
                if gn in active:
                    active.remove(gn)

            st_g = {}
            st_res = {}
            st_ed = {}
            for j in range(n_tiles):
                jp = j % 2
                t0 = j * T_tile
                st1 = None
                for c in range(CC):
                    while len(u_pend) >= 2:
                        emit_u_mm(u_pend.pop(0))
                    if st1 is None:
                        st1 = chunkA_stage1(j, 0)
                    st1_next = (chunkA_stage1(j, c + 1)
                                if c + 1 < CC else None)
                    chunkA_stage2(j, c, st1, pumpA)
                    st1 = st1_next
                    if j + 1 < n_tiles:
                        for (cc_, stage, s) in events:
                            if cc_ != c:
                                continue
                            if stage == "p1":
                                st_res[s] = {}
                                st_g[s] = gen_subA_p1(j + 1, s, st_res[s])
                                active.append(st_g[s])
                            elif stage == "p2a":
                                exhaust(st_g[s])
                                st_ed[s] = subA_p2a(j + 1, s,
                                                    st_res[s]["qd"])
                            elif stage == "p2b":
                                st_res[s] = {}
                                st_g[s] = gen_subA_p2b(j + 1, s, st_ed[s],
                                                       st_res[s])
                                active.append(st_g[s])
                            elif stage == "p3":
                                exhaust(st_g[s])
                                subA_p3(j + 1, s, st_res[s]["sd"])
                while u_pend:
                    emit_u_mm(u_pend.pop(0))
                xn_used2[jp] = tmm.now()
                u_done = tmm.now()
                u_ev = vc(lambda e: e.tensor_scalar(out=u_t[:, :],
                                                    in0=pu[:, 0:T_tile],
                                                    scalar1=0.1, scalar2=None,
                                                    op0=ALU.mult),
                          waits=[u_done, scan_done])
                bank_free[7] = u_ev
                hlp = j % 2
                if j == 0:
                    scan_done = vc(lambda e: e.tensor_tensor_scan(
                        out=hl_tmp[0][:, :], data0=d09[:, :],
                        data1=u_t[:, :], initial=0.0,
                        op0=ALU.mult, op1=ALU.add),
                        waits=[u_ev])
                else:
                    scan_done = vc(lambda e, hlp=hlp: e.tensor_tensor_scan(
                        out=hl_tmp[hlp][:, :], data0=d09[:, :],
                        data1=u_t[:, :],
                        initial=hl_tmp[1 - hlp][:, T_tile - 1:T_tile],
                        op0=ALU.mult, op1=ALU.add),
                        waits=[u_ev, hl_free[hlp]])
                dma(g, lambda e, t0=t0, hlp=hlp: e.dma_start(
                    out=dhlc[:, t0:t0 + T_tile], in_=hl_tmp[hlp][:, :]),
                    shs, [scan_done])
                hl_free[hlp] = shs.now()


        a_last_mm = tmm.now()

        if upto >= 3:
            dma(g, lambda e: e.dma_start(out=dcci[:, :],
                                         in_=hl_tmp[(n_tiles - 1) % 2][
                                             :, T_tile - 1:T_tile]),
                scc, [scan_done])
            g(lambda e: e.collective_compute(
                "AllGather", ALU.bypass,
                replica_groups=[list(range(n_cores))],
                ins=[dcci.ap().opt()], outs=[dcco.ap().opt()]),
              waits=[scc.now()], incs=[gev.inc()])
            dma(g, lambda e: e.dma_start(out=gbuf[0:DS * n_cores, :],
                                         in_=dcco.ap()),
                scc, [gev.now()])
            gh1 = vc(lambda e: e.tensor_copy(out=ghl[:, 0:1], in_=gbuf[:, :]),
                     waits=[scc.now()])
            ghl_done = vc(lambda e: e.tensor_sub(ghl[:, 1:2], gbuf[:, :],
                                                 ghl[:, 0:1]),
                          waits=[gh1])
            t(lambda e: e.matmul(pb[0][0:DS, 0:1], selb[:, :], ghl[:, 0:1],
                                 start=True, stop=False),
              waits=[ghl_done, bank_free[0]])
            t(lambda e: e.matmul(pb[0][0:DS, 0:1], selb[:, :], ghl[:, 1:2],
                                 start=False, stop=True),
              incs=[tmm.inc()])
            v(lambda e: e.tensor_copy(out=hin[:, :], in_=pb[0][0:DS, 0:1]),
              waits=[tmm.now()], incs=[vev.inc()])
            bank_free[0] = vev.now()
            hin_done = vev.now()
            decay_used = ("vv", 0)

        if upto >= 4:
            load_split_big([(k * DI, DI,
                             dwin[k * 128:(k + 1) * 128, DI:2 * DI])
                            for k in range(KC)], wait_first=[a_last_mm],
                           stages=[xnt2f[:, 0:D], xnt2f[:, D:2 * D],
                                   xnt2g[:, 0:D], xnt2g[:, D:2 * D]])
            b_w_done = [sev.now(), vev.now()]
            hh_used = ("tt", 0)
            decay_used = scan_done
            spill_a_done = [sspp[0].now(), sspp[1].now()]
            gk_free = [xp_free[0][0], xp_free[1][0]]
            ykh_free = [("dyh0", 0), ("dyh1", 0)]
            ykl_free = [("dyl0", 0), ("dyl1", 0)]
            yk_used = [("ss", 0)]

            def gen_subB_p1(j, s, res):
                gsub = j * n_sub + s
                par = gsub % 2
                trow = j * T_tile + s * 128
                dma(sy, lambda e, trow=trow, par=par: e.dma_start(
                    out=bufx2[par][:], in_=dx2[trow:trow + 128, :]),
                    sxp[par], list(spill_a_done) + list(x2_free[par]))
                extra = []
                r2 = {}
                yield from gen_ln2_part1(
                    bufx2[par], 128, [sxp[par].now()] + list(extra),
                    ZS[s % 2], RS[s % 2], r2)
                x2_free[par] = [r2["xn2"]]
                res["sd"] = r2["sd"]

            def subB_p1(j, s):
                res = {}
                run_gen(gen_subB_p1(j, s, res))
                return res["sd"]

            def subB_p3(j, s, sd):
                jp = j % 2
                fw = [xn_used2[jp]] if s == 0 else []
                emit_ln2_part2(128, s * 128, sd, xnhb[jp], xnlb[jp],
                               ZS[s % 2], RS[s % 2], first_wait=fw)
                if s == n_sub - 1:
                    xn_ready2[jp] = [vev.now(), sev.now()]

            def chunkB(j, c, hh_done):
                jp = j % 2
                bank = 5 + (c % 2)
                ybank = 1 + (c % 2)
                par = c % 2
                t0 = j * T_tile
                zw = list(xn_ready2[jp]) if c < 2 else []
                if j == 0 and c < 2:
                    zw = zw + b_w_done
                z_done = emit_win_chunk_mm(c, T_tile, bank, zw,
                                           xnhb[jp], xnlb[jp])
                for ui, (wc_, hb_) in enumerate(((wch, hh_t), (wch, hl_t),
                                                 (wcl, hh_t))):
                    wts = []
                    if ui == 0:
                        wts = [bank_free[ybank]]
                        if c < 2:
                            wts.append(hh_done)
                    t(lambda e, c=c, wc_=wc_, hb_=hb_, ybank=ybank,
                      ui=ui: e.matmul(
                          pb[ybank][:, 0:T_tile],
                          wc_[0:DS, c * 128:(c + 1) * 128],
                          hb_[0:DS, 0:T_tile],
                          start=(ui == 0), stop=(ui == 2)),
                      waits=wts, incs=([tmm.inc()] if ui == 2 else []))
                yp_done = tmm.now()
                a(lambda e, bank=bank, par=par: e.activation(
                    out=gk[par][:, :], in_=pb[bank][:, 0:T_tile],
                    func=AF.Silu),
                  waits=[z_done, gk_free[par]], incs=[sev.inc()])
                bank_free[bank] = sev.now()
                g_done = sev.now()
                v(lambda e, ybank=ybank, par=par: e.tensor_mul(
                    yk[:, :], pb[ybank][:, 0:T_tile], gk[par][:, :]),
                  waits=[yp_done, g_done] + list(yk_used), incs=[vev.inc()])
                bank_free[ybank] = vev.now()
                gk_free[par] = vev.now()
                yk_done = vev.now()
                a(lambda e, par=par: e.activation(
                    out=ykh[par][:, :], in_=yk[:, :], func=AF.Copy),
                  waits=[yk_done, ykh_free[par]], incs=[sev.inc()])
                ykh_done = sev.now()
                v(lambda e, par=par: e.tensor_sub(
                    ykl[par][:, :], yk[:, :], ykh[par][:, :]),
                  waits=[ykh_done, ykl_free[par]], incs=[vev.inc()])
                ykl_done = vev.now()
                yk_used[:] = [sev.now(), ykl_done]
                gs0 = t0 // 128
                dma(g, lambda e, c=c, gs0=gs0, par=par: e.dma_start(
                    out=dysh[gs0:gs0 + n_sub, :, c, :].rearrange(
                        "b p n -> p b n"),
                    in_=ykh[par][:, :].rearrange("p (b n) -> p b n",
                                                 b=n_sub)),
                    syph[par], [ykh_done])
                ykh_free[par] = syph[par].now()
                dma(g, lambda e, c=c, gs0=gs0, par=par: e.dma_start(
                    out=dysl[gs0:gs0 + n_sub, :, c, :].rearrange(
                        "b p n -> p b n"),
                    in_=ykl[par][:, :].rearrange("p (b n) -> p b n",
                                                 b=n_sub)),
                    sypl[par], [ykl_done])
                ykl_free[par] = sypl[par].now()

            subB_sd = {}
            for s0 in range(0, n_sub, 2):
                if s0 + 1 < n_sub:
                    r0, r1 = {}, {}
                    duet(gen_subB_p1(0, s0, r0),
                         gen_subB_p1(0, s0 + 1, r1))
                    subB_p3(0, s0, r0["sd"])
                    subB_p3(0, s0 + 1, r1["sd"])
                else:
                    sd = subB_p1(0, s0)
                    subB_p3(0, s0, sd)
            if n_sub == 4:
                p1_events = {1: (0, 1), 7: (2, 3)}
                p3_sched = {4: 0, 5: 1, 11: 2, 12: 3}
            else:
                p1_events = {1: tuple(range(n_sub))}
                p3_sched = ({6: 0, 8: 1} if n_sub == 2 else {6: 0})
            hstage = hl_tmp[0]
            hstage_used = hl_free[0] if n_tiles % 2 == 0 else hl_free[1]
            for j in range(n_tiles):
                jp = j % 2
                t0 = j * T_tile
                dma(sy, lambda e, t0=t0: e.dma_start(
                    out=decay_t[:, :], in_=ddecay[:, t0:t0 + T_tile]),
                    sdec, [decay_used])
                dma(sy, lambda e, t0=t0: e.dma_start(
                    out=hstage[:, :], in_=dhlc[:, t0:t0 + T_tile]),
                    shb, [hstage_used, hl_free[0], hl_free[1]])
                decay_used = vc(lambda e, t0=t0: e.scalar_tensor_tensor(
                    out=htmp[:, :], in0=decay_t[:, :], scalar=hin[:, :],
                    in1=hstage[:, :], op0=ALU.mult, op1=ALU.add),
                    waits=[sdec.now(), shb.now(), hin_done, hh_used])
                hstage_used = decay_used
                hcp = vc(lambda e: e.tensor_copy(out=hh_t[:, :],
                                                 in_=htmp[:, :]),
                         waits=[decay_used])
                hh_done = vc(lambda e: e.tensor_sub(hl_t[:, :], htmp[:, :],
                                                    hh_t[:, :]),
                             waits=[hcp])
                for c in range(CC):
                    chunkB(j, c, hh_done)
                    if j + 1 < n_tiles:
                        if c in p1_events:
                            ss = p1_events[c]
                            ress = {s: {} for s in ss}
                            duet(*[gen_subB_p1(j + 1, s, ress[s])
                                   for s in ss])
                            for s in ss:
                                subB_sd[s] = ress[s]["sd"]
                        if c in p3_sched:
                            s = p3_sched[c]
                            subB_p3(j + 1, s, subB_sd[s])
                xn_used2[jp] = tmm.now()
                hh_used = tmm.now()

            b_last_mm = tmm.now()
            spill_b_done = [syph[0].now(), syph[1].now(),
                            sypl[0].now(), sypl[1].now()]

        if upto >= 5:
            xntf = xnTh[:].bitcast(F32)
            xntg = xnTl[:].bitcast(F32)
            load_split_big([(k * D, D, dwout[k * 128:(k + 1) * 128, :])
                            for k in range(CC)],
                           wait_first=[xn_used2[0]],
                           stages=[xntf[:, 0:D], xntf[:, D:2 * D],
                                   xntg[:, 0:D], xntg[:, D:2 * D]],
                           copy_first=[b_last_mm])
            c_w_done = [sev.now(), vev.now()]
            ybh = [xnTh[:, 0:CC * 128], ybh1[:, 0:CC * 128]]
            ybl = [xnTl[:, 0:CC * 128], ybl1[:, 0:CC * 128]]
            HC = CC // 2
            yb_free_f = [b_last_mm, ("vv", 0)]
            yb_free_b = [b_last_mm, ("vv", 0)]
            out_free = [("dot0", 0), ("dot1", 0)]

            def emitC_yb_loads(gsub):
                par = gsub % 2
                trow = gsub * 128
                gsi = trow // 128
                cw = c_w_done if gsub < 2 else []
                dma(sy, lambda e, gsi=gsi, par=par: e.dma_start(
                    out=ybh[par][:, 0:HC * 128],
                    in_=dysh[gsi, :, 0:HC, :].rearrange("p c n -> p (c n)")),
                    sybp[par], list(spill_b_done) + [yb_free_f[par]] + cw)
                dma(g, lambda e, gsi=gsi, par=par: e.dma_start(
                    out=ybl[par][:, 0:HC * 128],
                    in_=dysl[gsi, :, 0:HC, :].rearrange("p c n -> p (c n)")),
                    sybl2[par], list(spill_b_done) + [yb_free_f[par]] + cw)
                ywf = [sybp[par].now(), sybl2[par].now()]
                dma(sy, lambda e, gsi=gsi, par=par: e.dma_start(
                    out=ybh[par][:, HC * 128:CC * 128],
                    in_=dysh[gsi, :, HC:CC, :].rearrange("p c n -> p (c n)")),
                    sybp[par], [yb_free_b[par]] + cw)
                dma(g, lambda e, gsi=gsi, par=par: e.dma_start(
                    out=ybl[par][:, HC * 128:CC * 128],
                    in_=dysl[gsi, :, HC:CC, :].rearrange("p c n -> p (c n)")),
                    sybl2[par], [yb_free_b[par]] + cw)
                ywb = [sybp[par].now(), sybl2[par].now()]
                return dict(gsub=gsub, par=par, trow=trow, ywf=ywf, ywb=ywb)

            def emitC_x2load(pd):
                par = pd["par"]
                trow = pd["trow"]
                dma(sy, lambda e, trow=trow, par=par: e.dma_start(
                    out=bufx2[par][:], in_=dx2[trow:trow + 128, :]),
                    sxp[par], list(x2_free[par]))
                pd["x2w"] = sxp[par].now()

            def emitC_wout(pd, first_sub):
                gsub = pd["gsub"]
                par = pd["par"]
                ywf = pd["ywf"]
                ywb = pd["ywb"]
                bp = (1, 2) if gsub % 2 == 0 else (5, 6)
                started = [False, False]
                back_waited = False
                for c in range(CC):
                    groups = (((ybh, wxh), (ybh, wxl)), ((ybl, wxh),))
                    for grp in groups:
                        for (yb_, wv) in grp:
                            for half in range(2):
                                is_first = not started[half]
                                started[half] = True
                                is_last = (c == CC - 1 and yb_ is ybl)
                                is_fend = (c == HC - 1 and yb_ is ybl)
                                wts = []
                                if is_first:
                                    wts = list(ywf) + [bank_free[bp[half]]]
                                    if first_sub and half == 0:
                                        wts += c_w_done
                                if c == HC and not back_waited:
                                    wts = wts + list(ywb)
                                    back_waited = True
                                t(lambda e, c=c, yb_=yb_, wv=wv, par=par,
                                  half=half, bp=bp, is_first=is_first,
                                  is_last=is_last: e.matmul(
                                      pb[bp[half]][:, :],
                                      yb_[par][:, c * 128:(c + 1) * 128],
                                      wv[:, c * D + half * 512:
                                         c * D + half * 512 + 512],
                                      start=is_first, stop=is_last),
                                  waits=wts,
                                  incs=([tmm.inc()]
                                        if ((is_last or is_fend)
                                            and half == 1) else []))
                    if c == HC - 1:
                        yb_free_f[par] = tmm.now()
                wo_done = tmm.now()
                yb_free_b[par] = wo_done
                pd["wo_done"] = wo_done
                pd["bp"] = bp

            def gen_x3_p1(pd):
                par = pd["par"]
                bp = pd["bp"]
                for half in range(2):
                    v(lambda e, half=half, par=par, bp=bp: e.tensor_add(
                        bufx[par][:, half * 512:half * 512 + 512],
                        pb[bp[half]][:, :],
                        bufx2[par][:, half * 512:half * 512 + 512]),
                      waits=([pd["wo_done"], pd["x2w"], x_free[par]]
                             if half == 0 else []),
                      incs=[vev.inc()])
                    bank_free[bp[half]] = vev.now()
                    if half == 0:
                        yield
                x3_done = vev.now()
                x2_free[par] = [x3_done]
                yield
                r = {}
                yield from gen_ffn_part1(bufx[par], 128, [x3_done],
                                         ZS[par], RS[par], r)
                pd["qd"] = r["qd"]

            def emitC_part2a(pd):
                pd["ed"] = emit_ffn_part2a(bufx[pd["par"]], 128, pd["qd"],
                                           ZS[pd["par"]], RS[pd["par"]],
                                           b_q=(0 if pd["par"] == 0 else 7),
                                           b_h=(3, 4))

            def gen_part2b_store(pd):
                par = pd["par"]
                trow = pd["trow"]
                r = {}
                yield from gen_ffn_part2b(bufx[par], bufx[par], 128,
                                          pd["ed"], ZS[par], (), r)
                x4_done = r["x2"]
                dma_nc(sy, lambda e, trow=trow, par=par: e.dma_start(
                    out=dout[trow:trow + 128, :], in_=bufx[par][:]),
                    sotp[par], [x4_done])
                x_free[par] = sotp[par].now()

            prev = None
            first_sub = True
            for gp in range(T_core // 256):
                if prev is not None:
                    duet(gen_x3_p1(prev[0]), gen_x3_p1(prev[1]))
                pds = []
                for gsub in (2 * gp, 2 * gp + 1):
                    pds.append(emitC_yb_loads(gsub))
                for pd in pds:
                    emitC_x2load(pd)
                for pd in pds:
                    emitC_wout(pd, first_sub)
                    first_sub = False
                if prev is not None:
                    emitC_part2a(prev[0])
                    emitC_part2a(prev[1])
                    duet(gen_part2b_store(prev[0]), gen_part2b_store(prev[1]))
                prev = pds
            duet(gen_x3_p1(prev[0]), gen_x3_p1(prev[1]))
            emitC_part2a(prev[0])
            emitC_part2a(prev[1])
            duet(gen_part2b_store(prev[0]), gen_part2b_store(prev[1]))

        if upto >= 5:
            g(None, waits=[sotp[0].now(), sotp[1].now()])
        else:
            g(None, waits=[("vv", P.streams["vv"].n), ("ss", P.streams["ss"].n),
                           ("tt", P.streams["tt"].n), ("gg", P.streams["gg"].n),
                           ("dsp0", P.streams["dsp0"].n), ("dsp1", P.streams["dsp1"].n),
                           ("dyh0", P.streams["dyh0"].n), ("dyh1", P.streams["dyh1"].n),
                           ("dyl0", P.streams["dyl0"].n), ("dyl1", P.streams["dyl1"].n),
                           ("dcc", P.streams["dcc"].n)])

        with nc.Block() as block:
            @block.sync
            def _(eng):
                P.run("sync", eng, sems)

            @block.tensor
            def _(eng):
                P.run("tensor", eng, sems)

            @block.vector
            def _(eng):
                P.run("vector", eng, sems)

            @block.scalar
            def _(eng):
                P.run("scalar", eng, sems)

            @block.gpsimd
            def _(eng):
                P.run("gpsimd", eng, sems)

    return nc


def make_core_inputs(x_flat, n_cores, T_core, weights):
    (w_in, conv_w, conv_b, w_B, w_C, w_out, wq, wsc, bias) = weights
    halves_per_seq = 2
    wq_t = np.ascontiguousarray(wq.T)
    conv_w_p = np.ascontiguousarray(
        conv_w[:, 0, :].reshape(CC, 128, 4).transpose(1, 0, 2).reshape(128, CC * 4))
    conv_b_p = np.ascontiguousarray(conv_b.reshape(CC, 128).T)
    wsb = np.ascontiguousarray(np.broadcast_to(wsc[None, :], (128, D)))
    bias_b = np.ascontiguousarray(np.broadcast_to(bias[None, :], (128, D)))
    identf = np.eye(128, dtype=np.float32)
    identb = np.eye(128, dtype=ml_dtypes.bfloat16)
    tt = np.arange(1, T_core + 1, dtype=np.float64)
    decay = np.broadcast_to((0.9 ** tt).astype(np.float32)[None, :],
                            (DS, T_core)).copy()
    in_maps = []
    for i in range(n_cores):
        r0 = i * T_core
        xc = np.ascontiguousarray(x_flat[r0:r0 + T_core])
        is_mid = (i % halves_per_seq) != 0
        if is_mid:
            xhalo = np.ascontiguousarray(x_flat[r0 - 3:r0])
            hm = np.ones((128, 3), dtype=np.float32)
            sel = np.zeros((128, DS), dtype=ml_dtypes.bfloat16)
            prev = i - 1
            for q in range(DS):
                sel[prev * DS + q, q] = 1.0
        else:
            xhalo = np.zeros((3, D), dtype=np.float32)
            hm = np.zeros((128, 3), dtype=np.float32)
            sel = np.zeros((128, DS), dtype=ml_dtypes.bfloat16)
        in_maps.append({
            "x": xc, "xhalo": xhalo, "w_in": w_in,
            "conv_w_p": conv_w_p, "conv_b_p": conv_b_p,
            "w_B": w_B, "w_C": w_C, "w_out": w_out, "wq_t": wq_t,
            "w_scale_b": wsb, "bias_b": bias_b, "halo_mask": hm,
            "sel_b": sel, "decay": decay, "ident_f": identf,
            "ident_b": identb,
        })
    return in_maps


def kernel(L0_attn_x, L0_attn_w_in, L0_attn_conv_w, L0_attn_conv_b,
           L0_attn_w_B, L0_attn_w_C, L0_attn_w_out, L0_ffn_weight_q,
           L0_ffn_weight_scale, L0_ffn_bias):
    n_cores = 8
    T_core = 4096
    T_tile = 512
    B, S, _ = L0_attn_x.shape
    x_flat = np.ascontiguousarray(
        np.asarray(L0_attn_x, dtype=np.float32).reshape(B * S, D))
    weights = (
        np.ascontiguousarray(np.asarray(L0_attn_w_in, dtype=np.float32)),
        np.ascontiguousarray(np.asarray(L0_attn_conv_w, dtype=np.float32)),
        np.ascontiguousarray(np.asarray(L0_attn_conv_b, dtype=np.float32)),
        np.ascontiguousarray(np.asarray(L0_attn_w_B, dtype=np.float32)),
        np.ascontiguousarray(np.asarray(L0_attn_w_C, dtype=np.float32)),
        np.ascontiguousarray(np.asarray(L0_attn_w_out, dtype=np.float32)),
        np.ascontiguousarray(np.asarray(L0_ffn_weight_q, dtype=np.int8)),
        np.ascontiguousarray(np.asarray(L0_ffn_weight_scale, dtype=np.float32)),
        np.ascontiguousarray(np.asarray(L0_ffn_bias, dtype=np.float32)),
    )
    nc = build_program(n_cores, T_core, T_tile)
    in_maps = make_core_inputs(x_flat, n_cores, T_core, weights)
    res = run_bass_kernel_spmd(nc, in_maps, core_ids=list(range(n_cores)))
    out = np.concatenate([np.asarray(r["out"]) for r in res.results], axis=0)
    return out.reshape(B, S, D).astype(np.float32)


# version 48
# speedup vs baseline: 1.0198x; 1.0048x over previous
import numpy as np
import ml_dtypes
import concourse.bass as bass
import concourse.mybir as mybir
from concourse.bass_utils import run_bass_kernel_spmd
from contextlib import ExitStack

F32 = mybir.dt.float32
BF16 = mybir.dt.bfloat16
I8 = mybir.dt.int8
I32 = mybir.dt.int32
AF = mybir.ActivationFunctionType
ALU = mybir.AluOpType
AXX = mybir.AxisListType.X

D = 1024
DI = 2048
DS = 16
KC = D // 128
CC = DI // 128
EPS_LN = 1e-5
EPS_Q = 1e-6


class Stream:
    def __init__(self, name, step=1):
        self.name = name
        self.step = step
        self.n = 0

    def inc(self, k=None):
        k = self.step if k is None else k
        self.n += k
        return (self.name, k)

    def now(self):
        return (self.name, self.n)


class Plan:
    def __init__(self):
        self.items = []
        self.streams = {}

    def S(self, name, step=1):
        if name not in self.streams:
            self.streams[name] = Stream(name, step)
        return self.streams[name]

    def emit(self, eng, f, waits=(), incs=()):
        self.items.append((eng, tuple(waits), tuple(incs), f))

    def drain(self, eng):
        self.items.append((eng, (), (), "drain"))

    def run(self, engname, engobj, sems):
        for (eng, waits, incs, f) in self.items:
            if eng != engname:
                continue
            if f == "drain":
                engobj.drain()
                continue
            for (s, vv) in waits:
                if vv > 0:
                    engobj.wait_ge(sems[s], vv)
            if f is None:
                continue
            inst = f(engobj)
            for (s, k) in incs:
                inst.then_inc(sems[s], k)


def build_program(n_cores, T_core, T_tile, upto=5):
    assert T_core % T_tile == 0 and T_tile % 128 == 0
    n_tiles = T_core // T_tile
    n_sub = T_tile // 128
    nc = bass.Bass("TRN2", target_bir_lowering=False)

    dx = nc.dram_tensor("x", [T_core, D], F32, kind="ExternalInput").ap()
    dxh = nc.dram_tensor("xhalo", [3, D], F32, kind="ExternalInput").ap()
    dwin = nc.dram_tensor("w_in", [D, 2 * DI], F32, kind="ExternalInput").ap()
    dconvw = nc.dram_tensor("conv_w_p", [128, CC * 4], F32, kind="ExternalInput").ap()
    dconvb = nc.dram_tensor("conv_b_p", [128, CC], F32, kind="ExternalInput").ap()
    dwB = nc.dram_tensor("w_B", [DI, DS], F32, kind="ExternalInput").ap()
    dwC = nc.dram_tensor("w_C", [DS, DI], F32, kind="ExternalInput").ap()
    dwout = nc.dram_tensor("w_out", [DI, D], F32, kind="ExternalInput").ap()
    dwq = nc.dram_tensor("wq_t", [D, D], I8, kind="ExternalInput").ap()
    dwsb = nc.dram_tensor("w_scale_b", [128, D], F32, kind="ExternalInput").ap()
    dbb = nc.dram_tensor("bias_b", [128, D], F32, kind="ExternalInput").ap()
    dhm = nc.dram_tensor("halo_mask", [128, 3], F32, kind="ExternalInput").ap()
    dsel = nc.dram_tensor("sel_b", [128, DS], BF16, kind="ExternalInput").ap()
    ddecay = nc.dram_tensor("decay", [DS, T_core], F32, kind="ExternalInput").ap()
    didf = nc.dram_tensor("ident_f", [128, 128], F32, kind="ExternalInput").ap()
    didb = nc.dram_tensor("ident_b", [128, 128], BF16, kind="ExternalInput").ap()
    dout = nc.dram_tensor("out", [T_core, D], F32, kind="ExternalOutput").ap()
    dx2 = nc.dram_tensor("x2_sp", [T_core, D], F32).ap()
    dysh = nc.dram_tensor("ysp_h", [T_core // 128, 128, CC, 128],
                          BF16).ap()
    dysl = nc.dram_tensor("ysp_l", [T_core // 128, 128, CC, 128],
                          BF16).ap()
    dcci = nc.dram_tensor("cc_in", [DS, 1], F32)
    dcco = nc.dram_tensor("cc_out", [DS * n_cores, 1], F32)
    dhlc = nc.dram_tensor("hloc_sp", [DS, T_core], F32).ap()

    P = Plan()
    es = ExitStack()

    def sbt(name, shape, dt):
        return es.enter_context(nc.sbuf_tensor(name, shape, dt))

    def pst(name, shape, dt):
        return es.enter_context(nc.psum_tensor(name, shape, dt))

    with es:
        wq_b = sbt("wq_b", [128, KC * D], BF16)
        wsb = sbt("wsb", [128, D], F32)
        biasb = sbt("biasb", [128, D], F32)
        wxh = sbt("wxh", [128, KC * DI], BF16)
        wxl = sbt("wxl", [128, KC * DI], BF16)
        wbh = sbt("wbh", [128, CC * DS], BF16)
        wbl = sbt("wbl", [128, CC * DS], BF16)
        wch = sbt("wch", [DS, DI], BF16)
        wcl = sbt("wcl", [DS, DI], BF16)
        convw = sbt("convw", [128, CC * 4], F32)
        convb = sbt("convb", [128, CC], F32)
        selb = sbt("selb", [128, DS], BF16)
        identb = sbt("identb", [128, 128], BF16)
        halom = sbt("halom", [128, 3], F32)
        halobuf = sbt("halobuf", [128, CC * 3], F32)
        epsln = sbt("epsln", [128, 1], F32)
        u_t = sbt("u_t", [DS, T_tile], F32)
        d09 = sbt("d09", [DS, T_tile], F32)
        hl_tmp = [sbt(f"hl_tmp{i}", [DS, T_tile], F32) for i in range(2)]
        hh_t = sbt("hh_t", [DS, T_tile], BF16)
        hl_t = sbt("hl_t", [DS, T_tile], BF16)
        decay_t = d09
        htmp = u_t
        hin = sbt("hin", [DS, 1], F32)
        gbuf = sbt("gbuf", [128, 1], F32)
        ghl = sbt("ghl", [128, 2], BF16)

        bufx = [sbt(f"bufx{i}", [128, D], F32) for i in range(2)]
        bufx2 = [sbt(f"bufx2{i}", [128, D], F32) for i in range(2)]
        sqS = sbt("sqS", [128, D], F32)
        sq_guard = [("ss", 0)]
        ZS = []
        for zi in range(2):
            ZS.append(dict(
                scrA=sbt(f"scrA{zi}", [128, D], F32),
                scrB=sbt(f"scrB{zi}", [128, D], F32),
                scrGE=sbt(f"scrGE{zi}", [128, D], BF16),
                scrLE=sbt(f"scrLE{zi}", [128, D], BF16),
                qbf=sbt(f"qbf{zi}", [128, D], BF16),
                xqT=sbt(f"xqT{zi}", [128, KC * 128], BF16),
            ))
        xnTh = sbt("xnTh", [128, KC * T_tile], BF16)
        xnTl = sbt("xnTl", [128, KC * T_tile], BF16)
        xnTh2 = sbt("xnTh2", [128, KC * T_tile], BF16)
        xnTl2 = sbt("xnTl2", [128, KC * T_tile], BF16)
        ybh1 = xnTh2
        ybl1 = xnTl2
        wstage = ZS[0]["scrB"]
        wstage2 = ZS[1]["scrB"]
        wqi8 = ZS[0]["scrGE"][:, 0:D // 2].bitcast(I8)
        wcstage = wstage[0:DS, 0:D]
        wbstage = wstage[:, 0:CC * DS]
        xp = [sbt(f"xp{i}", [128, T_tile + 3], F32) for i in range(2)]
        xcf = sbt("xcf", [128, T_tile], F32)
        xcg = sbt("xcg", [128, T_tile], F32)
        xch = [sbt(f"xch{i}", [128, T_tile], BF16) for i in range(2)]
        xcl = [sbt(f"xcl{i}", [128, T_tile], BF16) for i in range(2)]
        gk = [xp[0][:, 0:T_tile], xp[1][:, 0:T_tile]]
        yk = xcf
        xcg_b = xcg[:].bitcast(BF16)
        ykh = [xcg_b[:, 0:T_tile], xch[0][:]]
        ykl = [xcg_b[:, T_tile:2 * T_tile], xcl[0][:]]
        for zi in range(2):
            ZS[zi]["stats"] = sbt(f"stats{zi}", [128, 16], F32)
            ZS[zi]["vtail"] = ("vv", 0)
            ZS[zi]["stail"] = ("ss", 0)
            ZS[zi]["sq_done"] = ("ss", 0)
        pb = [pst(f"pb{i}", [128, 512], F32) for i in range(8)]
        pu = pb[7][0:DS, :]

        sem_names = ["dhs", "dhb", "dw2",
                     "dw", "dcc", "ddc", "dhl",
                     "dxa0", "dxa1", "dsp0", "dsp1",
                     "dyh0", "dyh1", "dyl0", "dyl1",
                     "dyb0", "dyb1", "dot0", "dot1",
                     "vv", "ss", "tt", "gg"]
        sems = {n: es.enter_context(nc.semaphore(n)) for n in sem_names}

        sw = P.S("dw", 16)
        sw2 = P.S("dw2", 16)
        shs = P.S("dhs", 16)
        shb = P.S("dhb", 16)
        scc = P.S("dcc", 16)
        sdec = P.S("ddc", 16)
        shl = P.S("dhl", 16)
        sxp = [P.S("dxa0", 16), P.S("dxa1", 16)]
        sspp = [P.S("dsp0", 16), P.S("dsp1", 16)]
        syph = [P.S("dyh0", 16), P.S("dyh1", 16)]
        sypl = [P.S("dyl0", 16), P.S("dyl1", 16)]
        sybp = [P.S("dyb0", 16), P.S("dyb1", 16)]
        sybl2 = [P.S("dyl0", 16), P.S("dyl1", 16)]
        sotp = [P.S("dot0", 16), P.S("dot1", 16)]
        vev = P.S("vv")
        sev = P.S("ss")
        tmm = P.S("tt")
        gev = P.S("gg")

        def v(f, waits=(), incs=()):
            P.emit("vector", f, waits, incs)

        def a(f, waits=(), incs=()):
            P.emit("scalar", f, waits, incs)

        def t(f, waits=(), incs=()):
            P.emit("tensor", f, waits, incs)

        def g(f, waits=(), incs=()):
            P.emit("gpsimd", f, waits, incs)

        def sy(f, waits=(), incs=()):
            P.emit("sync", f, waits, incs)

        def dma(emitfn, f, stream, waits=()):
            emitfn(f, waits=[stream.now()] + list(waits), incs=[stream.inc()])

        def dma_nc(emitfn, f, stream, waits=()):
            emitfn(f, waits=list(waits), incs=[stream.inc()])

        v(lambda e: e.memset(epsln[:], EPS_LN), incs=[vev.inc()])
        v(lambda e: e.memset(d09[:], 0.9), incs=[vev.inc()])
        v(lambda e: e.memset(halobuf[:], 0.0), incs=[vev.inc()])
        v(lambda e: e.memset(gbuf[:], 0.0), incs=[vev.inc()])
        P.drain("vector")

        for (dst, src) in ((convw, dconvw), (convb, dconvb), (wsb, dwsb),
                           (biasb, dbb), (selb, dsel),
                           (identb, didb), (halom, dhm)):
            dma(sy, lambda e, dst=dst, src=src: e.dma_start(out=dst[:],
                                                            in_=src[:]), sw)
        dma(sy, lambda e: e.dma_start(
            out=wbstage.rearrange("p (c s) -> p c s", c=CC),
            in_=dwB[:].rearrange("(c p) s -> p c s", p=128)), sw)
        a(lambda e: e.activation(out=wbh[:], in_=wbstage, func=AF.Copy),
          waits=[sw.now()], incs=[sev.inc()])
        v(lambda e: e.tensor_sub(wbl[:], wbstage, wbh[:]),
          waits=[sev.now()], incs=[vev.inc()])
        for h in range(2):
            dma(sy, lambda e, h=h: e.dma_start(
                out=wcstage, in_=dwC[:, h * D:(h + 1) * D]), sw,
                [sev.now(), vev.now()])
            a(lambda e, h=h: e.activation(out=wch[:, h * D:(h + 1) * D],
                                          in_=wcstage, func=AF.Copy),
              waits=[sw.now()], incs=[sev.inc()])
            v(lambda e, h=h: e.tensor_sub(wcl[:, h * D:(h + 1) * D], wcstage,
                                          wch[:, h * D:(h + 1) * D]),
              waits=[sev.now()], incs=[vev.inc()])
        for k in range(KC):
            wts = [sev.now()] if k > 0 else []
            dma(sy, lambda e, k=k: e.dma_start(
                out=wqi8[:], in_=dwq[k * 128:(k + 1) * 128, :]), sw, wts)
            a(lambda e, k=k: e.activation(
                out=wq_b[:, k * D:(k + 1) * D], in_=wqi8[:], func=AF.Copy),
              waits=[sw.now()], incs=[sev.inc()])

        def load_split_big(chunks, wait_first=(), stages=None,
                           copy_first=()):
            if stages is None:
                stages = [wstage[:, 0:D], wstage2[:, 0:D],
                          ZS[0]["scrA"][:, 0:D], ZS[1]["scrA"][:, 0:D]]
            marks = [list(wait_first) for _ in stages]
            i = 0
            pieces = []
            for (off0, width0, src0) in chunks:
                nh = (width0 + D - 1) // D
                for h in range(nh):
                    pieces.append((off0 + h * D,
                                   min(D, width0 - h * D),
                                   src0[:, h * D:h * D + min(D, width0 - h * D)]))
            for (off, width, src) in pieces:
                stg = stages[i % len(stages)]
                swk = sw if i % 2 == 0 else sw2
                dma(sy, lambda e, off=off, width=width, src=src,
                    stg=stg: e.dma_start(
                    out=stg[:, :width] if False else stg[0:128, 0:width],
                    in_=src), swk, marks[i % len(stages)])
                a(lambda e, off=off, width=width, stg=stg: e.activation(
                    out=wxh[:, off:off + width], in_=stg[0:128, 0:width],
                    func=AF.Copy),
                  waits=[swk.now()] + (list(copy_first)
                                       if i < len(stages) else []),
                  incs=[sev.inc()])
                v(lambda e, off=off, width=width, stg=stg: e.tensor_sub(
                    wxl[:, off:off + width], stg[0:128, 0:width],
                    wxh[:, off:off + width]),
                  waits=[sev.now()], incs=[vev.inc()])
                marks[i % len(stages)] = [sev.now(), vev.now()]
                i += 1

        prep_done = [sev.now(), vev.now(), sw.now()]
        xnt2f = xnTh2[:].bitcast(F32)
        xnt2g = xnTl2[:].bitcast(F32)
        load_split_big([(k * DI, DI, dwin[k * 128:(k + 1) * 128, 0:DI])
                        for k in range(KC)],
                       wait_first=[sev.now(), vev.now()],
                       stages=[xnt2f[:, 0:D], xnt2f[:, D:2 * D],
                               xnt2g[:, 0:D], xnt2g[:, D:2 * D]])

        def vc(f, waits=()):
            v(f, waits=waits, incs=[vev.inc()])
            return vev.now()

        def ac(f, waits=()):
            a(f, waits=waits, incs=[sev.inc()])
            return sev.now()

        def run_gen(gn):
            for _ in gn:
                pass

        def duet(*gens_in):
            gens = list(gens_in)
            while gens:
                nxt = []
                for gn in gens:
                    try:
                        next(gn)
                        nxt.append(gn)
                    except StopIteration:
                        pass
                gens = nxt

        def gen_ln(xv, ntok, need_quant, waits0, Z, res):
            scrA = Z["scrA"]; stats = Z["stats"]
            w0 = list(waits0) + [Z["vtail"], Z["stail"]]
            sq_done = ac(lambda e: e.activation(out=sqS[0:ntok, :], in_=xv,
                                                func=AF.Square,
                                                accum_out=stats[0:ntok, 1:2]),
                         waits=w0 + [sq_guard[0]])
            sq_guard[0] = sq_done
            e1 = vc(lambda e: e.tensor_reduce(out=stats[0:ntok, 0:1], in_=xv,
                                              axis=AXX, op=ALU.add),
                    waits=w0)
            yield
            e2 = vc(lambda e: e.tensor_scalar(
                out=stats[0:ntok, 2:3], in0=stats[0:ntok, 0:1],
                scalar1=-1.0 / D, scalar2=None, op0=ALU.mult), waits=[e1])
            yield
            e3 = vc(lambda e: e.tensor_mul(stats[0:ntok, 3:4],
                                           stats[0:ntok, 2:3],
                                           stats[0:ntok, 2:3]), waits=[e2])
            yield
            e4 = vc(lambda e: e.tensor_scalar(
                out=stats[0:ntok, 4:5], in0=stats[0:ntok, 1:2],
                scalar1=1.0 / D, scalar2=stats[0:ntok, 3:4],
                op0=ALU.mult, op1=ALU.subtract), waits=[e3, sq_done])
            sq2 = ac(lambda e: e.activation(out=stats[0:ntok, 5:6],
                                            in_=stats[0:ntok, 4:5],
                                            func=AF.Sqrt,
                                            bias=epsln[0:ntok, :]),
                     waits=[e4])
            yield
            if need_quant:
                e7 = vc(lambda e: e.tensor_scalar(
                    out=scrA[0:ntok, :], in0=xv, scalar1=stats[0:ntok, 2:3],
                    scalar2=None, op0=ALU.add), waits=[e2])
                yield
            e6 = vc(lambda e: e.reciprocal(out=stats[0:ntok, 6:7],
                                           in_=stats[0:ntok, 5:6]),
                    waits=[sq2])
            yield
            if not need_quant:
                Z["vtail"] = e6
                Z["stail"] = sq2
                res["ln"] = e6
                return
            e8 = vc(lambda e: e.tensor_reduce(out=stats[0:ntok, 7:8],
                                              in_=scrA[0:ntok, :], axis=AXX,
                                              op=ALU.max,
                                              apply_absolute_value=True),
                    waits=[e7])
            yield
            e9 = vc(lambda e: e.tensor_mul(stats[0:ntok, 8:9],
                                           stats[0:ntok, 7:8],
                                           stats[0:ntok, 6:7]),
                    waits=[e8, e6])
            yield
            e10 = vc(lambda e: e.tensor_scalar(
                out=stats[0:ntok, 9:10], in0=stats[0:ntok, 8:9],
                scalar1=EPS_Q, scalar2=1.0 / 127.0,
                op0=ALU.max, op1=ALU.mult), waits=[e9])
            yield
            e13 = vc(lambda e: e.tensor_scalar(
                out=stats[0:ntok, 13:14], in0=stats[0:ntok, 9:10],
                scalar1=0.5, scalar2=None, op0=ALU.mult), waits=[e10])
            yield
            e11 = vc(lambda e: e.reciprocal(out=stats[0:ntok, 11:12],
                                            in_=stats[0:ntok, 9:10]),
                     waits=[e10])
            yield
            e12 = vc(lambda e: e.tensor_mul(stats[0:ntok, 12:13],
                                            stats[0:ntok, 6:7],
                                            stats[0:ntok, 11:12]),
                     waits=[e11])
            Z["vtail"] = e12
            Z["stail"] = sq2
            res["ln"] = e12

        def emit_ln(xv, ntok, need_quant, waits0, Z):
            res = {}
            run_gen(gen_ln(xv, ntok, need_quant, waits0, Z, res))
            return res["ln"]

        def gen_quant(ntok, Z, R, lnq, res):
            scrA = Z["scrA"]; scrB = Z["scrB"]
            scrGE = Z["scrGE"]; scrLE = Z["scrLE"]
            qbf = Z["qbf"]; stats = Z["stats"]
            q1 = ac(lambda e: e.activation(
                out=scrB[0:ntok, :], in_=scrA[0:ntok, :], func=AF.Copy,
                scale=stats[0:ntok, 12:13]), waits=[lnq])
            q2 = vc(lambda e: e.tensor_copy(out=scrA[0:ntok, :].bitcast(I32),
                                            in_=scrB[0:ntok, :]),
                    waits=[q1])
            yield
            q3 = vc(lambda e: e.tensor_copy(out=qbf[0:ntok, :],
                                            in_=scrA[0:ntok, :].bitcast(I32)),
                    waits=[q2, R["qbf"]])
            yield
            q4 = vc(lambda e: e.tensor_sub(scrB[0:ntok, :], scrB[0:ntok, :],
                                           qbf[0:ntok, :]), waits=[q3])
            yield
            qGE = vc(lambda e: e.tensor_scalar(
                out=scrGE[0:ntok, :], in0=scrB[0:ntok, :], scalar1=0.5,
                scalar2=None, op0=ALU.is_ge), waits=[q4, R["xn2"]])
            yield
            qLE = vc(lambda e: e.tensor_scalar(
                out=scrLE[0:ntok, :], in0=scrB[0:ntok, :], scalar1=-0.5,
                scalar2=None, op0=ALU.is_le), waits=[q4])
            yield
            q7 = vc(lambda e: e.tensor_sub(scrB[0:ntok, :], scrGE[0:ntok, :],
                                           scrLE[0:ntok, :]),
                    waits=[qGE, qLE])
            yield
            q8 = vc(lambda e: e.tensor_add(qbf[0:ntok, :], qbf[0:ntok, :],
                                           scrB[0:ntok, :]),
                    waits=[q7])
            Z["vtail"] = q8
            res["qd"] = q8

        bank_free = {i: ("vv", 0) for i in range(8)}
        RS = [{"xn2": ("tt", 0), "qbf": ("tt", 0), "xqT": ("tt", 0)}
              for _ in range(2)]

        def gen_ffn_part1(xbuf, ntok, x_wait, Z, R, res):
            r = {}
            yield from gen_ln(xbuf[0:ntok, :], ntok, True, x_wait, Z, r)
            yield from gen_quant(ntok, Z, R, r["ln"], res)

        def emit_ffn_part1(xbuf, ntok, x_wait, Z, R):
            res = {}
            run_gen(gen_ffn_part1(xbuf, ntok, x_wait, Z, R, res))
            return res["qd"]

        def emit_ffn_part2a(xbuf, ntok, qdone, Z, R, b_q=0, b_h=(1, 2)):
            qbf = Z["qbf"]; xqT = Z["xqT"]; stats = Z["stats"]
            scrB = Z["scrB"]
            pbq = pb[b_q][:].bitcast(BF16)
            stq = ntok if ntok % 2 == 0 else ntok + 1
            for c in range(KC):
                wts = [qdone, bank_free[b_q]] if c == 0 else []
                t(lambda e, c=c, ntok=ntok, stq=stq, pbq=pbq,
                  qbf=qbf: e.matmul(
                    pbq[:, c * stq:c * stq + ntok],
                    qbf[0:ntok, c * 128:(c + 1) * 128],
                    identb[0:ntok, 0:ntok],
                    is_transpose=True, start=(c == 0), stop=(c == KC - 1)),
                  waits=wts, incs=([tmm.inc()] if c == KC - 1 else []))
            trq_done = tmm.now()
            R["qbf"] = trq_done
            if stq == ntok:
                a(lambda e, ntok=ntok, pbq=pbq, xqT=xqT: e.activation(
                    out=xqT[:, 0:KC * ntok], in_=pbq[:, 0:KC * ntok],
                    func=AF.Copy),
                  waits=[trq_done, R["xqT"]], incs=[sev.inc()])
            else:
                a(lambda e, ntok=ntok, stq=stq, pbq=pbq,
                  xqT=xqT: e.activation(
                    out=xqT[:, 0:KC * ntok].rearrange(
                        "p (c q) -> p c q", q=ntok),
                    in_=pbq[:, 0:KC * stq].rearrange(
                        "p (c q) -> p c q", q=stq)[:, :, 0:ntok],
                    func=AF.Copy),
                  waits=[trq_done, R["xqT"]], incs=[sev.inc()])
            bank_free[b_q] = sev.now()
            xqt_done = sev.now()
            for k in range(KC):
                for half in range(2):
                    wts = []
                    if k == 0:
                        wts = [bank_free[b_h[half]]]
                        if half == 0:
                            wts.append(xqt_done)
                    t(lambda e, half=half, k=k, ntok=ntok, xqT=xqT: e.matmul(
                        pb[b_h[half]][0:ntok, :],
                        xqT[:, k * ntok:(k + 1) * ntok],
                        wq_b[:, k * D + half * 512: k * D + half * 512 + 512],
                        start=(k == 0), stop=(k == KC - 1)),
                      waits=wts, incs=([tmm.inc()]
                                       if (k == KC - 1 and half == 1) else []))
            ffn_done = tmm.now()
            R["xqT"] = ffn_done
            for half in range(2):
                a(lambda e, half=half, ntok=ntok, scrB=scrB,
                  stats=stats: e.activation(
                    out=scrB[0:ntok, half * 512:half * 512 + 512],
                    in_=pb[b_h[half]][0:ntok, :], func=AF.Copy,
                    scale=stats[0:ntok, 13:14]),
                  waits=([ffn_done] if half == 0 else []), incs=[sev.inc()])
                bank_free[b_h[half]] = sev.now()
            Z["stail"] = sev.now()
            return sev.now()

        def gen_ffn_part2b(xbuf, x2buf, ntok, evac_done, Z, x2_wait, res):
            scrB = Z["scrB"]
            p1 = vc(lambda e, ntok=ntok, scrB=scrB: e.tensor_mul(
                scrB[0:ntok, :], scrB[0:ntok, :], wsb[0:ntok, :]),
                waits=[evac_done, Z["vtail"]])
            yield
            p2 = vc(lambda e, ntok=ntok, scrB=scrB: e.tensor_add(
                scrB[0:ntok, :], scrB[0:ntok, :], xbuf[0:ntok, :]),
                waits=[p1])
            yield
            p3 = vc(lambda e, ntok=ntok, scrB=scrB: e.scalar_tensor_tensor(
                out=x2buf[0:ntok, :], in0=biasb[0:ntok, :], scalar=0.5,
                in1=scrB[0:ntok, :], op0=ALU.mult, op1=ALU.add),
                waits=[p2] + list(x2_wait))
            Z["vtail"] = p3
            res["x2"] = p3

        def emit_ffn_part2b(xbuf, x2buf, ntok, evac_done, Z, x2_wait=()):
            res = {}
            run_gen(gen_ffn_part2b(xbuf, x2buf, ntok, evac_done, Z,
                                   x2_wait, res))
            return res["x2"]

        def emit_ffn_part2(xbuf, x2buf, ntok, qdone, Z, R, b_q=0,
                           b_h=(1, 2), x2_wait=()):
            ed = emit_ffn_part2a(xbuf, ntok, qdone, Z, R, b_q, b_h)
            return emit_ffn_part2b(xbuf, x2buf, ntok, ed, Z, x2_wait)

        def emit_ffn_block(xbuf, x2buf, ntok, x_wait, Z, R, b_q=0,
                           b_h=(1, 2), x2_wait=()):
            qdone = emit_ffn_part1(xbuf, ntok, x_wait, Z, R)
            return emit_ffn_part2(xbuf, x2buf, ntok, qdone, Z, R, b_q, b_h,
                                  x2_wait)

        def gen_ln2_part1(x2buf, ntok, x2_wait, Z, R, res):
            scrA = Z["scrA"]; scrGE = Z["scrGE"]; scrLE = Z["scrLE"]
            stats = Z["stats"]
            r = {}
            yield from gen_ln(x2buf[0:ntok, :], ntok, False, x2_wait, Z, r)
            xn2_done = vc(
                lambda e, ntok=ntok, scrA=scrA, stats=stats: e.tensor_scalar(
                    out=scrA[0:ntok, :], in0=x2buf[0:ntok, :],
                    scalar1=stats[0:ntok, 2:3], scalar2=stats[0:ntok, 6:7],
                    op0=ALU.add, op1=ALU.mult),
                waits=[r["ln"]])
            yield
            res["xn2"] = xn2_done
            sp1 = ac(lambda e, ntok=ntok, scrGE=scrGE, scrA=scrA: e.activation(
                out=scrGE[0:ntok, :], in_=scrA[0:ntok, :], func=AF.Copy),
                waits=[xn2_done, R["xn2"]])
            sp2 = vc(lambda e, ntok=ntok, scrLE=scrLE, scrA=scrA,
                     scrGE=scrGE: e.tensor_sub(scrLE[0:ntok, :],
                                               scrA[0:ntok, :],
                                               scrGE[0:ntok, :]),
                     waits=[sp1])
            Z["vtail"] = sp2
            res["sd"] = sp2

        def emit_ln2_part1(x2buf, ntok, x2_wait, Z, R):
            res = {}
            run_gen(gen_ln2_part1(x2buf, ntok, x2_wait, Z, R, res))
            return res["xn2"], res["sd"]

        def emit_ln2_part2(ntok, col_off, split_done, xnh, xnl, Z, R,
                           banks=(3, 4), first_wait=()):
            scrGE = Z["scrGE"]; scrLE = Z["scrLE"]
            """bf16 transposes of scrGE/scrLE into xnh/xnl columns."""
            stq = ntok if ntok % 2 == 0 else ntok + 1
            for grp, (srcb, dstb) in enumerate(((scrGE, xnh), (scrLE, xnl))):
                bank = banks[grp]
                pbv = pb[bank][:].bitcast(BF16)
                for c in range(KC):
                    wts = ([split_done, bank_free[bank]] + list(first_wait)) \
                        if c == 0 else []
                    t(lambda e, c=c, ntok=ntok, stq=stq, pbv=pbv,
                      srcb=srcb: e.matmul(
                        pbv[:, c * stq:c * stq + ntok],
                        srcb[0:ntok, c * 128:(c + 1) * 128],
                        identb[0:ntok, 0:ntok],
                        is_transpose=True, start=(c == 0), stop=(c == KC - 1)),
                      waits=wts, incs=([tmm.inc()] if c == KC - 1 else []))
                tr_done = tmm.now()
                dst_ap = dstb[:].rearrange(
                    "p (c q) -> p c q", q=T_tile)[:, :,
                                                  col_off:col_off + ntok]
                src_ap = pbv[:, 0:KC * stq].rearrange(
                    "p (c q) -> p c q", q=stq)[:, :, 0:ntok]
                a(lambda e, dst_ap=dst_ap, src_ap=src_ap: e.activation(
                    out=dst_ap, in_=src_ap, func=AF.Copy),
                  waits=[tr_done], incs=[sev.inc()])
                bank_free[bank] = sev.now()
            R["xn2"] = tmm.now()

        def emit_ln2_transpose(x2buf, ntok, col_off, x2_wait, Z, R,
                               banks=(3, 4), xnh=None, xnl=None,
                               first_wait=()):
            if xnh is None:
                xnh, xnl = xnTh, xnTl
            xn2_done, split_done = emit_ln2_part1(x2buf, ntok, x2_wait, Z, R)
            emit_ln2_part2(ntok, col_off, split_done, xnh, xnl, Z, R, banks,
                           first_wait)
            return xn2_done


        def emit_win_chunk_mm(cchunk, ntok, bank, ready_wait,
                              xnh=None, xnl=None):
            if xnh is None:
                xnh, xnl = xnTh, xnTl
            n0 = cchunk * 128
            first = True
            for k in range(KC):
                for (wbuf, xbuf_) in ((wxh, xnh), (wxh, xnl), (wxl, xnh)):
                    is_first = first
                    first = False
                    is_last = (k == KC - 1 and wbuf is wxl)
                    wts = (list(ready_wait) + [bank_free[bank]]) \
                        if is_first else []
                    t(lambda e, k=k, wbuf=wbuf, xbuf_=xbuf_, n0=n0,
                      is_first=is_first, is_last=is_last, ntok=ntok: e.matmul(
                          pb[bank][:, 0:ntok],
                          wbuf[:, k * DI + n0:k * DI + n0 + 128],
                          xbuf_[:, k * T_tile:k * T_tile + ntok],
                          start=is_first, stop=is_last),
                      waits=wts, incs=([tmm.inc()] if is_last else []))
            return tmm.now()

        if upto >= 1:
            dma(sy, lambda e: e.dma_start(out=bufx[0][0:3, :], in_=dxh[:]),
                sxp[0])
            halo_wait = [sxp[0].now()] + prep_done
            halo_x2 = emit_ffn_block(bufx[0], bufx2[0], 3, halo_wait,
                                     ZS[0], RS[0])
            halo_xn2 = emit_ln2_transpose(bufx2[0], 3, 0, [halo_x2],
                                          ZS[0], RS[0])
            halo_xn = [vev.now(), sev.now()]
            hbuf_mark = [None] * CC
            for c in range(CC):
                bank = 5 + (c % 2)
                mm_done = emit_win_chunk_mm(c, 3, bank, halo_xn if c < 2 else [])
                hbuf_mark[c] = vc(lambda e, c=c, bank=bank: e.tensor_mul(
                    halobuf[:, c * 3:(c + 1) * 3], pb[bank][:, 0:3], halom[:]),
                    waits=[mm_done])
                bank_free[bank] = hbuf_mark[c]
        else:
            halo_x2 = ("vv", 0)
            halo_xn2 = ("vv", 0)
            hbuf_mark = [("vv", 0)] * CC
        x_free = [halo_x2, ("vv", 0)]
        x2_free = [[halo_xn2, ("dsp0", 0)], [("vv", 0), ("dsp1", 0)]]
        xn_used_halo = tmm.now()
        xp_free = [[("vv", 0)], [("vv", 0)]]
        xc_free = [("tt", 0)] * 2
        scan_done = ("vv", 0)
        hl_free = [("dhs", 0), ("dhs", 0)]

        xnhb = [xnTh, xnTh2]
        xnlb = [xnTl, xnTl2]
        xn_used2 = [xn_used_halo, ("tt", 0)]
        xn_ready2 = [[("vv", 0)], [("vv", 0)]]
        cbuf_free = {"xcf": [("ss", 0)], "xcg": ("ss", 0), "g": ("vv", 0)}

        def gen_subA_p1(j, s, res):
            gsub = j * n_sub + s
            par = gsub % 2
            trow = j * T_tile + s * 128
            dma(sy, lambda e, trow=trow, par=par: e.dma_start(
                out=bufx[par][:], in_=dx[trow:trow + 128, :]),
                sxp[par], [x_free[par]])
            yield from gen_ffn_part1(bufx[par], 128, [sxp[par].now()],
                                     ZS[s % 2], RS[s % 2], res)

        def subA_p1(j, s):
            res = {}
            run_gen(gen_subA_p1(j, s, res))
            return res["qd"]

        def subA_p2a(j, s, qd):
            return emit_ffn_part2a(bufx[(j * n_sub + s) % 2], 128, qd,
                                   ZS[s % 2], RS[s % 2])

        def gen_subA_p2b(j, s, ed, res):
            gsub = j * n_sub + s
            par = gsub % 2
            trow = j * T_tile + s * 128
            r = {}
            yield from gen_ffn_part2b(bufx[par], bufx2[par], 128, ed,
                                      ZS[s % 2], x2_free[par], r)
            x2done = r["x2"]
            x_free[par] = x2done
            dma(g, lambda e, trow=trow, par=par: e.dma_start(
                out=dx2[trow:trow + 128, :], in_=bufx2[par][:]),
                sspp[par], [x2done])
            r2 = {}
            yield from gen_ln2_part1(bufx2[par], 128, [x2done],
                                     ZS[s % 2], RS[s % 2], r2)
            x2_free[par] = [r2["xn2"], sspp[par].now()]
            res["sd"] = r2["sd"]

        def subA_p2b(j, s, ed):
            res = {}
            run_gen(gen_subA_p2b(j, s, ed, res))
            return res["sd"]

        def subA_p3(j, s, sd):
            jp = j % 2
            fw = [xn_used2[jp]] if s == 0 else []
            emit_ln2_part2(128, s * 128, sd, xnhb[jp], xnlb[jp],
                           ZS[s % 2], RS[s % 2], first_wait=fw)
            if s == n_sub - 1:
                xn_ready2[jp] = [vev.now(), sev.now()]

        def emit_u_mm(centry):
            c, par, xch_done, xcl_done = centry
            for ui, (wb_, xc_) in enumerate(((wbh, xch), (wbh, xcl),
                                             (wbl, xch))):
                is_first_g = (c == 0 and ui == 0)
                is_last_g = (c == CC - 1 and ui == 2)
                wts = []
                if ui == 0:
                    wts = [xch_done, xcl_done]
                    if is_first_g:
                        wts.append(bank_free[7])
                t(lambda e, c=c, wb_=wb_, xc_=xc_, par=par,
                  is_first_g=is_first_g, is_last_g=is_last_g: e.matmul(
                      pu[:, 0:T_tile],
                      wb_[:, c * DS:(c + 1) * DS],
                      xc_[par][:, :],
                      start=is_first_g, stop=is_last_g),
                  waits=wts, incs=[tmm.inc()] if ui == 2 else [])
            xc_free[par] = tmm.now()

        def chunkA_stage1(j, c):
            jp = j % 2
            bank = 3 + (c % 4)
            par = c % 2
            mm_done = emit_win_chunk_mm(c, T_tile, bank,
                                        xn_ready2[jp] if c < 2 else [],
                                        xnhb[jp], xnlb[jp])
            halo_ld = ac(lambda e, c=c, par=par: e.activation(
                out=xp[par][:, 0:3], in_=halobuf[:, c * 3:(c + 1) * 3],
                func=AF.Copy),
                waits=list(xp_free[par]) + [hbuf_mark[c]])
            evx = ac(lambda e, par=par, bank=bank: e.activation(
                out=xp[par][:, 3:3 + T_tile], in_=pb[bank][:, 0:T_tile],
                func=AF.Copy),
                waits=[mm_done])
            bank_free[bank] = evx
            hbuf_mark[c] = ac(lambda e, c=c, par=par: e.activation(
                out=halobuf[:, c * 3:(c + 1) * 3],
                in_=xp[par][:, T_tile:T_tile + 3], func=AF.Copy),
                waits=[evx])
            return dict(evx=evx, halo_ld=halo_ld, par=par)

        def chunkA_stage2(j, c, st, pump=lambda: None):
            par = st["par"]
            par4 = c % 2
            xp_ready = [st["evx"], st["halo_ld"]]
            ct0 = vc(lambda e, c=c, par=par: e.tensor_scalar(
                out=xcf[:, :], in0=xp[par][:, 0:T_tile],
                scalar1=convw[:, 4 * c:4 * c + 1], scalar2=None,
                op0=ALU.mult),
                waits=list(xp_ready) + list(cbuf_free["xcf"]))
            pump()
            ct1 = vc(lambda e, c=c, par=par: e.scalar_tensor_tensor(
                out=xcg[:, :], in0=xp[par][:, 1:1 + T_tile],
                scalar=convw[:, 4 * c + 1:4 * c + 2], in1=xcf[:, :],
                op0=ALU.mult, op1=ALU.add),
                waits=[ct0, cbuf_free["xcg"]])
            pump()
            ct2 = vc(lambda e, c=c, par=par: e.scalar_tensor_tensor(
                out=xcf[:, :], in0=xp[par][:, 2:2 + T_tile],
                scalar=convw[:, 4 * c + 2:4 * c + 3], in1=xcg[:, :],
                op0=ALU.mult, op1=ALU.add),
                waits=[ct1])
            pump()
            conv_done = vc(lambda e, c=c, par=par: e.scalar_tensor_tensor(
                out=xcg[:, :], in0=xp[par][:, 3:3 + T_tile],
                scalar=convw[:, 4 * c + 3:4 * c + 4], in1=xcf[:, :],
                op0=ALU.mult, op1=ALU.add),
                waits=[ct2])
            xp_free[par] = [conv_done]
            sl = ac(lambda e, c=c: e.activation(
                out=xcf[:, :], in_=xcg[:, :], func=AF.Silu,
                bias=convb[:, c:c + 1]),
                waits=[conv_done])
            cbuf_free["xcg"] = sl
            pump()
            xch_done = ac(lambda e, par4=par4: e.activation(
                out=xch[par4][:, :], in_=xcf[:, :], func=AF.Copy),
                waits=[sl, xc_free[par4]])
            pump()
            v(lambda e, par4=par4: e.tensor_sub(
                xcl[par4][:, :], xcf[:, :], xch[par4][:, :]),
              waits=[xch_done], incs=[vev.inc()])
            xcl_done = vev.now()
            pump()
            cbuf_free["xcf"] = [xch_done, xcl_done]
            u_pend.append((c, par4, xch_done, xcl_done))

        if upto >= 2:
            u_pend = []
            subA_p1_q = {}
            subA_p2_sd = {}
            subA_all_first = True
            for s0 in range(0, n_sub, 2):
                if s0 + 1 < n_sub:
                    r0, r1 = {}, {}
                    duet(gen_subA_p1(0, s0, r0),
                         gen_subA_p1(0, s0 + 1, r1))
                    ed0 = subA_p2a(0, s0, r0["qd"])
                    ed1 = subA_p2a(0, s0 + 1, r1["qd"])
                    h0, h1 = {}, {}
                    duet(gen_subA_p2b(0, s0, ed0, h0),
                         gen_subA_p2b(0, s0 + 1, ed1, h1))
                    subA_p3(0, s0, h0["sd"])
                    subA_p3(0, s0 + 1, h1["sd"])
                else:
                    qd = subA_p1(0, s0)
                    ed = subA_p2a(0, s0, qd)
                    sd = subA_p2b(0, s0, ed)
                    subA_p3(0, s0, sd)
            if n_sub == 2:
                events = [(0, "p1", 0), (0, "p1", 1), (9, "p2a", 0),
                          (10, "p2a", 1), (10, "p2b", 0), (10, "p2b", 1),
                          (14, "p3", 0), (15, "p3", 1)]
            elif n_sub == 4:
                events = [(0, "p1", 0), (0, "p1", 1),
                          (4, "p2a", 0), (4, "p2b", 0),
                          (5, "p2a", 1), (5, "p2b", 1),
                          (8, "p3", 0), (8, "p1", 2),
                          (9, "p3", 1), (9, "p1", 3),
                          (11, "p2a", 2), (11, "p2b", 2),
                          (12, "p2a", 3), (12, "p2b", 3),
                          (14, "p3", 2), (14, "p3", 3)]
            else:
                events = []
                for s in range(n_sub):
                    base = (CC // n_sub) * s
                    events += [(base, "p1", s), (base + 1, "p2a", s),
                               (base + 2, "p2b", s), (base + 3, "p3", s)]
            active = []
            rr = [0]

            def pumpA():
                done = 0
                while active and done < 2:
                    i = rr[0] % len(active)
                    try:
                        next(active[i])
                        rr[0] = i + 1
                        done += 1
                    except StopIteration:
                        active.pop(i)

            def exhaust(gn):
                try:
                    while True:
                        next(gn)
                except StopIteration:
                    pass
                if gn in active:
                    active.remove(gn)

            st_g = {}
            st_res = {}
            st_ed = {}
            for j in range(n_tiles):
                jp = j % 2
                t0 = j * T_tile
                st1 = None
                for c in range(CC):
                    while len(u_pend) >= 2:
                        emit_u_mm(u_pend.pop(0))
                    if st1 is None:
                        st1 = chunkA_stage1(j, 0)
                    st1_next = (chunkA_stage1(j, c + 1)
                                if c + 1 < CC else None)
                    chunkA_stage2(j, c, st1, pumpA)
                    st1 = st1_next
                    if j + 1 < n_tiles:
                        for (cc_, stage, s) in events:
                            if cc_ != c:
                                continue
                            if stage == "p1":
                                st_res[s] = {}
                                st_g[s] = gen_subA_p1(j + 1, s, st_res[s])
                                active.append(st_g[s])
                            elif stage == "p2a":
                                exhaust(st_g[s])
                                st_ed[s] = subA_p2a(j + 1, s,
                                                    st_res[s]["qd"])
                            elif stage == "p2b":
                                st_res[s] = {}
                                st_g[s] = gen_subA_p2b(j + 1, s, st_ed[s],
                                                       st_res[s])
                                active.append(st_g[s])
                            elif stage == "p3":
                                exhaust(st_g[s])
                                subA_p3(j + 1, s, st_res[s]["sd"])
                while u_pend:
                    emit_u_mm(u_pend.pop(0))
                xn_used2[jp] = tmm.now()
                u_done = tmm.now()
                u_ev = vc(lambda e: e.tensor_scalar(out=u_t[:, :],
                                                    in0=pu[:, 0:T_tile],
                                                    scalar1=0.1, scalar2=None,
                                                    op0=ALU.mult),
                          waits=[u_done, scan_done])
                bank_free[7] = u_ev
                hlp = j % 2
                if j == 0:
                    scan_done = vc(lambda e: e.tensor_tensor_scan(
                        out=hl_tmp[0][:, :], data0=d09[:, :],
                        data1=u_t[:, :], initial=0.0,
                        op0=ALU.mult, op1=ALU.add),
                        waits=[u_ev])
                else:
                    scan_done = vc(lambda e, hlp=hlp: e.tensor_tensor_scan(
                        out=hl_tmp[hlp][:, :], data0=d09[:, :],
                        data1=u_t[:, :],
                        initial=hl_tmp[1 - hlp][:, T_tile - 1:T_tile],
                        op0=ALU.mult, op1=ALU.add),
                        waits=[u_ev, hl_free[hlp]])
                dma(g, lambda e, t0=t0, hlp=hlp: e.dma_start(
                    out=dhlc[:, t0:t0 + T_tile], in_=hl_tmp[hlp][:, :]),
                    shs, [scan_done])
                hl_free[hlp] = shs.now()


        a_last_mm = tmm.now()

        if upto >= 3:
            dma(g, lambda e: e.dma_start(out=dcci[:, :],
                                         in_=hl_tmp[(n_tiles - 1) % 2][
                                             :, T_tile - 1:T_tile]),
                scc, [scan_done])
            g(lambda e: e.collective_compute(
                "AllGather", ALU.bypass,
                replica_groups=[list(range(n_cores))],
                ins=[dcci.ap().opt()], outs=[dcco.ap().opt()]),
              waits=[scc.now()], incs=[gev.inc()])
            dma(g, lambda e: e.dma_start(out=gbuf[0:DS * n_cores, :],
                                         in_=dcco.ap()),
                scc, [gev.now()])
            gh1 = vc(lambda e: e.tensor_copy(out=ghl[:, 0:1], in_=gbuf[:, :]),
                     waits=[scc.now()])
            ghl_done = vc(lambda e: e.tensor_sub(ghl[:, 1:2], gbuf[:, :],
                                                 ghl[:, 0:1]),
                          waits=[gh1])
            t(lambda e: e.matmul(pb[0][0:DS, 0:1], selb[:, :], ghl[:, 0:1],
                                 start=True, stop=False),
              waits=[ghl_done, bank_free[0]])
            t(lambda e: e.matmul(pb[0][0:DS, 0:1], selb[:, :], ghl[:, 1:2],
                                 start=False, stop=True),
              incs=[tmm.inc()])
            v(lambda e: e.tensor_copy(out=hin[:, :], in_=pb[0][0:DS, 0:1]),
              waits=[tmm.now()], incs=[vev.inc()])
            bank_free[0] = vev.now()
            hin_done = vev.now()
            decay_used = ("vv", 0)

        if upto >= 4:
            load_split_big([(k * DI, DI,
                             dwin[k * 128:(k + 1) * 128, DI:2 * DI])
                            for k in range(KC)], wait_first=[a_last_mm],
                           stages=[xnt2f[:, 0:D], xnt2f[:, D:2 * D],
                                   xnt2g[:, 0:D], xnt2g[:, D:2 * D]])
            b_w_done = [sev.now(), vev.now()]
            hh_used = ("tt", 0)
            decay_used = scan_done
            spill_a_done = [sspp[0].now(), sspp[1].now()]
            gk_free = [xp_free[0][0], xp_free[1][0]]
            ykh_free = [("dyh0", 0), ("dyh1", 0)]
            ykl_free = [("dyl0", 0), ("dyl1", 0)]
            yk_used = [("ss", 0)]

            def gen_subB_p1(j, s, res):
                gsub = j * n_sub + s
                par = gsub % 2
                trow = j * T_tile + s * 128
                dma(sy, lambda e, trow=trow, par=par: e.dma_start(
                    out=bufx2[par][:], in_=dx2[trow:trow + 128, :]),
                    sxp[par], list(spill_a_done) + list(x2_free[par]))
                extra = []
                r2 = {}
                yield from gen_ln2_part1(
                    bufx2[par], 128, [sxp[par].now()] + list(extra),
                    ZS[s % 2], RS[s % 2], r2)
                x2_free[par] = [r2["xn2"]]
                res["sd"] = r2["sd"]

            def subB_p1(j, s):
                res = {}
                run_gen(gen_subB_p1(j, s, res))
                return res["sd"]

            def subB_p3(j, s, sd):
                jp = j % 2
                fw = [xn_used2[jp]] if s == 0 else []
                emit_ln2_part2(128, s * 128, sd, xnhb[jp], xnlb[jp],
                               ZS[s % 2], RS[s % 2], first_wait=fw)
                if s == n_sub - 1:
                    xn_ready2[jp] = [vev.now(), sev.now()]

            def chunkB(j, c, hh_done):
                jp = j % 2
                bank = 5 + (c % 2)
                ybank = 1 + (c % 2)
                par = c % 2
                t0 = j * T_tile
                zw = list(xn_ready2[jp]) if c < 2 else []
                if j == 0 and c < 2:
                    zw = zw + b_w_done
                z_done = emit_win_chunk_mm(c, T_tile, bank, zw,
                                           xnhb[jp], xnlb[jp])
                for ui, (wc_, hb_) in enumerate(((wch, hh_t), (wch, hl_t),
                                                 (wcl, hh_t))):
                    wts = []
                    if ui == 0:
                        wts = [bank_free[ybank]]
                        if c < 2:
                            wts.append(hh_done)
                    t(lambda e, c=c, wc_=wc_, hb_=hb_, ybank=ybank,
                      ui=ui: e.matmul(
                          pb[ybank][:, 0:T_tile],
                          wc_[0:DS, c * 128:(c + 1) * 128],
                          hb_[0:DS, 0:T_tile],
                          start=(ui == 0), stop=(ui == 2)),
                      waits=wts, incs=([tmm.inc()] if ui == 2 else []))
                yp_done = tmm.now()
                a(lambda e, bank=bank, par=par: e.activation(
                    out=gk[par][:, :], in_=pb[bank][:, 0:T_tile],
                    func=AF.Silu),
                  waits=[z_done, gk_free[par]], incs=[sev.inc()])
                bank_free[bank] = sev.now()
                g_done = sev.now()
                v(lambda e, ybank=ybank, par=par: e.tensor_mul(
                    yk[:, :], pb[ybank][:, 0:T_tile], gk[par][:, :]),
                  waits=[yp_done, g_done] + list(yk_used), incs=[vev.inc()])
                bank_free[ybank] = vev.now()
                gk_free[par] = vev.now()
                yk_done = vev.now()
                a(lambda e, par=par: e.activation(
                    out=ykh[par][:, :], in_=yk[:, :], func=AF.Copy),
                  waits=[yk_done, ykh_free[par]], incs=[sev.inc()])
                ykh_done = sev.now()
                v(lambda e, par=par: e.tensor_sub(
                    ykl[par][:, :], yk[:, :], ykh[par][:, :]),
                  waits=[ykh_done, ykl_free[par]], incs=[vev.inc()])
                ykl_done = vev.now()
                yk_used[:] = [sev.now(), ykl_done]
                gs0 = t0 // 128
                dma(g, lambda e, c=c, gs0=gs0, par=par: e.dma_start(
                    out=dysh[gs0:gs0 + n_sub, :, c, :].rearrange(
                        "b p n -> p b n"),
                    in_=ykh[par][:, :].rearrange("p (b n) -> p b n",
                                                 b=n_sub)),
                    syph[par], [ykh_done])
                ykh_free[par] = syph[par].now()
                dma(g, lambda e, c=c, gs0=gs0, par=par: e.dma_start(
                    out=dysl[gs0:gs0 + n_sub, :, c, :].rearrange(
                        "b p n -> p b n"),
                    in_=ykl[par][:, :].rearrange("p (b n) -> p b n",
                                                 b=n_sub)),
                    sypl[par], [ykl_done])
                ykl_free[par] = sypl[par].now()

            subB_sd = {}
            for s0 in range(0, n_sub, 2):
                if s0 + 1 < n_sub:
                    r0, r1 = {}, {}
                    duet(gen_subB_p1(0, s0, r0),
                         gen_subB_p1(0, s0 + 1, r1))
                    subB_p3(0, s0, r0["sd"])
                    subB_p3(0, s0 + 1, r1["sd"])
                else:
                    sd = subB_p1(0, s0)
                    subB_p3(0, s0, sd)
            if n_sub == 4:
                p1_events = {1: (0, 1), 7: (2, 3)}
                p3_sched = {4: 0, 5: 1, 11: 2, 12: 3}
            else:
                p1_events = {1: tuple(range(n_sub))}
                p3_sched = ({6: 0, 8: 1} if n_sub == 2 else {6: 0})
            hstage = hl_tmp[0]
            hstage_used = hl_free[0] if n_tiles % 2 == 0 else hl_free[1]
            for j in range(n_tiles):
                jp = j % 2
                t0 = j * T_tile
                dma(sy, lambda e, t0=t0: e.dma_start(
                    out=decay_t[:, :], in_=ddecay[:, t0:t0 + T_tile]),
                    sdec, [decay_used])
                dma(sy, lambda e, t0=t0: e.dma_start(
                    out=hstage[:, :], in_=dhlc[:, t0:t0 + T_tile]),
                    shb, [hstage_used, hl_free[0], hl_free[1]])
                decay_used = vc(lambda e, t0=t0: e.scalar_tensor_tensor(
                    out=htmp[:, :], in0=decay_t[:, :], scalar=hin[:, :],
                    in1=hstage[:, :], op0=ALU.mult, op1=ALU.add),
                    waits=[sdec.now(), shb.now(), hin_done, hh_used])
                hstage_used = decay_used
                hcp = vc(lambda e: e.tensor_copy(out=hh_t[:, :],
                                                 in_=htmp[:, :]),
                         waits=[decay_used])
                hh_done = vc(lambda e: e.tensor_sub(hl_t[:, :], htmp[:, :],
                                                    hh_t[:, :]),
                             waits=[hcp])
                for c in range(CC):
                    chunkB(j, c, hh_done)
                    if j + 1 < n_tiles:
                        if c in p1_events:
                            ss = p1_events[c]
                            ress = {s: {} for s in ss}
                            duet(*[gen_subB_p1(j + 1, s, ress[s])
                                   for s in ss])
                            for s in ss:
                                subB_sd[s] = ress[s]["sd"]
                        if c in p3_sched:
                            s = p3_sched[c]
                            subB_p3(j + 1, s, subB_sd[s])
                xn_used2[jp] = tmm.now()
                hh_used = tmm.now()

            b_last_mm = tmm.now()
            spill_b_done = [syph[0].now(), syph[1].now(),
                            sypl[0].now(), sypl[1].now()]

        if upto >= 5:
            xntf = xnTh[:].bitcast(F32)
            xntg = xnTl[:].bitcast(F32)
            load_split_big([(k * D, D, dwout[k * 128:(k + 1) * 128, :])
                            for k in range(CC)],
                           wait_first=[xn_used2[0]],
                           stages=[xntf[:, 0:D], xntf[:, D:2 * D],
                                   xntg[:, 0:D], xntg[:, D:2 * D]],
                           copy_first=[b_last_mm])
            c_w_done = [sev.now(), vev.now()]
            ybh = [xnTh[:, 0:CC * 128], ybh1[:, 0:CC * 128]]
            ybl = [xnTl[:, 0:CC * 128], ybl1[:, 0:CC * 128]]
            HC = CC // 2
            yb_free_f = [b_last_mm, ("vv", 0)]
            yb_free_b = [b_last_mm, ("vv", 0)]
            out_free = [("dot0", 0), ("dot1", 0)]

            def emitC_yb_loads(gsub):
                par = gsub % 2
                trow = gsub * 128
                gsi = trow // 128
                cw = c_w_done if gsub < 2 else []
                dma(sy, lambda e, gsi=gsi, par=par: e.dma_start(
                    out=ybh[par][:, 0:HC * 128],
                    in_=dysh[gsi, :, 0:HC, :].rearrange("p c n -> p (c n)")),
                    sybp[par], list(spill_b_done) + [yb_free_f[par]] + cw)
                dma(g, lambda e, gsi=gsi, par=par: e.dma_start(
                    out=ybl[par][:, 0:HC * 128],
                    in_=dysl[gsi, :, 0:HC, :].rearrange("p c n -> p (c n)")),
                    sybl2[par], list(spill_b_done) + [yb_free_f[par]] + cw)
                ywf = [sybp[par].now(), sybl2[par].now()]
                dma(sy, lambda e, gsi=gsi, par=par: e.dma_start(
                    out=ybh[par][:, HC * 128:CC * 128],
                    in_=dysh[gsi, :, HC:CC, :].rearrange("p c n -> p (c n)")),
                    sybp[par], [yb_free_b[par]] + cw)
                dma(g, lambda e, gsi=gsi, par=par: e.dma_start(
                    out=ybl[par][:, HC * 128:CC * 128],
                    in_=dysl[gsi, :, HC:CC, :].rearrange("p c n -> p (c n)")),
                    sybl2[par], [yb_free_b[par]] + cw)
                ywb = [sybp[par].now(), sybl2[par].now()]
                return dict(gsub=gsub, par=par, trow=trow, ywf=ywf, ywb=ywb)

            def emitC_x2load(pd):
                par = pd["par"]
                trow = pd["trow"]
                dma(sy, lambda e, trow=trow, par=par: e.dma_start(
                    out=bufx2[par][:], in_=dx2[trow:trow + 128, :]),
                    sxp[par], list(x2_free[par]))
                pd["x2w"] = sxp[par].now()

            def emitC_wout(pd, first_sub):
                gsub = pd["gsub"]
                par = pd["par"]
                ywf = pd["ywf"]
                ywb = pd["ywb"]
                bp = (1, 2) if gsub % 2 == 0 else (5, 6)
                started = [False, False]
                back_waited = False
                for c in range(CC):
                    groups = (((ybh, wxh), (ybh, wxl)), ((ybl, wxh),))
                    for grp in groups:
                        for (yb_, wv) in grp:
                            for half in range(2):
                                is_first = not started[half]
                                started[half] = True
                                is_last = (c == CC - 1 and yb_ is ybl)
                                is_fend = (c == HC - 1 and yb_ is ybl)
                                wts = []
                                if is_first:
                                    wts = list(ywf) + [bank_free[bp[half]]]
                                    if first_sub and half == 0:
                                        wts += c_w_done
                                if c == HC and not back_waited:
                                    wts = wts + list(ywb)
                                    back_waited = True
                                t(lambda e, c=c, yb_=yb_, wv=wv, par=par,
                                  half=half, bp=bp, is_first=is_first,
                                  is_last=is_last: e.matmul(
                                      pb[bp[half]][:, :],
                                      yb_[par][:, c * 128:(c + 1) * 128],
                                      wv[:, c * D + half * 512:
                                         c * D + half * 512 + 512],
                                      start=is_first, stop=is_last),
                                  waits=wts,
                                  incs=([tmm.inc()]
                                        if ((is_last or is_fend)
                                            and half == 1) else []))
                    if c == HC - 1:
                        yb_free_f[par] = tmm.now()
                wo_done = tmm.now()
                yb_free_b[par] = wo_done
                pd["wo_done"] = wo_done
                pd["bp"] = bp

            def gen_x3_p1(pd):
                par = pd["par"]
                bp = pd["bp"]
                for half in range(2):
                    v(lambda e, half=half, par=par, bp=bp: e.tensor_add(
                        bufx[par][:, half * 512:half * 512 + 512],
                        pb[bp[half]][:, :],
                        bufx2[par][:, half * 512:half * 512 + 512]),
                      waits=([pd["wo_done"], pd["x2w"], x_free[par]]
                             if half == 0 else []),
                      incs=[vev.inc()])
                    bank_free[bp[half]] = vev.now()
                    if half == 0:
                        yield
                x3_done = vev.now()
                x2_free[par] = [x3_done]
                yield
                r = {}
                yield from gen_ffn_part1(bufx[par], 128, [x3_done],
                                         ZS[par], RS[par], r)
                pd["qd"] = r["qd"]

            def emitC_part2a(pd):
                pd["ed"] = emit_ffn_part2a(bufx[pd["par"]], 128, pd["qd"],
                                           ZS[pd["par"]], RS[pd["par"]],
                                           b_q=0, b_h=(3, 4))

            def gen_part2b_store(pd):
                par = pd["par"]
                trow = pd["trow"]
                r = {}
                yield from gen_ffn_part2b(bufx[par], bufx[par], 128,
                                          pd["ed"], ZS[par], (), r)
                x4_done = r["x2"]
                dma_nc(a, lambda e, trow=trow, par=par: e.dma_start(
                    out=dout[trow:trow + 128, :], in_=bufx[par][:]),
                    sotp[par], [x4_done])
                x_free[par] = sotp[par].now()

            prev = None
            first_sub = True
            for gp in range(T_core // 256):
                if prev is not None:
                    duet(gen_x3_p1(prev[0]), gen_x3_p1(prev[1]))
                pds = []
                for gsub in (2 * gp, 2 * gp + 1):
                    pds.append(emitC_yb_loads(gsub))
                for pd in pds:
                    emitC_x2load(pd)
                for pd in pds:
                    emitC_wout(pd, first_sub)
                    first_sub = False
                if prev is not None:
                    emitC_part2a(prev[0])
                    emitC_part2a(prev[1])
                    duet(gen_part2b_store(prev[0]), gen_part2b_store(prev[1]))
                prev = pds
            duet(gen_x3_p1(prev[0]), gen_x3_p1(prev[1]))
            emitC_part2a(prev[0])
            emitC_part2a(prev[1])
            duet(gen_part2b_store(prev[0]), gen_part2b_store(prev[1]))

        if upto >= 5:
            g(None, waits=[sotp[0].now(), sotp[1].now()])
        else:
            g(None, waits=[("vv", P.streams["vv"].n), ("ss", P.streams["ss"].n),
                           ("tt", P.streams["tt"].n), ("gg", P.streams["gg"].n),
                           ("dsp0", P.streams["dsp0"].n), ("dsp1", P.streams["dsp1"].n),
                           ("dyh0", P.streams["dyh0"].n), ("dyh1", P.streams["dyh1"].n),
                           ("dyl0", P.streams["dyl0"].n), ("dyl1", P.streams["dyl1"].n),
                           ("dcc", P.streams["dcc"].n)])

        with nc.Block() as block:
            @block.sync
            def _(eng):
                P.run("sync", eng, sems)

            @block.tensor
            def _(eng):
                P.run("tensor", eng, sems)

            @block.vector
            def _(eng):
                P.run("vector", eng, sems)

            @block.scalar
            def _(eng):
                P.run("scalar", eng, sems)

            @block.gpsimd
            def _(eng):
                P.run("gpsimd", eng, sems)

    return nc


def make_core_inputs(x_flat, n_cores, T_core, weights):
    (w_in, conv_w, conv_b, w_B, w_C, w_out, wq, wsc, bias) = weights
    halves_per_seq = 2
    wq_t = np.ascontiguousarray(wq.T)
    conv_w_p = np.ascontiguousarray(
        conv_w[:, 0, :].reshape(CC, 128, 4).transpose(1, 0, 2).reshape(128, CC * 4))
    conv_b_p = np.ascontiguousarray(conv_b.reshape(CC, 128).T)
    wsb = np.ascontiguousarray(np.broadcast_to(wsc[None, :], (128, D)))
    bias_b = np.ascontiguousarray(np.broadcast_to(bias[None, :], (128, D)))
    identf = np.eye(128, dtype=np.float32)
    identb = np.eye(128, dtype=ml_dtypes.bfloat16)
    tt = np.arange(1, T_core + 1, dtype=np.float64)
    decay = np.broadcast_to((0.9 ** tt).astype(np.float32)[None, :],
                            (DS, T_core)).copy()
    in_maps = []
    for i in range(n_cores):
        r0 = i * T_core
        xc = np.ascontiguousarray(x_flat[r0:r0 + T_core])
        is_mid = (i % halves_per_seq) != 0
        if is_mid:
            xhalo = np.ascontiguousarray(x_flat[r0 - 3:r0])
            hm = np.ones((128, 3), dtype=np.float32)
            sel = np.zeros((128, DS), dtype=ml_dtypes.bfloat16)
            prev = i - 1
            for q in range(DS):
                sel[prev * DS + q, q] = 1.0
        else:
            xhalo = np.zeros((3, D), dtype=np.float32)
            hm = np.zeros((128, 3), dtype=np.float32)
            sel = np.zeros((128, DS), dtype=ml_dtypes.bfloat16)
        in_maps.append({
            "x": xc, "xhalo": xhalo, "w_in": w_in,
            "conv_w_p": conv_w_p, "conv_b_p": conv_b_p,
            "w_B": w_B, "w_C": w_C, "w_out": w_out, "wq_t": wq_t,
            "w_scale_b": wsb, "bias_b": bias_b, "halo_mask": hm,
            "sel_b": sel, "decay": decay, "ident_f": identf,
            "ident_b": identb,
        })
    return in_maps


def kernel(L0_attn_x, L0_attn_w_in, L0_attn_conv_w, L0_attn_conv_b,
           L0_attn_w_B, L0_attn_w_C, L0_attn_w_out, L0_ffn_weight_q,
           L0_ffn_weight_scale, L0_ffn_bias):
    n_cores = 8
    T_core = 4096
    T_tile = 512
    B, S, _ = L0_attn_x.shape
    x_flat = np.ascontiguousarray(
        np.asarray(L0_attn_x, dtype=np.float32).reshape(B * S, D))
    weights = (
        np.ascontiguousarray(np.asarray(L0_attn_w_in, dtype=np.float32)),
        np.ascontiguousarray(np.asarray(L0_attn_conv_w, dtype=np.float32)),
        np.ascontiguousarray(np.asarray(L0_attn_conv_b, dtype=np.float32)),
        np.ascontiguousarray(np.asarray(L0_attn_w_B, dtype=np.float32)),
        np.ascontiguousarray(np.asarray(L0_attn_w_C, dtype=np.float32)),
        np.ascontiguousarray(np.asarray(L0_attn_w_out, dtype=np.float32)),
        np.ascontiguousarray(np.asarray(L0_ffn_weight_q, dtype=np.int8)),
        np.ascontiguousarray(np.asarray(L0_ffn_weight_scale, dtype=np.float32)),
        np.ascontiguousarray(np.asarray(L0_ffn_bias, dtype=np.float32)),
    )
    nc = build_program(n_cores, T_core, T_tile)
    in_maps = make_core_inputs(x_flat, n_cores, T_core, weights)
    res = run_bass_kernel_spmd(nc, in_maps, core_ids=list(range(n_cores)))
    out = np.concatenate([np.asarray(r["out"]) for r in res.results], axis=0)
    return out.reshape(B, S, D).astype(np.float32)


# version 49
# speedup vs baseline: 1.0231x; 1.0032x over previous
import numpy as np
import ml_dtypes
import concourse.bass as bass
import concourse.mybir as mybir
from concourse.bass_utils import run_bass_kernel_spmd
from contextlib import ExitStack

F32 = mybir.dt.float32
BF16 = mybir.dt.bfloat16
I8 = mybir.dt.int8
I32 = mybir.dt.int32
AF = mybir.ActivationFunctionType
ALU = mybir.AluOpType
AXX = mybir.AxisListType.X

D = 1024
DI = 2048
DS = 16
KC = D // 128
CC = DI // 128
EPS_LN = 1e-5
EPS_Q = 1e-6


class Stream:
    def __init__(self, name, step=1):
        self.name = name
        self.step = step
        self.n = 0

    def inc(self, k=None):
        k = self.step if k is None else k
        self.n += k
        return (self.name, k)

    def now(self):
        return (self.name, self.n)


class Plan:
    def __init__(self):
        self.items = []
        self.streams = {}

    def S(self, name, step=1):
        if name not in self.streams:
            self.streams[name] = Stream(name, step)
        return self.streams[name]

    def emit(self, eng, f, waits=(), incs=()):
        self.items.append((eng, tuple(waits), tuple(incs), f))

    def drain(self, eng):
        self.items.append((eng, (), (), "drain"))

    def run(self, engname, engobj, sems):
        for (eng, waits, incs, f) in self.items:
            if eng != engname:
                continue
            if f == "drain":
                engobj.drain()
                continue
            for (s, vv) in waits:
                if vv > 0:
                    engobj.wait_ge(sems[s], vv)
            if f is None:
                continue
            inst = f(engobj)
            for (s, k) in incs:
                inst.then_inc(sems[s], k)


def build_program(n_cores, T_core, T_tile, upto=5):
    assert T_core % T_tile == 0 and T_tile % 128 == 0
    n_tiles = T_core // T_tile
    n_sub = T_tile // 128
    nc = bass.Bass("TRN2", target_bir_lowering=False)

    dx = nc.dram_tensor("x", [T_core, D], F32, kind="ExternalInput").ap()
    dxh = nc.dram_tensor("xhalo", [3, D], F32, kind="ExternalInput").ap()
    dwin = nc.dram_tensor("w_in", [D, 2 * DI], F32, kind="ExternalInput").ap()
    dconvw = nc.dram_tensor("conv_w_p", [128, CC * 4], F32, kind="ExternalInput").ap()
    dconvb = nc.dram_tensor("conv_b_p", [128, CC], F32, kind="ExternalInput").ap()
    dwB = nc.dram_tensor("w_B", [DI, DS], F32, kind="ExternalInput").ap()
    dwC = nc.dram_tensor("w_C", [DS, DI], F32, kind="ExternalInput").ap()
    dwout = nc.dram_tensor("w_out", [DI, D], F32, kind="ExternalInput").ap()
    dwq = nc.dram_tensor("wq_t", [D, D], I8, kind="ExternalInput").ap()
    dwsb = nc.dram_tensor("w_scale_b", [128, D], F32, kind="ExternalInput").ap()
    dbb = nc.dram_tensor("bias_b", [128, D], F32, kind="ExternalInput").ap()
    dhm = nc.dram_tensor("halo_mask", [128, 3], F32, kind="ExternalInput").ap()
    dsel = nc.dram_tensor("sel_b", [128, DS], BF16, kind="ExternalInput").ap()
    ddecay = nc.dram_tensor("decay", [DS, T_core], F32, kind="ExternalInput").ap()
    didf = nc.dram_tensor("ident_f", [128, 128], F32, kind="ExternalInput").ap()
    didb = nc.dram_tensor("ident_b", [128, 128], BF16, kind="ExternalInput").ap()
    dout = nc.dram_tensor("out", [T_core, D], F32, kind="ExternalOutput").ap()
    dx2 = nc.dram_tensor("x2_sp", [T_core, D], F32).ap()
    dysh = nc.dram_tensor("ysp_h", [T_core // 128, 128, CC, 128],
                          BF16).ap()
    dysl = nc.dram_tensor("ysp_l", [T_core // 128, 128, CC, 128],
                          BF16).ap()
    dcci = nc.dram_tensor("cc_in", [DS, 1], F32)
    dcco = nc.dram_tensor("cc_out", [DS * n_cores, 1], F32)
    dhlc = nc.dram_tensor("hloc_sp", [DS, T_core], F32).ap()

    P = Plan()
    es = ExitStack()

    def sbt(name, shape, dt):
        return es.enter_context(nc.sbuf_tensor(name, shape, dt))

    def pst(name, shape, dt):
        return es.enter_context(nc.psum_tensor(name, shape, dt))

    with es:
        wq_b = sbt("wq_b", [128, KC * D], BF16)
        wsb = sbt("wsb", [128, D], F32)
        biasb = sbt("biasb", [128, D], F32)
        wxh = sbt("wxh", [128, KC * DI], BF16)
        wxl = sbt("wxl", [128, KC * DI], BF16)
        wbh = sbt("wbh", [128, CC * DS], BF16)
        wbl = sbt("wbl", [128, CC * DS], BF16)
        wch = sbt("wch", [DS, DI], BF16)
        wcl = sbt("wcl", [DS, DI], BF16)
        convw = sbt("convw", [128, CC * 4], F32)
        convb = sbt("convb", [128, CC], F32)
        selb = sbt("selb", [128, DS], BF16)
        identb = sbt("identb", [128, 128], BF16)
        halom = sbt("halom", [128, 3], F32)
        halobuf = sbt("halobuf", [128, CC * 3], F32)
        epsln = sbt("epsln", [128, 1], F32)
        u_t = sbt("u_t", [DS, T_tile], F32)
        d09 = sbt("d09", [DS, T_tile], F32)
        hl_tmp = [sbt(f"hl_tmp{i}", [DS, T_tile], F32) for i in range(2)]
        hh_t = sbt("hh_t", [DS, T_tile], BF16)
        hl_t = sbt("hl_t", [DS, T_tile], BF16)
        decay_t = d09
        htmp = u_t
        hin = sbt("hin", [DS, 1], F32)
        gbuf = sbt("gbuf", [128, 1], F32)
        ghl = sbt("ghl", [128, 2], BF16)

        bufx = [sbt(f"bufx{i}", [128, D], F32) for i in range(2)]
        bufx2 = [sbt(f"bufx2{i}", [128, D], F32) for i in range(2)]
        sqS = sbt("sqS", [128, D], F32)
        sq_guard = [("ss", 0)]
        ZS = []
        for zi in range(2):
            ZS.append(dict(
                scrA=sbt(f"scrA{zi}", [128, D], F32),
                scrB=sbt(f"scrB{zi}", [128, D], F32),
                scrGE=sbt(f"scrGE{zi}", [128, D], BF16),
                scrLE=sbt(f"scrLE{zi}", [128, D], BF16),
                qbf=sbt(f"qbf{zi}", [128, D], BF16),
                xqT=sbt(f"xqT{zi}", [128, KC * 128], BF16),
            ))
        xnTh = sbt("xnTh", [128, KC * T_tile], BF16)
        xnTl = sbt("xnTl", [128, KC * T_tile], BF16)
        xnTh2 = sbt("xnTh2", [128, KC * T_tile], BF16)
        xnTl2 = sbt("xnTl2", [128, KC * T_tile], BF16)
        ybh1 = xnTh2
        ybl1 = xnTl2
        wstage = ZS[0]["scrB"]
        wstage2 = ZS[1]["scrB"]
        wqi8 = ZS[0]["scrGE"][:, 0:D // 2].bitcast(I8)
        wcstage = wstage[0:DS, 0:D]
        wbstage = wstage[:, 0:CC * DS]
        xp = [sbt(f"xp{i}", [128, T_tile + 3], F32) for i in range(2)]
        xcf = sbt("xcf", [128, T_tile], F32)
        xcg = sbt("xcg", [128, T_tile], F32)
        xch = [sbt(f"xch{i}", [128, T_tile], BF16) for i in range(2)]
        xcl = [sbt(f"xcl{i}", [128, T_tile], BF16) for i in range(2)]
        gk = [xp[0][:, 0:T_tile], xp[1][:, 0:T_tile]]
        yk = xcf
        xcg_b = xcg[:].bitcast(BF16)
        ykh = [xcg_b[:, 0:T_tile], xch[0][:]]
        ykl = [xcg_b[:, T_tile:2 * T_tile], xcl[0][:]]
        for zi in range(2):
            ZS[zi]["stats"] = sbt(f"stats{zi}", [128, 16], F32)
            ZS[zi]["vtail"] = ("vv", 0)
            ZS[zi]["stail"] = ("ss", 0)
            ZS[zi]["sq_done"] = ("ss", 0)
        pb = [pst(f"pb{i}", [128, 512], F32) for i in range(8)]
        pu = pb[7][0:DS, :]

        sem_names = ["dhs", "dhb", "dw2",
                     "dw", "dcc", "ddc", "dhl",
                     "dxa0", "dxa1", "dsp0", "dsp1",
                     "dyh0", "dyh1", "dyl0", "dyl1",
                     "dyb0", "dyb1", "dot0", "dot1",
                     "vv", "ss", "tt", "gg"]
        sems = {n: es.enter_context(nc.semaphore(n)) for n in sem_names}

        sw = P.S("dw", 16)
        sw2 = P.S("dw2", 16)
        shs = P.S("dhs", 16)
        shb = P.S("dhb", 16)
        scc = P.S("dcc", 16)
        sdec = P.S("ddc", 16)
        shl = P.S("dhl", 16)
        sxp = [P.S("dxa0", 16), P.S("dxa1", 16)]
        sspp = [P.S("dsp0", 16), P.S("dsp1", 16)]
        syph = [P.S("dyh0", 16), P.S("dyh1", 16)]
        sypl = [P.S("dyl0", 16), P.S("dyl1", 16)]
        sybp = [P.S("dyb0", 16), P.S("dyb1", 16)]
        sybl2 = [P.S("dyl0", 16), P.S("dyl1", 16)]
        sotp = [P.S("dot0", 16), P.S("dot1", 16)]
        vev = P.S("vv")
        sev = P.S("ss")
        tmm = P.S("tt")
        gev = P.S("gg")

        def v(f, waits=(), incs=()):
            P.emit("vector", f, waits, incs)

        def a(f, waits=(), incs=()):
            P.emit("scalar", f, waits, incs)

        def t(f, waits=(), incs=()):
            P.emit("tensor", f, waits, incs)

        def g(f, waits=(), incs=()):
            P.emit("gpsimd", f, waits, incs)

        def sy(f, waits=(), incs=()):
            P.emit("sync", f, waits, incs)

        def dma(emitfn, f, stream, waits=()):
            emitfn(f, waits=[stream.now()] + list(waits), incs=[stream.inc()])

        def dma_nc(emitfn, f, stream, waits=()):
            emitfn(f, waits=list(waits), incs=[stream.inc()])

        v(lambda e: e.memset(epsln[:], EPS_LN), incs=[vev.inc()])
        v(lambda e: e.memset(d09[:], 0.9), incs=[vev.inc()])
        v(lambda e: e.memset(halobuf[:], 0.0), incs=[vev.inc()])
        v(lambda e: e.memset(gbuf[:], 0.0), incs=[vev.inc()])
        P.drain("vector")

        for (dst, src) in ((convw, dconvw), (convb, dconvb), (wsb, dwsb),
                           (biasb, dbb), (selb, dsel),
                           (identb, didb), (halom, dhm)):
            dma(sy, lambda e, dst=dst, src=src: e.dma_start(out=dst[:],
                                                            in_=src[:]), sw)
        dma(sy, lambda e: e.dma_start(
            out=wbstage.rearrange("p (c s) -> p c s", c=CC),
            in_=dwB[:].rearrange("(c p) s -> p c s", p=128)), sw)
        a(lambda e: e.activation(out=wbh[:], in_=wbstage, func=AF.Copy),
          waits=[sw.now()], incs=[sev.inc()])
        v(lambda e: e.tensor_sub(wbl[:], wbstage, wbh[:]),
          waits=[sev.now()], incs=[vev.inc()])
        for h in range(2):
            dma(sy, lambda e, h=h: e.dma_start(
                out=wcstage, in_=dwC[:, h * D:(h + 1) * D]), sw,
                [sev.now(), vev.now()])
            a(lambda e, h=h: e.activation(out=wch[:, h * D:(h + 1) * D],
                                          in_=wcstage, func=AF.Copy),
              waits=[sw.now()], incs=[sev.inc()])
            v(lambda e, h=h: e.tensor_sub(wcl[:, h * D:(h + 1) * D], wcstage,
                                          wch[:, h * D:(h + 1) * D]),
              waits=[sev.now()], incs=[vev.inc()])
        for k in range(KC):
            wts = [sev.now()] if k > 0 else []
            dma(sy, lambda e, k=k: e.dma_start(
                out=wqi8[:], in_=dwq[k * 128:(k + 1) * 128, :]), sw, wts)
            a(lambda e, k=k: e.activation(
                out=wq_b[:, k * D:(k + 1) * D], in_=wqi8[:], func=AF.Copy),
              waits=[sw.now()], incs=[sev.inc()])

        def load_split_big(chunks, wait_first=(), stages=None,
                           copy_first=()):
            if stages is None:
                stages = [wstage[:, 0:D], wstage2[:, 0:D],
                          ZS[0]["scrA"][:, 0:D], ZS[1]["scrA"][:, 0:D]]
            marks = [list(wait_first) for _ in stages]
            i = 0
            pieces = []
            for (off0, width0, src0) in chunks:
                nh = (width0 + D - 1) // D
                for h in range(nh):
                    pieces.append((off0 + h * D,
                                   min(D, width0 - h * D),
                                   src0[:, h * D:h * D + min(D, width0 - h * D)]))
            for (off, width, src) in pieces:
                stg = stages[i % len(stages)]
                swk = sw if i % 2 == 0 else sw2
                dma(sy, lambda e, off=off, width=width, src=src,
                    stg=stg: e.dma_start(
                    out=stg[:, :width] if False else stg[0:128, 0:width],
                    in_=src), swk, marks[i % len(stages)])
                a(lambda e, off=off, width=width, stg=stg: e.activation(
                    out=wxh[:, off:off + width], in_=stg[0:128, 0:width],
                    func=AF.Copy),
                  waits=[swk.now()] + (list(copy_first)
                                       if i < len(stages) else []),
                  incs=[sev.inc()])
                v(lambda e, off=off, width=width, stg=stg: e.tensor_sub(
                    wxl[:, off:off + width], stg[0:128, 0:width],
                    wxh[:, off:off + width]),
                  waits=[sev.now()], incs=[vev.inc()])
                marks[i % len(stages)] = [sev.now(), vev.now()]
                i += 1

        prep_done = [sev.now(), vev.now(), sw.now()]
        xnt2f = xnTh2[:].bitcast(F32)
        xnt2g = xnTl2[:].bitcast(F32)
        load_split_big([(k * DI, DI, dwin[k * 128:(k + 1) * 128, 0:DI])
                        for k in range(KC)],
                       wait_first=[sev.now(), vev.now()],
                       stages=[xnt2f[:, 0:D], xnt2f[:, D:2 * D],
                               xnt2g[:, 0:D], xnt2g[:, D:2 * D]])

        def vc(f, waits=()):
            v(f, waits=waits, incs=[vev.inc()])
            return vev.now()

        def ac(f, waits=()):
            a(f, waits=waits, incs=[sev.inc()])
            return sev.now()

        def run_gen(gn):
            for _ in gn:
                pass

        def duet(*gens_in):
            gens = list(gens_in)
            while gens:
                nxt = []
                for gn in gens:
                    try:
                        next(gn)
                        nxt.append(gn)
                    except StopIteration:
                        pass
                gens = nxt

        def gen_ln(xv, ntok, need_quant, waits0, Z, res):
            scrA = Z["scrA"]; stats = Z["stats"]
            w0 = list(waits0) + [Z["vtail"], Z["stail"]]
            sq_done = ac(lambda e: e.activation(out=sqS[0:ntok, :], in_=xv,
                                                func=AF.Square,
                                                accum_out=stats[0:ntok, 1:2]),
                         waits=w0 + [sq_guard[0]])
            sq_guard[0] = sq_done
            e1 = vc(lambda e: e.tensor_reduce(out=stats[0:ntok, 0:1], in_=xv,
                                              axis=AXX, op=ALU.add),
                    waits=w0)
            yield
            e2 = vc(lambda e: e.tensor_scalar(
                out=stats[0:ntok, 2:3], in0=stats[0:ntok, 0:1],
                scalar1=-1.0 / D, scalar2=None, op0=ALU.mult), waits=[e1])
            yield
            e3 = vc(lambda e: e.tensor_mul(stats[0:ntok, 3:4],
                                           stats[0:ntok, 2:3],
                                           stats[0:ntok, 2:3]), waits=[e2])
            yield
            e4 = vc(lambda e: e.tensor_scalar(
                out=stats[0:ntok, 4:5], in0=stats[0:ntok, 1:2],
                scalar1=1.0 / D, scalar2=stats[0:ntok, 3:4],
                op0=ALU.mult, op1=ALU.subtract), waits=[e3, sq_done])
            sq2 = ac(lambda e: e.activation(out=stats[0:ntok, 5:6],
                                            in_=stats[0:ntok, 4:5],
                                            func=AF.Sqrt,
                                            bias=epsln[0:ntok, :]),
                     waits=[e4])
            yield
            if need_quant:
                e7 = vc(lambda e: e.tensor_scalar(
                    out=scrA[0:ntok, :], in0=xv, scalar1=stats[0:ntok, 2:3],
                    scalar2=None, op0=ALU.add), waits=[e2])
                yield
            e6 = vc(lambda e: e.reciprocal(out=stats[0:ntok, 6:7],
                                           in_=stats[0:ntok, 5:6]),
                    waits=[sq2])
            yield
            if not need_quant:
                Z["vtail"] = e6
                Z["stail"] = sq2
                res["ln"] = e6
                return
            e8 = vc(lambda e: e.tensor_reduce(out=stats[0:ntok, 7:8],
                                              in_=scrA[0:ntok, :], axis=AXX,
                                              op=ALU.max,
                                              apply_absolute_value=True),
                    waits=[e7])
            yield
            e9 = vc(lambda e: e.tensor_mul(stats[0:ntok, 8:9],
                                           stats[0:ntok, 7:8],
                                           stats[0:ntok, 6:7]),
                    waits=[e8, e6])
            yield
            e10 = vc(lambda e: e.tensor_scalar(
                out=stats[0:ntok, 9:10], in0=stats[0:ntok, 8:9],
                scalar1=EPS_Q, scalar2=1.0 / 127.0,
                op0=ALU.max, op1=ALU.mult), waits=[e9])
            yield
            e13 = vc(lambda e: e.tensor_scalar(
                out=stats[0:ntok, 13:14], in0=stats[0:ntok, 9:10],
                scalar1=0.5, scalar2=None, op0=ALU.mult), waits=[e10])
            yield
            e11 = vc(lambda e: e.reciprocal(out=stats[0:ntok, 11:12],
                                            in_=stats[0:ntok, 9:10]),
                     waits=[e10])
            yield
            e12 = vc(lambda e: e.tensor_mul(stats[0:ntok, 12:13],
                                            stats[0:ntok, 6:7],
                                            stats[0:ntok, 11:12]),
                     waits=[e11])
            Z["vtail"] = e12
            Z["stail"] = sq2
            res["ln"] = e12

        def emit_ln(xv, ntok, need_quant, waits0, Z):
            res = {}
            run_gen(gen_ln(xv, ntok, need_quant, waits0, Z, res))
            return res["ln"]

        def gen_quant(ntok, Z, R, lnq, res):
            scrA = Z["scrA"]; scrB = Z["scrB"]
            scrGE = Z["scrGE"]; scrLE = Z["scrLE"]
            qbf = Z["qbf"]; stats = Z["stats"]
            q1 = ac(lambda e: e.activation(
                out=scrB[0:ntok, :], in_=scrA[0:ntok, :], func=AF.Copy,
                scale=stats[0:ntok, 12:13]), waits=[lnq])
            q2 = vc(lambda e: e.tensor_copy(out=scrA[0:ntok, :].bitcast(I32),
                                            in_=scrB[0:ntok, :]),
                    waits=[q1])
            yield
            q3 = vc(lambda e: e.tensor_copy(out=qbf[0:ntok, :],
                                            in_=scrA[0:ntok, :].bitcast(I32)),
                    waits=[q2, R["qbf"]])
            yield
            q4 = vc(lambda e: e.tensor_sub(scrB[0:ntok, :], scrB[0:ntok, :],
                                           qbf[0:ntok, :]), waits=[q3])
            yield
            qGE = vc(lambda e: e.tensor_scalar(
                out=scrGE[0:ntok, :], in0=scrB[0:ntok, :], scalar1=0.5,
                scalar2=None, op0=ALU.is_ge), waits=[q4, R["xn2"]])
            yield
            qLE = vc(lambda e: e.tensor_scalar(
                out=scrLE[0:ntok, :], in0=scrB[0:ntok, :], scalar1=-0.5,
                scalar2=None, op0=ALU.is_le), waits=[q4])
            yield
            q7a = vc(lambda e: e.tensor_sub(scrB[0:ntok, 0:512],
                                            scrGE[0:ntok, 0:512],
                                            scrLE[0:ntok, 0:512]),
                     waits=[qGE, qLE])
            yield
            q8a = vc(lambda e: e.tensor_add(qbf[0:ntok, 0:512],
                                            qbf[0:ntok, 0:512],
                                            scrB[0:ntok, 0:512]),
                     waits=[q7a])
            yield
            q7b = vc(lambda e: e.tensor_sub(scrB[0:ntok, 512:1024],
                                            scrGE[0:ntok, 512:1024],
                                            scrLE[0:ntok, 512:1024]),
                     waits=[qLE])
            yield
            q8b = vc(lambda e: e.tensor_add(qbf[0:ntok, 512:1024],
                                            qbf[0:ntok, 512:1024],
                                            scrB[0:ntok, 512:1024]),
                     waits=[q7b])
            Z["vtail"] = q8b
            res["qd"] = q8b
            res["qd_f"] = q8a

        bank_free = {i: ("vv", 0) for i in range(8)}
        RS = [{"xn2": ("tt", 0), "qbf": ("tt", 0), "xqT": ("tt", 0)}
              for _ in range(2)]

        def gen_ffn_part1(xbuf, ntok, x_wait, Z, R, res):
            r = {}
            yield from gen_ln(xbuf[0:ntok, :], ntok, True, x_wait, Z, r)
            yield from gen_quant(ntok, Z, R, r["ln"], res)

        def emit_ffn_part1(xbuf, ntok, x_wait, Z, R):
            res = {}
            run_gen(gen_ffn_part1(xbuf, ntok, x_wait, Z, R, res))
            return res["qd"]

        def emit_ffn_part2a(xbuf, ntok, qdone, Z, R, b_q=0, b_h=(1, 2),
                            qdone_f=None):
            qbf = Z["qbf"]; xqT = Z["xqT"]; stats = Z["stats"]
            scrB = Z["scrB"]
            pbq = pb[b_q][:].bitcast(BF16)
            stq = ntok if ntok % 2 == 0 else ntok + 1
            for c in range(KC):
                if c == 0:
                    wts = [qdone_f if qdone_f is not None else qdone,
                           bank_free[b_q]]
                elif c == KC // 2 and qdone_f is not None:
                    wts = [qdone]
                else:
                    wts = []
                t(lambda e, c=c, ntok=ntok, stq=stq, pbq=pbq,
                  qbf=qbf: e.matmul(
                    pbq[:, c * stq:c * stq + ntok],
                    qbf[0:ntok, c * 128:(c + 1) * 128],
                    identb[0:ntok, 0:ntok],
                    is_transpose=True, start=(c == 0), stop=(c == KC - 1)),
                  waits=wts, incs=([tmm.inc()] if c == KC - 1 else []))
            trq_done = tmm.now()
            R["qbf"] = trq_done
            if stq == ntok:
                a(lambda e, ntok=ntok, pbq=pbq, xqT=xqT: e.activation(
                    out=xqT[:, 0:KC * ntok], in_=pbq[:, 0:KC * ntok],
                    func=AF.Copy),
                  waits=[trq_done, R["xqT"]], incs=[sev.inc()])
            else:
                a(lambda e, ntok=ntok, stq=stq, pbq=pbq,
                  xqT=xqT: e.activation(
                    out=xqT[:, 0:KC * ntok].rearrange(
                        "p (c q) -> p c q", q=ntok),
                    in_=pbq[:, 0:KC * stq].rearrange(
                        "p (c q) -> p c q", q=stq)[:, :, 0:ntok],
                    func=AF.Copy),
                  waits=[trq_done, R["xqT"]], incs=[sev.inc()])
            bank_free[b_q] = sev.now()
            xqt_done = sev.now()
            for k in range(KC):
                for half in range(2):
                    wts = []
                    if k == 0:
                        wts = [bank_free[b_h[half]]]
                        if half == 0:
                            wts.append(xqt_done)
                    t(lambda e, half=half, k=k, ntok=ntok, xqT=xqT: e.matmul(
                        pb[b_h[half]][0:ntok, :],
                        xqT[:, k * ntok:(k + 1) * ntok],
                        wq_b[:, k * D + half * 512: k * D + half * 512 + 512],
                        start=(k == 0), stop=(k == KC - 1)),
                      waits=wts, incs=([tmm.inc()]
                                       if (k == KC - 1 and half == 1) else []))
            ffn_done = tmm.now()
            R["xqT"] = ffn_done
            for half in range(2):
                a(lambda e, half=half, ntok=ntok, scrB=scrB,
                  stats=stats: e.activation(
                    out=scrB[0:ntok, half * 512:half * 512 + 512],
                    in_=pb[b_h[half]][0:ntok, :], func=AF.Copy,
                    scale=stats[0:ntok, 13:14]),
                  waits=([ffn_done] if half == 0 else []), incs=[sev.inc()])
                bank_free[b_h[half]] = sev.now()
            Z["stail"] = sev.now()
            return sev.now()

        def gen_ffn_part2b(xbuf, x2buf, ntok, evac_done, Z, x2_wait, res):
            scrB = Z["scrB"]
            p1 = vc(lambda e, ntok=ntok, scrB=scrB: e.tensor_mul(
                scrB[0:ntok, :], scrB[0:ntok, :], wsb[0:ntok, :]),
                waits=[evac_done, Z["vtail"]])
            yield
            p2 = vc(lambda e, ntok=ntok, scrB=scrB: e.tensor_add(
                scrB[0:ntok, :], scrB[0:ntok, :], xbuf[0:ntok, :]),
                waits=[p1])
            yield
            p3 = vc(lambda e, ntok=ntok, scrB=scrB: e.scalar_tensor_tensor(
                out=x2buf[0:ntok, :], in0=biasb[0:ntok, :], scalar=0.5,
                in1=scrB[0:ntok, :], op0=ALU.mult, op1=ALU.add),
                waits=[p2] + list(x2_wait))
            Z["vtail"] = p3
            res["x2"] = p3

        def emit_ffn_part2b(xbuf, x2buf, ntok, evac_done, Z, x2_wait=()):
            res = {}
            run_gen(gen_ffn_part2b(xbuf, x2buf, ntok, evac_done, Z,
                                   x2_wait, res))
            return res["x2"]

        def emit_ffn_part2(xbuf, x2buf, ntok, qdone, Z, R, b_q=0,
                           b_h=(1, 2), x2_wait=()):
            ed = emit_ffn_part2a(xbuf, ntok, qdone, Z, R, b_q, b_h)
            return emit_ffn_part2b(xbuf, x2buf, ntok, ed, Z, x2_wait)

        def emit_ffn_block(xbuf, x2buf, ntok, x_wait, Z, R, b_q=0,
                           b_h=(1, 2), x2_wait=()):
            qdone = emit_ffn_part1(xbuf, ntok, x_wait, Z, R)
            return emit_ffn_part2(xbuf, x2buf, ntok, qdone, Z, R, b_q, b_h,
                                  x2_wait)

        def gen_ln2_part1(x2buf, ntok, x2_wait, Z, R, res):
            scrA = Z["scrA"]; scrGE = Z["scrGE"]; scrLE = Z["scrLE"]
            stats = Z["stats"]
            r = {}
            yield from gen_ln(x2buf[0:ntok, :], ntok, False, x2_wait, Z, r)
            xn2_done = vc(
                lambda e, ntok=ntok, scrA=scrA, stats=stats: e.tensor_scalar(
                    out=scrA[0:ntok, :], in0=x2buf[0:ntok, :],
                    scalar1=stats[0:ntok, 2:3], scalar2=stats[0:ntok, 6:7],
                    op0=ALU.add, op1=ALU.mult),
                waits=[r["ln"]])
            yield
            res["xn2"] = xn2_done
            sp1 = ac(lambda e, ntok=ntok, scrGE=scrGE, scrA=scrA: e.activation(
                out=scrGE[0:ntok, :], in_=scrA[0:ntok, :], func=AF.Copy),
                waits=[xn2_done, R["xn2"]])
            sp2 = vc(lambda e, ntok=ntok, scrLE=scrLE, scrA=scrA,
                     scrGE=scrGE: e.tensor_sub(scrLE[0:ntok, :],
                                               scrA[0:ntok, :],
                                               scrGE[0:ntok, :]),
                     waits=[sp1])
            Z["vtail"] = sp2
            res["sd"] = sp2

        def emit_ln2_part1(x2buf, ntok, x2_wait, Z, R):
            res = {}
            run_gen(gen_ln2_part1(x2buf, ntok, x2_wait, Z, R, res))
            return res["xn2"], res["sd"]

        def emit_ln2_part2(ntok, col_off, split_done, xnh, xnl, Z, R,
                           banks=(3, 4), first_wait=()):
            scrGE = Z["scrGE"]; scrLE = Z["scrLE"]
            """bf16 transposes of scrGE/scrLE into xnh/xnl columns."""
            stq = ntok if ntok % 2 == 0 else ntok + 1
            for grp, (srcb, dstb) in enumerate(((scrGE, xnh), (scrLE, xnl))):
                bank = banks[grp]
                pbv = pb[bank][:].bitcast(BF16)
                for c in range(KC):
                    wts = ([split_done, bank_free[bank]] + list(first_wait)) \
                        if c == 0 else []
                    t(lambda e, c=c, ntok=ntok, stq=stq, pbv=pbv,
                      srcb=srcb: e.matmul(
                        pbv[:, c * stq:c * stq + ntok],
                        srcb[0:ntok, c * 128:(c + 1) * 128],
                        identb[0:ntok, 0:ntok],
                        is_transpose=True, start=(c == 0), stop=(c == KC - 1)),
                      waits=wts, incs=([tmm.inc()] if c == KC - 1 else []))
                tr_done = tmm.now()
                dst_ap = dstb[:].rearrange(
                    "p (c q) -> p c q", q=T_tile)[:, :,
                                                  col_off:col_off + ntok]
                src_ap = pbv[:, 0:KC * stq].rearrange(
                    "p (c q) -> p c q", q=stq)[:, :, 0:ntok]
                a(lambda e, dst_ap=dst_ap, src_ap=src_ap: e.activation(
                    out=dst_ap, in_=src_ap, func=AF.Copy),
                  waits=[tr_done], incs=[sev.inc()])
                bank_free[bank] = sev.now()
            R["xn2"] = tmm.now()

        def emit_ln2_transpose(x2buf, ntok, col_off, x2_wait, Z, R,
                               banks=(3, 4), xnh=None, xnl=None,
                               first_wait=()):
            if xnh is None:
                xnh, xnl = xnTh, xnTl
            xn2_done, split_done = emit_ln2_part1(x2buf, ntok, x2_wait, Z, R)
            emit_ln2_part2(ntok, col_off, split_done, xnh, xnl, Z, R, banks,
                           first_wait)
            return xn2_done


        def emit_win_chunk_mm(cchunk, ntok, bank, ready_wait,
                              xnh=None, xnl=None):
            if xnh is None:
                xnh, xnl = xnTh, xnTl
            n0 = cchunk * 128
            first = True
            for k in range(KC):
                for (wbuf, xbuf_) in ((wxh, xnh), (wxh, xnl), (wxl, xnh)):
                    is_first = first
                    first = False
                    is_last = (k == KC - 1 and wbuf is wxl)
                    wts = (list(ready_wait) + [bank_free[bank]]) \
                        if is_first else []
                    t(lambda e, k=k, wbuf=wbuf, xbuf_=xbuf_, n0=n0,
                      is_first=is_first, is_last=is_last, ntok=ntok: e.matmul(
                          pb[bank][:, 0:ntok],
                          wbuf[:, k * DI + n0:k * DI + n0 + 128],
                          xbuf_[:, k * T_tile:k * T_tile + ntok],
                          start=is_first, stop=is_last),
                      waits=wts, incs=([tmm.inc()] if is_last else []))
            return tmm.now()

        if upto >= 1:
            dma(sy, lambda e: e.dma_start(out=bufx[0][0:3, :], in_=dxh[:]),
                sxp[0])
            halo_wait = [sxp[0].now()] + prep_done
            halo_x2 = emit_ffn_block(bufx[0], bufx2[0], 3, halo_wait,
                                     ZS[0], RS[0])
            halo_xn2 = emit_ln2_transpose(bufx2[0], 3, 0, [halo_x2],
                                          ZS[0], RS[0])
            halo_xn = [vev.now(), sev.now()]
            hbuf_mark = [None] * CC
            for c in range(CC):
                bank = 5 + (c % 2)
                mm_done = emit_win_chunk_mm(c, 3, bank, halo_xn if c < 2 else [])
                hbuf_mark[c] = vc(lambda e, c=c, bank=bank: e.tensor_mul(
                    halobuf[:, c * 3:(c + 1) * 3], pb[bank][:, 0:3], halom[:]),
                    waits=[mm_done])
                bank_free[bank] = hbuf_mark[c]
        else:
            halo_x2 = ("vv", 0)
            halo_xn2 = ("vv", 0)
            hbuf_mark = [("vv", 0)] * CC
        x_free = [halo_x2, ("vv", 0)]
        x2_free = [[halo_xn2, ("dsp0", 0)], [("vv", 0), ("dsp1", 0)]]
        xn_used_halo = tmm.now()
        xp_free = [[("vv", 0)], [("vv", 0)]]
        xc_free = [("tt", 0)] * 2
        scan_done = ("vv", 0)
        hl_free = [("dhs", 0), ("dhs", 0)]

        xnhb = [xnTh, xnTh2]
        xnlb = [xnTl, xnTl2]
        xn_used2 = [xn_used_halo, ("tt", 0)]
        xn_ready2 = [[("vv", 0)], [("vv", 0)]]
        cbuf_free = {"xcf": [("ss", 0)], "xcg": ("ss", 0), "g": ("vv", 0)}

        def gen_subA_p1(j, s, res):
            gsub = j * n_sub + s
            par = gsub % 2
            trow = j * T_tile + s * 128
            dma(sy, lambda e, trow=trow, par=par: e.dma_start(
                out=bufx[par][:], in_=dx[trow:trow + 128, :]),
                sxp[par], [x_free[par]])
            yield from gen_ffn_part1(bufx[par], 128, [sxp[par].now()],
                                     ZS[s % 2], RS[s % 2], res)

        def subA_p1(j, s):
            res = {}
            run_gen(gen_subA_p1(j, s, res))
            return res["qd"]

        def subA_p2a(j, s, r):
            return emit_ffn_part2a(bufx[(j * n_sub + s) % 2], 128, r["qd"],
                                   ZS[s % 2], RS[s % 2],
                                   qdone_f=r.get("qd_f"))

        def gen_subA_p2b(j, s, ed, res):
            gsub = j * n_sub + s
            par = gsub % 2
            trow = j * T_tile + s * 128
            r = {}
            yield from gen_ffn_part2b(bufx[par], bufx2[par], 128, ed,
                                      ZS[s % 2], x2_free[par], r)
            x2done = r["x2"]
            x_free[par] = x2done
            dma(g, lambda e, trow=trow, par=par: e.dma_start(
                out=dx2[trow:trow + 128, :], in_=bufx2[par][:]),
                sspp[par], [x2done])
            r2 = {}
            yield from gen_ln2_part1(bufx2[par], 128, [x2done],
                                     ZS[s % 2], RS[s % 2], r2)
            x2_free[par] = [r2["xn2"], sspp[par].now()]
            res["sd"] = r2["sd"]

        def subA_p2b(j, s, ed):
            res = {}
            run_gen(gen_subA_p2b(j, s, ed, res))
            return res["sd"]

        def subA_p3(j, s, sd):
            jp = j % 2
            fw = [xn_used2[jp]] if s == 0 else []
            emit_ln2_part2(128, s * 128, sd, xnhb[jp], xnlb[jp],
                           ZS[s % 2], RS[s % 2], first_wait=fw)
            if s == n_sub - 1:
                xn_ready2[jp] = [vev.now(), sev.now()]

        def emit_u_mm(centry):
            c, par, xch_done, xcl_done = centry
            for ui, (wb_, xc_) in enumerate(((wbh, xch), (wbh, xcl),
                                             (wbl, xch))):
                is_first_g = (c == 0 and ui == 0)
                is_last_g = (c == CC - 1 and ui == 2)
                wts = []
                if ui == 0:
                    wts = [xch_done, xcl_done]
                    if is_first_g:
                        wts.append(bank_free[7])
                t(lambda e, c=c, wb_=wb_, xc_=xc_, par=par,
                  is_first_g=is_first_g, is_last_g=is_last_g: e.matmul(
                      pu[:, 0:T_tile],
                      wb_[:, c * DS:(c + 1) * DS],
                      xc_[par][:, :],
                      start=is_first_g, stop=is_last_g),
                  waits=wts, incs=[tmm.inc()] if ui == 2 else [])
            xc_free[par] = tmm.now()

        def chunkA_stage1(j, c):
            jp = j % 2
            bank = 3 + (c % 4)
            par = c % 2
            mm_done = emit_win_chunk_mm(c, T_tile, bank,
                                        xn_ready2[jp] if c < 2 else [],
                                        xnhb[jp], xnlb[jp])
            halo_ld = ac(lambda e, c=c, par=par: e.activation(
                out=xp[par][:, 0:3], in_=halobuf[:, c * 3:(c + 1) * 3],
                func=AF.Copy),
                waits=list(xp_free[par]) + [hbuf_mark[c]])
            evx = ac(lambda e, par=par, bank=bank: e.activation(
                out=xp[par][:, 3:3 + T_tile], in_=pb[bank][:, 0:T_tile],
                func=AF.Copy),
                waits=[mm_done])
            bank_free[bank] = evx
            hbuf_mark[c] = ac(lambda e, c=c, par=par: e.activation(
                out=halobuf[:, c * 3:(c + 1) * 3],
                in_=xp[par][:, T_tile:T_tile + 3], func=AF.Copy),
                waits=[evx])
            return dict(evx=evx, halo_ld=halo_ld, par=par)

        def chunkA_stage2(j, c, st, pump=lambda: None):
            par = st["par"]
            par4 = c % 2
            xp_ready = [st["evx"], st["halo_ld"]]
            ct0 = vc(lambda e, c=c, par=par: e.tensor_scalar(
                out=xcf[:, :], in0=xp[par][:, 0:T_tile],
                scalar1=convw[:, 4 * c:4 * c + 1], scalar2=None,
                op0=ALU.mult),
                waits=list(xp_ready) + list(cbuf_free["xcf"]))
            pump()
            ct1 = vc(lambda e, c=c, par=par: e.scalar_tensor_tensor(
                out=xcg[:, :], in0=xp[par][:, 1:1 + T_tile],
                scalar=convw[:, 4 * c + 1:4 * c + 2], in1=xcf[:, :],
                op0=ALU.mult, op1=ALU.add),
                waits=[ct0, cbuf_free["xcg"]])
            pump()
            ct2 = vc(lambda e, c=c, par=par: e.scalar_tensor_tensor(
                out=xcf[:, :], in0=xp[par][:, 2:2 + T_tile],
                scalar=convw[:, 4 * c + 2:4 * c + 3], in1=xcg[:, :],
                op0=ALU.mult, op1=ALU.add),
                waits=[ct1])
            pump()
            conv_done = vc(lambda e, c=c, par=par: e.scalar_tensor_tensor(
                out=xcg[:, :], in0=xp[par][:, 3:3 + T_tile],
                scalar=convw[:, 4 * c + 3:4 * c + 4], in1=xcf[:, :],
                op0=ALU.mult, op1=ALU.add),
                waits=[ct2])
            xp_free[par] = [conv_done]
            sl = ac(lambda e, c=c: e.activation(
                out=xcf[:, :], in_=xcg[:, :], func=AF.Silu,
                bias=convb[:, c:c + 1]),
                waits=[conv_done])
            cbuf_free["xcg"] = sl
            pump()
            xch_done = ac(lambda e, par4=par4: e.activation(
                out=xch[par4][:, :], in_=xcf[:, :], func=AF.Copy),
                waits=[sl, xc_free[par4]])
            pump()
            v(lambda e, par4=par4: e.tensor_sub(
                xcl[par4][:, :], xcf[:, :], xch[par4][:, :]),
              waits=[xch_done], incs=[vev.inc()])
            xcl_done = vev.now()
            pump()
            cbuf_free["xcf"] = [xch_done, xcl_done]
            u_pend.append((c, par4, xch_done, xcl_done))

        if upto >= 2:
            u_pend = []
            subA_p1_q = {}
            subA_p2_sd = {}
            subA_all_first = True
            for s0 in range(0, n_sub, 2):
                if s0 + 1 < n_sub:
                    r0, r1 = {}, {}
                    duet(gen_subA_p1(0, s0, r0),
                         gen_subA_p1(0, s0 + 1, r1))
                    ed0 = subA_p2a(0, s0, r0)
                    ed1 = subA_p2a(0, s0 + 1, r1)
                    h0, h1 = {}, {}
                    duet(gen_subA_p2b(0, s0, ed0, h0),
                         gen_subA_p2b(0, s0 + 1, ed1, h1))
                    subA_p3(0, s0, h0["sd"])
                    subA_p3(0, s0 + 1, h1["sd"])
                else:
                    rz = {}
                    run_gen(gen_subA_p1(0, s0, rz))
                    ed = subA_p2a(0, s0, rz)
                    sd = subA_p2b(0, s0, ed)
                    subA_p3(0, s0, sd)
            if n_sub == 2:
                events = [(0, "p1", 0), (0, "p1", 1), (9, "p2a", 0),
                          (10, "p2a", 1), (10, "p2b", 0), (10, "p2b", 1),
                          (14, "p3", 0), (15, "p3", 1)]
            elif n_sub == 4:
                events = [(0, "p1", 0), (0, "p1", 1),
                          (4, "p2a", 0), (4, "p2b", 0),
                          (5, "p2a", 1), (5, "p2b", 1),
                          (8, "p3", 0), (8, "p1", 2),
                          (9, "p3", 1), (9, "p1", 3),
                          (11, "p2a", 2), (11, "p2b", 2),
                          (12, "p2a", 3), (12, "p2b", 3),
                          (14, "p3", 2), (14, "p3", 3)]
            else:
                events = []
                for s in range(n_sub):
                    base = (CC // n_sub) * s
                    events += [(base, "p1", s), (base + 1, "p2a", s),
                               (base + 2, "p2b", s), (base + 3, "p3", s)]
            active = []
            rr = [0]

            def pumpA():
                done = 0
                while active and done < 2:
                    i = rr[0] % len(active)
                    try:
                        next(active[i])
                        rr[0] = i + 1
                        done += 1
                    except StopIteration:
                        active.pop(i)

            def exhaust(gn):
                try:
                    while True:
                        next(gn)
                except StopIteration:
                    pass
                if gn in active:
                    active.remove(gn)

            st_g = {}
            st_res = {}
            st_ed = {}
            for j in range(n_tiles):
                jp = j % 2
                t0 = j * T_tile
                st1 = None
                for c in range(CC):
                    while len(u_pend) >= 2:
                        emit_u_mm(u_pend.pop(0))
                    if st1 is None:
                        st1 = chunkA_stage1(j, 0)
                    st1_next = (chunkA_stage1(j, c + 1)
                                if c + 1 < CC else None)
                    chunkA_stage2(j, c, st1, pumpA)
                    st1 = st1_next
                    if j + 1 < n_tiles:
                        for (cc_, stage, s) in events:
                            if cc_ != c:
                                continue
                            if stage == "p1":
                                st_res[s] = {}
                                st_g[s] = gen_subA_p1(j + 1, s, st_res[s])
                                active.append(st_g[s])
                            elif stage == "p2a":
                                exhaust(st_g[s])
                                st_ed[s] = subA_p2a(j + 1, s, st_res[s])
                            elif stage == "p2b":
                                st_res[s] = {}
                                st_g[s] = gen_subA_p2b(j + 1, s, st_ed[s],
                                                       st_res[s])
                                active.append(st_g[s])
                            elif stage == "p3":
                                exhaust(st_g[s])
                                subA_p3(j + 1, s, st_res[s]["sd"])
                while u_pend:
                    emit_u_mm(u_pend.pop(0))
                xn_used2[jp] = tmm.now()
                u_done = tmm.now()
                u_ev = vc(lambda e: e.tensor_scalar(out=u_t[:, :],
                                                    in0=pu[:, 0:T_tile],
                                                    scalar1=0.1, scalar2=None,
                                                    op0=ALU.mult),
                          waits=[u_done, scan_done])
                bank_free[7] = u_ev
                hlp = j % 2
                if j == 0:
                    scan_done = vc(lambda e: e.tensor_tensor_scan(
                        out=hl_tmp[0][:, :], data0=d09[:, :],
                        data1=u_t[:, :], initial=0.0,
                        op0=ALU.mult, op1=ALU.add),
                        waits=[u_ev])
                else:
                    scan_done = vc(lambda e, hlp=hlp: e.tensor_tensor_scan(
                        out=hl_tmp[hlp][:, :], data0=d09[:, :],
                        data1=u_t[:, :],
                        initial=hl_tmp[1 - hlp][:, T_tile - 1:T_tile],
                        op0=ALU.mult, op1=ALU.add),
                        waits=[u_ev, hl_free[hlp]])
                dma(g, lambda e, t0=t0, hlp=hlp: e.dma_start(
                    out=dhlc[:, t0:t0 + T_tile], in_=hl_tmp[hlp][:, :]),
                    shs, [scan_done])
                hl_free[hlp] = shs.now()


        a_last_mm = tmm.now()

        if upto >= 3:
            dma(g, lambda e: e.dma_start(out=dcci[:, :],
                                         in_=hl_tmp[(n_tiles - 1) % 2][
                                             :, T_tile - 1:T_tile]),
                scc, [scan_done])
            g(lambda e: e.collective_compute(
                "AllGather", ALU.bypass,
                replica_groups=[list(range(n_cores))],
                ins=[dcci.ap().opt()], outs=[dcco.ap().opt()]),
              waits=[scc.now()], incs=[gev.inc()])
            dma(g, lambda e: e.dma_start(out=gbuf[0:DS * n_cores, :],
                                         in_=dcco.ap()),
                scc, [gev.now()])
            gh1 = vc(lambda e: e.tensor_copy(out=ghl[:, 0:1], in_=gbuf[:, :]),
                     waits=[scc.now()])
            ghl_done = vc(lambda e: e.tensor_sub(ghl[:, 1:2], gbuf[:, :],
                                                 ghl[:, 0:1]),
                          waits=[gh1])
            t(lambda e: e.matmul(pb[0][0:DS, 0:1], selb[:, :], ghl[:, 0:1],
                                 start=True, stop=False),
              waits=[ghl_done, bank_free[0]])
            t(lambda e: e.matmul(pb[0][0:DS, 0:1], selb[:, :], ghl[:, 1:2],
                                 start=False, stop=True),
              incs=[tmm.inc()])
            v(lambda e: e.tensor_copy(out=hin[:, :], in_=pb[0][0:DS, 0:1]),
              waits=[tmm.now()], incs=[vev.inc()])
            bank_free[0] = vev.now()
            hin_done = vev.now()
            decay_used = ("vv", 0)

        if upto >= 4:
            load_split_big([(k * DI, DI,
                             dwin[k * 128:(k + 1) * 128, DI:2 * DI])
                            for k in range(KC)], wait_first=[a_last_mm],
                           stages=[xnt2f[:, 0:D], xnt2f[:, D:2 * D],
                                   xnt2g[:, 0:D], xnt2g[:, D:2 * D]])
            b_w_done = [sev.now(), vev.now()]
            hh_used = ("tt", 0)
            decay_used = scan_done
            spill_a_done = [sspp[0].now(), sspp[1].now()]
            gk_free = [xp_free[0][0], xp_free[1][0]]
            ykh_free = [("dyh0", 0), ("dyh1", 0)]
            ykl_free = [("dyl0", 0), ("dyl1", 0)]
            yk_used = [("ss", 0)]

            def gen_subB_p1(j, s, res):
                gsub = j * n_sub + s
                par = gsub % 2
                trow = j * T_tile + s * 128
                dma(sy, lambda e, trow=trow, par=par: e.dma_start(
                    out=bufx2[par][:], in_=dx2[trow:trow + 128, :]),
                    sxp[par], list(spill_a_done) + list(x2_free[par]))
                extra = []
                r2 = {}
                yield from gen_ln2_part1(
                    bufx2[par], 128, [sxp[par].now()] + list(extra),
                    ZS[s % 2], RS[s % 2], r2)
                x2_free[par] = [r2["xn2"]]
                res["sd"] = r2["sd"]

            def subB_p1(j, s):
                res = {}
                run_gen(gen_subB_p1(j, s, res))
                return res["sd"]

            def subB_p3(j, s, sd):
                jp = j % 2
                fw = [xn_used2[jp]] if s == 0 else []
                emit_ln2_part2(128, s * 128, sd, xnhb[jp], xnlb[jp],
                               ZS[s % 2], RS[s % 2], first_wait=fw)
                if s == n_sub - 1:
                    xn_ready2[jp] = [vev.now(), sev.now()]

            def chunkB(j, c, hh_done):
                jp = j % 2
                bank = 5 + (c % 2)
                ybank = 1 + (c % 2)
                par = c % 2
                t0 = j * T_tile
                zw = list(xn_ready2[jp]) if c < 2 else []
                if j == 0 and c < 2:
                    zw = zw + b_w_done
                z_done = emit_win_chunk_mm(c, T_tile, bank, zw,
                                           xnhb[jp], xnlb[jp])
                for ui, (wc_, hb_) in enumerate(((wch, hh_t), (wch, hl_t),
                                                 (wcl, hh_t))):
                    wts = []
                    if ui == 0:
                        wts = [bank_free[ybank]]
                        if c < 2:
                            wts.append(hh_done)
                    t(lambda e, c=c, wc_=wc_, hb_=hb_, ybank=ybank,
                      ui=ui: e.matmul(
                          pb[ybank][:, 0:T_tile],
                          wc_[0:DS, c * 128:(c + 1) * 128],
                          hb_[0:DS, 0:T_tile],
                          start=(ui == 0), stop=(ui == 2)),
                      waits=wts, incs=([tmm.inc()] if ui == 2 else []))
                yp_done = tmm.now()
                a(lambda e, bank=bank, par=par: e.activation(
                    out=gk[par][:, :], in_=pb[bank][:, 0:T_tile],
                    func=AF.Silu),
                  waits=[z_done, gk_free[par]], incs=[sev.inc()])
                bank_free[bank] = sev.now()
                g_done = sev.now()
                v(lambda e, ybank=ybank, par=par: e.tensor_mul(
                    yk[:, :], pb[ybank][:, 0:T_tile], gk[par][:, :]),
                  waits=[yp_done, g_done] + list(yk_used), incs=[vev.inc()])
                bank_free[ybank] = vev.now()
                gk_free[par] = vev.now()
                yk_done = vev.now()
                a(lambda e, par=par: e.activation(
                    out=ykh[par][:, :], in_=yk[:, :], func=AF.Copy),
                  waits=[yk_done, ykh_free[par]], incs=[sev.inc()])
                ykh_done = sev.now()
                v(lambda e, par=par: e.tensor_sub(
                    ykl[par][:, :], yk[:, :], ykh[par][:, :]),
                  waits=[ykh_done, ykl_free[par]], incs=[vev.inc()])
                ykl_done = vev.now()
                yk_used[:] = [sev.now(), ykl_done]
                gs0 = t0 // 128
                dma(g, lambda e, c=c, gs0=gs0, par=par: e.dma_start(
                    out=dysh[gs0:gs0 + n_sub, :, c, :].rearrange(
                        "b p n -> p b n"),
                    in_=ykh[par][:, :].rearrange("p (b n) -> p b n",
                                                 b=n_sub)),
                    syph[par], [ykh_done])
                ykh_free[par] = syph[par].now()
                dma(g, lambda e, c=c, gs0=gs0, par=par: e.dma_start(
                    out=dysl[gs0:gs0 + n_sub, :, c, :].rearrange(
                        "b p n -> p b n"),
                    in_=ykl[par][:, :].rearrange("p (b n) -> p b n",
                                                 b=n_sub)),
                    sypl[par], [ykl_done])
                ykl_free[par] = sypl[par].now()

            subB_sd = {}
            for s0 in range(0, n_sub, 2):
                if s0 + 1 < n_sub:
                    r0, r1 = {}, {}
                    duet(gen_subB_p1(0, s0, r0),
                         gen_subB_p1(0, s0 + 1, r1))
                    subB_p3(0, s0, r0["sd"])
                    subB_p3(0, s0 + 1, r1["sd"])
                else:
                    sd = subB_p1(0, s0)
                    subB_p3(0, s0, sd)
            if n_sub == 4:
                p1_events = {1: (0, 1), 7: (2, 3)}
                p3_sched = {4: 0, 5: 1, 11: 2, 12: 3}
            else:
                p1_events = {1: tuple(range(n_sub))}
                p3_sched = ({6: 0, 8: 1} if n_sub == 2 else {6: 0})
            hstage = hl_tmp[0]
            hstage_used = hl_free[0] if n_tiles % 2 == 0 else hl_free[1]
            for j in range(n_tiles):
                jp = j % 2
                t0 = j * T_tile
                dma(sy, lambda e, t0=t0: e.dma_start(
                    out=decay_t[:, :], in_=ddecay[:, t0:t0 + T_tile]),
                    sdec, [decay_used])
                dma(sy, lambda e, t0=t0: e.dma_start(
                    out=hstage[:, :], in_=dhlc[:, t0:t0 + T_tile]),
                    shb, [hstage_used, hl_free[0], hl_free[1]])
                decay_used = vc(lambda e, t0=t0: e.scalar_tensor_tensor(
                    out=htmp[:, :], in0=decay_t[:, :], scalar=hin[:, :],
                    in1=hstage[:, :], op0=ALU.mult, op1=ALU.add),
                    waits=[sdec.now(), shb.now(), hin_done, hh_used])
                hstage_used = decay_used
                hcp = vc(lambda e: e.tensor_copy(out=hh_t[:, :],
                                                 in_=htmp[:, :]),
                         waits=[decay_used])
                hh_done = vc(lambda e: e.tensor_sub(hl_t[:, :], htmp[:, :],
                                                    hh_t[:, :]),
                             waits=[hcp])
                for c in range(CC):
                    chunkB(j, c, hh_done)
                    if j + 1 < n_tiles:
                        if c in p1_events:
                            ss = p1_events[c]
                            ress = {s: {} for s in ss}
                            duet(*[gen_subB_p1(j + 1, s, ress[s])
                                   for s in ss])
                            for s in ss:
                                subB_sd[s] = ress[s]["sd"]
                        if c in p3_sched:
                            s = p3_sched[c]
                            subB_p3(j + 1, s, subB_sd[s])
                xn_used2[jp] = tmm.now()
                hh_used = tmm.now()

            b_last_mm = tmm.now()
            spill_b_done = [syph[0].now(), syph[1].now(),
                            sypl[0].now(), sypl[1].now()]

        if upto >= 5:
            xntf = xnTh[:].bitcast(F32)
            xntg = xnTl[:].bitcast(F32)
            load_split_big([(k * D, D, dwout[k * 128:(k + 1) * 128, :])
                            for k in range(CC)],
                           wait_first=[xn_used2[0]],
                           stages=[xntf[:, 0:D], xntf[:, D:2 * D],
                                   xntg[:, 0:D], xntg[:, D:2 * D]],
                           copy_first=[b_last_mm])
            c_w_done = [sev.now(), vev.now()]
            ybh = [xnTh[:, 0:CC * 128], ybh1[:, 0:CC * 128]]
            ybl = [xnTl[:, 0:CC * 128], ybl1[:, 0:CC * 128]]
            HC = CC // 2
            yb_free_f = [b_last_mm, ("vv", 0)]
            yb_free_b = [b_last_mm, ("vv", 0)]
            out_free = [("dot0", 0), ("dot1", 0)]

            def emitC_yb_loads(gsub):
                par = gsub % 2
                trow = gsub * 128
                gsi = trow // 128
                cw = c_w_done if gsub < 2 else []
                dma(sy, lambda e, gsi=gsi, par=par: e.dma_start(
                    out=ybh[par][:, 0:HC * 128],
                    in_=dysh[gsi, :, 0:HC, :].rearrange("p c n -> p (c n)")),
                    sybp[par], list(spill_b_done) + [yb_free_f[par]] + cw)
                dma(g, lambda e, gsi=gsi, par=par: e.dma_start(
                    out=ybl[par][:, 0:HC * 128],
                    in_=dysl[gsi, :, 0:HC, :].rearrange("p c n -> p (c n)")),
                    sybl2[par], list(spill_b_done) + [yb_free_f[par]] + cw)
                ywf = [sybp[par].now(), sybl2[par].now()]
                dma(sy, lambda e, gsi=gsi, par=par: e.dma_start(
                    out=ybh[par][:, HC * 128:CC * 128],
                    in_=dysh[gsi, :, HC:CC, :].rearrange("p c n -> p (c n)")),
                    sybp[par], [yb_free_b[par]] + cw)
                dma(g, lambda e, gsi=gsi, par=par: e.dma_start(
                    out=ybl[par][:, HC * 128:CC * 128],
                    in_=dysl[gsi, :, HC:CC, :].rearrange("p c n -> p (c n)")),
                    sybl2[par], [yb_free_b[par]] + cw)
                ywb = [sybp[par].now(), sybl2[par].now()]
                return dict(gsub=gsub, par=par, trow=trow, ywf=ywf, ywb=ywb)

            def emitC_x2load(pd):
                par = pd["par"]
                trow = pd["trow"]
                dma(sy, lambda e, trow=trow, par=par: e.dma_start(
                    out=bufx2[par][:], in_=dx2[trow:trow + 128, :]),
                    sxp[par], list(x2_free[par]))
                pd["x2w"] = sxp[par].now()

            def emitC_wout(pd, first_sub):
                gsub = pd["gsub"]
                par = pd["par"]
                ywf = pd["ywf"]
                ywb = pd["ywb"]
                bp = (1, 2) if gsub % 2 == 0 else (5, 6)
                started = [False, False]
                back_waited = False
                for c in range(CC):
                    groups = (((ybh, wxh), (ybh, wxl)), ((ybl, wxh),))
                    for grp in groups:
                        for (yb_, wv) in grp:
                            for half in range(2):
                                is_first = not started[half]
                                started[half] = True
                                is_last = (c == CC - 1 and yb_ is ybl)
                                is_fend = (c == HC - 1 and yb_ is ybl)
                                wts = []
                                if is_first:
                                    wts = list(ywf) + [bank_free[bp[half]]]
                                    if first_sub and half == 0:
                                        wts += c_w_done
                                if c == HC and not back_waited:
                                    wts = wts + list(ywb)
                                    back_waited = True
                                t(lambda e, c=c, yb_=yb_, wv=wv, par=par,
                                  half=half, bp=bp, is_first=is_first,
                                  is_last=is_last: e.matmul(
                                      pb[bp[half]][:, :],
                                      yb_[par][:, c * 128:(c + 1) * 128],
                                      wv[:, c * D + half * 512:
                                         c * D + half * 512 + 512],
                                      start=is_first, stop=is_last),
                                  waits=wts,
                                  incs=([tmm.inc()]
                                        if ((is_last or is_fend)
                                            and half == 1) else []))
                    if c == HC - 1:
                        yb_free_f[par] = tmm.now()
                wo_done = tmm.now()
                yb_free_b[par] = wo_done
                pd["wo_done"] = wo_done
                pd["bp"] = bp

            def gen_x3_p1(pd):
                par = pd["par"]
                bp = pd["bp"]
                for half in range(2):
                    v(lambda e, half=half, par=par, bp=bp: e.tensor_add(
                        bufx[par][:, half * 512:half * 512 + 512],
                        pb[bp[half]][:, :],
                        bufx2[par][:, half * 512:half * 512 + 512]),
                      waits=([pd["wo_done"], pd["x2w"], x_free[par]]
                             if half == 0 else []),
                      incs=[vev.inc()])
                    bank_free[bp[half]] = vev.now()
                    if half == 0:
                        yield
                x3_done = vev.now()
                x2_free[par] = [x3_done]
                yield
                r = {}
                yield from gen_ffn_part1(bufx[par], 128, [x3_done],
                                         ZS[par], RS[par], r)
                pd["qd"] = r["qd"]
                pd["qd_f"] = r.get("qd_f")

            def emitC_part2a(pd):
                pd["ed"] = emit_ffn_part2a(bufx[pd["par"]], 128, pd["qd"],
                                           ZS[pd["par"]], RS[pd["par"]],
                                           b_q=0, b_h=(3, 4),
                                           qdone_f=pd.get("qd_f"))

            def gen_part2b_store(pd):
                par = pd["par"]
                trow = pd["trow"]
                r = {}
                yield from gen_ffn_part2b(bufx[par], bufx[par], 128,
                                          pd["ed"], ZS[par], (), r)
                x4_done = r["x2"]
                dma_nc(sy, lambda e, trow=trow, par=par: e.dma_start(
                    out=dout[trow:trow + 128, :], in_=bufx[par][:]),
                    sotp[par], [x4_done])
                x_free[par] = sotp[par].now()

            prev = None
            first_sub = True
            for gp in range(T_core // 256):
                if prev is not None:
                    duet(gen_x3_p1(prev[0]), gen_x3_p1(prev[1]))
                pds = []
                for gsub in (2 * gp, 2 * gp + 1):
                    pds.append(emitC_yb_loads(gsub))
                for pd in pds:
                    emitC_x2load(pd)
                for pd in pds:
                    emitC_wout(pd, first_sub)
                    first_sub = False
                if prev is not None:
                    emitC_part2a(prev[0])
                    emitC_part2a(prev[1])
                    duet(gen_part2b_store(prev[0]), gen_part2b_store(prev[1]))
                prev = pds
            duet(gen_x3_p1(prev[0]), gen_x3_p1(prev[1]))
            emitC_part2a(prev[0])
            emitC_part2a(prev[1])
            duet(gen_part2b_store(prev[0]), gen_part2b_store(prev[1]))

        if upto >= 5:
            g(None, waits=[sotp[0].now(), sotp[1].now()])
        else:
            g(None, waits=[("vv", P.streams["vv"].n), ("ss", P.streams["ss"].n),
                           ("tt", P.streams["tt"].n), ("gg", P.streams["gg"].n),
                           ("dsp0", P.streams["dsp0"].n), ("dsp1", P.streams["dsp1"].n),
                           ("dyh0", P.streams["dyh0"].n), ("dyh1", P.streams["dyh1"].n),
                           ("dyl0", P.streams["dyl0"].n), ("dyl1", P.streams["dyl1"].n),
                           ("dcc", P.streams["dcc"].n)])

        with nc.Block() as block:
            @block.sync
            def _(eng):
                P.run("sync", eng, sems)

            @block.tensor
            def _(eng):
                P.run("tensor", eng, sems)

            @block.vector
            def _(eng):
                P.run("vector", eng, sems)

            @block.scalar
            def _(eng):
                P.run("scalar", eng, sems)

            @block.gpsimd
            def _(eng):
                P.run("gpsimd", eng, sems)

    return nc


def make_core_inputs(x_flat, n_cores, T_core, weights):
    (w_in, conv_w, conv_b, w_B, w_C, w_out, wq, wsc, bias) = weights
    halves_per_seq = 2
    wq_t = np.ascontiguousarray(wq.T)
    conv_w_p = np.ascontiguousarray(
        conv_w[:, 0, :].reshape(CC, 128, 4).transpose(1, 0, 2).reshape(128, CC * 4))
    conv_b_p = np.ascontiguousarray(conv_b.reshape(CC, 128).T)
    wsb = np.ascontiguousarray(np.broadcast_to(wsc[None, :], (128, D)))
    bias_b = np.ascontiguousarray(np.broadcast_to(bias[None, :], (128, D)))
    identf = np.eye(128, dtype=np.float32)
    identb = np.eye(128, dtype=ml_dtypes.bfloat16)
    tt = np.arange(1, T_core + 1, dtype=np.float64)
    decay = np.broadcast_to((0.9 ** tt).astype(np.float32)[None, :],
                            (DS, T_core)).copy()
    in_maps = []
    for i in range(n_cores):
        r0 = i * T_core
        xc = np.ascontiguousarray(x_flat[r0:r0 + T_core])
        is_mid = (i % halves_per_seq) != 0
        if is_mid:
            xhalo = np.ascontiguousarray(x_flat[r0 - 3:r0])
            hm = np.ones((128, 3), dtype=np.float32)
            sel = np.zeros((128, DS), dtype=ml_dtypes.bfloat16)
            prev = i - 1
            for q in range(DS):
                sel[prev * DS + q, q] = 1.0
        else:
            xhalo = np.zeros((3, D), dtype=np.float32)
            hm = np.zeros((128, 3), dtype=np.float32)
            sel = np.zeros((128, DS), dtype=ml_dtypes.bfloat16)
        in_maps.append({
            "x": xc, "xhalo": xhalo, "w_in": w_in,
            "conv_w_p": conv_w_p, "conv_b_p": conv_b_p,
            "w_B": w_B, "w_C": w_C, "w_out": w_out, "wq_t": wq_t,
            "w_scale_b": wsb, "bias_b": bias_b, "halo_mask": hm,
            "sel_b": sel, "decay": decay, "ident_f": identf,
            "ident_b": identb,
        })
    return in_maps


def kernel(L0_attn_x, L0_attn_w_in, L0_attn_conv_w, L0_attn_conv_b,
           L0_attn_w_B, L0_attn_w_C, L0_attn_w_out, L0_ffn_weight_q,
           L0_ffn_weight_scale, L0_ffn_bias):
    n_cores = 8
    T_core = 4096
    T_tile = 512
    B, S, _ = L0_attn_x.shape
    x_flat = np.ascontiguousarray(
        np.asarray(L0_attn_x, dtype=np.float32).reshape(B * S, D))
    weights = (
        np.ascontiguousarray(np.asarray(L0_attn_w_in, dtype=np.float32)),
        np.ascontiguousarray(np.asarray(L0_attn_conv_w, dtype=np.float32)),
        np.ascontiguousarray(np.asarray(L0_attn_conv_b, dtype=np.float32)),
        np.ascontiguousarray(np.asarray(L0_attn_w_B, dtype=np.float32)),
        np.ascontiguousarray(np.asarray(L0_attn_w_C, dtype=np.float32)),
        np.ascontiguousarray(np.asarray(L0_attn_w_out, dtype=np.float32)),
        np.ascontiguousarray(np.asarray(L0_ffn_weight_q, dtype=np.int8)),
        np.ascontiguousarray(np.asarray(L0_ffn_weight_scale, dtype=np.float32)),
        np.ascontiguousarray(np.asarray(L0_ffn_bias, dtype=np.float32)),
    )
    nc = build_program(n_cores, T_core, T_tile)
    in_maps = make_core_inputs(x_flat, n_cores, T_core, weights)
    res = run_bass_kernel_spmd(nc, in_maps, core_ids=list(range(n_cores)))
    out = np.concatenate([np.asarray(r["out"]) for r in res.results], axis=0)
    return out.reshape(B, S, D).astype(np.float32)


# version 50 (final)
# speedup vs baseline: 1.0343x; 1.0110x over previous
import numpy as np
import ml_dtypes
import concourse.bass as bass
import concourse.mybir as mybir
from concourse.bass_utils import run_bass_kernel_spmd
from contextlib import ExitStack

F32 = mybir.dt.float32
BF16 = mybir.dt.bfloat16
I8 = mybir.dt.int8
I32 = mybir.dt.int32
AF = mybir.ActivationFunctionType
ALU = mybir.AluOpType
AXX = mybir.AxisListType.X

D = 1024
DI = 2048
DS = 16
KC = D // 128
CC = DI // 128
EPS_LN = 1e-5
EPS_Q = 1e-6


class Stream:
    def __init__(self, name, step=1):
        self.name = name
        self.step = step
        self.n = 0

    def inc(self, k=None):
        k = self.step if k is None else k
        self.n += k
        return (self.name, k)

    def now(self):
        return (self.name, self.n)


class Plan:
    def __init__(self):
        self.items = []
        self.streams = {}

    def S(self, name, step=1):
        if name not in self.streams:
            self.streams[name] = Stream(name, step)
        return self.streams[name]

    def emit(self, eng, f, waits=(), incs=()):
        self.items.append((eng, tuple(waits), tuple(incs), f))

    def drain(self, eng):
        self.items.append((eng, (), (), "drain"))

    def run(self, engname, engobj, sems):
        for (eng, waits, incs, f) in self.items:
            if eng != engname:
                continue
            if f == "drain":
                engobj.drain()
                continue
            for (s, vv) in waits:
                if vv > 0:
                    engobj.wait_ge(sems[s], vv)
            if f is None:
                continue
            inst = f(engobj)
            for (s, k) in incs:
                inst.then_inc(sems[s], k)


def build_program(n_cores, T_core, T_tile, upto=5):
    assert T_core % T_tile == 0 and T_tile % 128 == 0
    n_tiles = T_core // T_tile
    n_sub = T_tile // 128
    nc = bass.Bass("TRN2", target_bir_lowering=False)

    dx = nc.dram_tensor("x", [T_core, D], F32, kind="ExternalInput").ap()
    dxh = nc.dram_tensor("xhalo", [3, D], F32, kind="ExternalInput").ap()
    dwin = nc.dram_tensor("w_in", [D, 2 * DI], F32, kind="ExternalInput").ap()
    dconvw = nc.dram_tensor("conv_w_p", [128, CC * 4], F32, kind="ExternalInput").ap()
    dconvb = nc.dram_tensor("conv_b_p", [128, CC], F32, kind="ExternalInput").ap()
    dwB = nc.dram_tensor("w_B", [DI, DS], F32, kind="ExternalInput").ap()
    dwC = nc.dram_tensor("w_C", [DS, DI], F32, kind="ExternalInput").ap()
    dwout = nc.dram_tensor("w_out", [DI, D], F32, kind="ExternalInput").ap()
    dwq = nc.dram_tensor("wq_t", [D, D], I8, kind="ExternalInput").ap()
    dwsb = nc.dram_tensor("w_scale_b", [128, D], F32, kind="ExternalInput").ap()
    dbb = nc.dram_tensor("bias_b", [128, D], F32, kind="ExternalInput").ap()
    dhm = nc.dram_tensor("halo_mask", [128, 3], F32, kind="ExternalInput").ap()
    dsel = nc.dram_tensor("sel_b", [128, DS], BF16, kind="ExternalInput").ap()
    ddecay = nc.dram_tensor("decay", [DS, T_core], F32, kind="ExternalInput").ap()
    didf = nc.dram_tensor("ident_f", [128, 128], F32, kind="ExternalInput").ap()
    didb = nc.dram_tensor("ident_b", [128, 128], BF16, kind="ExternalInput").ap()
    dout = nc.dram_tensor("out", [T_core, D], F32, kind="ExternalOutput").ap()
    dx2 = nc.dram_tensor("x2_sp", [T_core, D], F32).ap()
    dysh = nc.dram_tensor("ysp_h", [T_core // 128, 128, CC, 128],
                          BF16).ap()
    dysl = nc.dram_tensor("ysp_l", [T_core // 128, 128, CC, 128],
                          BF16).ap()
    dcci = nc.dram_tensor("cc_in", [DS, 1], F32)
    dcco = nc.dram_tensor("cc_out", [DS * n_cores, 1], F32)
    dhlc = nc.dram_tensor("hloc_sp", [DS, T_core], F32).ap()

    P = Plan()
    es = ExitStack()

    def sbt(name, shape, dt):
        return es.enter_context(nc.sbuf_tensor(name, shape, dt))

    def pst(name, shape, dt):
        return es.enter_context(nc.psum_tensor(name, shape, dt))

    with es:
        wq_b = sbt("wq_b", [128, KC * D], BF16)
        wsb = sbt("wsb", [128, D], F32)
        biasb = sbt("biasb", [128, D], F32)
        wxh = sbt("wxh", [128, KC * DI], BF16)
        wxl = sbt("wxl", [128, KC * DI], BF16)
        wbh = sbt("wbh", [128, CC * DS], BF16)
        wbl = sbt("wbl", [128, CC * DS], BF16)
        wch = sbt("wch", [DS, DI], BF16)
        wcl = sbt("wcl", [DS, DI], BF16)
        convw = sbt("convw", [128, CC * 4], F32)
        convb = sbt("convb", [128, CC], F32)
        selb = sbt("selb", [128, DS], BF16)
        identb = sbt("identb", [128, 128], BF16)
        halom = sbt("halom", [128, 3], F32)
        halobuf = sbt("halobuf", [128, CC * 3], F32)
        epsln = sbt("epsln", [128, 1], F32)
        u_t = sbt("u_t", [DS, T_tile], F32)
        d09 = sbt("d09", [DS, T_tile], F32)
        hl_tmp = [sbt(f"hl_tmp{i}", [DS, T_tile], F32) for i in range(2)]
        hh_t = sbt("hh_t", [DS, T_tile], BF16)
        hl_t = sbt("hl_t", [DS, T_tile], BF16)
        decay_t = d09
        htmp = u_t
        hin = sbt("hin", [DS, 1], F32)
        gbuf = sbt("gbuf", [128, 1], F32)
        ghl = sbt("ghl", [128, 2], BF16)

        bufx = [sbt(f"bufx{i}", [128, D], F32) for i in range(2)]
        bufx2 = [sbt(f"bufx2{i}", [128, D], F32) for i in range(2)]
        sqS = sbt("sqS", [128, D], F32)
        sq_guard = [("ss", 0)]
        ZS = []
        for zi in range(2):
            ZS.append(dict(
                scrA=sbt(f"scrA{zi}", [128, D], F32),
                scrB=sbt(f"scrB{zi}", [128, D], F32),
                scrGE=sbt(f"scrGE{zi}", [128, D], BF16),
                scrLE=sbt(f"scrLE{zi}", [128, D], BF16),
                qbf=sbt(f"qbf{zi}", [128, D], BF16),
                xqT=sbt(f"xqT{zi}", [128, KC * 128], BF16),
            ))
        xnTh = sbt("xnTh", [128, KC * T_tile], BF16)
        xnTl = sbt("xnTl", [128, KC * T_tile], BF16)
        xnTh2 = sbt("xnTh2", [128, KC * T_tile], BF16)
        xnTl2 = sbt("xnTl2", [128, KC * T_tile], BF16)
        ybh1 = xnTh2
        ybl1 = xnTl2
        wstage = ZS[0]["scrB"]
        wstage2 = ZS[1]["scrB"]
        wqi8 = ZS[0]["scrGE"][:, 0:D // 2].bitcast(I8)
        wcstage = wstage[0:DS, 0:D]
        wbstage = wstage[:, 0:CC * DS]
        xp = [sbt(f"xp{i}", [128, T_tile + 3], F32) for i in range(2)]
        xcf = sbt("xcf", [128, T_tile], F32)
        xcg = sbt("xcg", [128, T_tile], F32)
        xch = [sbt(f"xch{i}", [128, T_tile], BF16) for i in range(2)]
        xcl = [sbt(f"xcl{i}", [128, T_tile], BF16) for i in range(2)]
        gk = [xp[0][:, 0:T_tile], xp[1][:, 0:T_tile]]
        yk = xcf
        xcg_b = xcg[:].bitcast(BF16)
        ykh = [xcg_b[:, 0:T_tile], xch[0][:]]
        ykl = [xcg_b[:, T_tile:2 * T_tile], xcl[0][:]]
        for zi in range(2):
            ZS[zi]["stats"] = sbt(f"stats{zi}", [128, 16], F32)
            ZS[zi]["vtail"] = ("vv", 0)
            ZS[zi]["stail"] = ("ss", 0)
            ZS[zi]["sq_done"] = ("ss", 0)
        pb = [pst(f"pb{i}", [128, 512], F32) for i in range(8)]
        pu = pb[7][0:DS, :]

        sem_names = ["dhs", "dhb", "dw2",
                     "dw", "dcc", "ddc", "dhl",
                     "dxa0", "dxa1", "dsp0", "dsp1",
                     "dyh0", "dyh1", "dyl0", "dyl1",
                     "dyb0", "dyb1", "dot0", "dot1",
                     "vv", "ss", "tt", "gg"]
        sems = {n: es.enter_context(nc.semaphore(n)) for n in sem_names}

        sw = P.S("dw", 16)
        sw2 = P.S("dw2", 16)
        shs = P.S("dhs", 16)
        shb = P.S("dhb", 16)
        scc = P.S("dcc", 16)
        sdec = P.S("ddc", 16)
        shl = P.S("dhl", 16)
        sxp = [P.S("dxa0", 16), P.S("dxa1", 16)]
        sspp = [P.S("dsp0", 16), P.S("dsp1", 16)]
        syph = [P.S("dyh0", 16), P.S("dyh1", 16)]
        sypl = [P.S("dyl0", 16), P.S("dyl1", 16)]
        sybp = [P.S("dyb0", 16), P.S("dyb1", 16)]
        sybl2 = [P.S("dyl0", 16), P.S("dyl1", 16)]
        sotp = [P.S("dot0", 16), P.S("dot1", 16)]
        vev = P.S("vv")
        sev = P.S("ss")
        tmm = P.S("tt")
        gev = P.S("gg")

        def v(f, waits=(), incs=()):
            P.emit("vector", f, waits, incs)

        def a(f, waits=(), incs=()):
            P.emit("scalar", f, waits, incs)

        def t(f, waits=(), incs=()):
            P.emit("tensor", f, waits, incs)

        def g(f, waits=(), incs=()):
            P.emit("gpsimd", f, waits, incs)

        def sy(f, waits=(), incs=()):
            P.emit("sync", f, waits, incs)

        def dma(emitfn, f, stream, waits=()):
            emitfn(f, waits=[stream.now()] + list(waits), incs=[stream.inc()])

        def dma_nc(emitfn, f, stream, waits=()):
            emitfn(f, waits=list(waits), incs=[stream.inc()])

        v(lambda e: e.memset(epsln[:], EPS_LN), incs=[vev.inc()])
        v(lambda e: e.memset(d09[:], 0.9), incs=[vev.inc()])
        v(lambda e: e.memset(halobuf[:], 0.0), incs=[vev.inc()])
        v(lambda e: e.memset(gbuf[:], 0.0), incs=[vev.inc()])
        P.drain("vector")

        for (dst, src) in ((convw, dconvw), (convb, dconvb), (wsb, dwsb),
                           (biasb, dbb), (selb, dsel),
                           (identb, didb), (halom, dhm)):
            dma(sy, lambda e, dst=dst, src=src: e.dma_start(out=dst[:],
                                                            in_=src[:]), sw)
        dma(sy, lambda e: e.dma_start(
            out=wbstage.rearrange("p (c s) -> p c s", c=CC),
            in_=dwB[:].rearrange("(c p) s -> p c s", p=128)), sw)
        a(lambda e: e.activation(out=wbh[:], in_=wbstage, func=AF.Copy),
          waits=[sw.now()], incs=[sev.inc()])
        v(lambda e: e.tensor_sub(wbl[:], wbstage, wbh[:]),
          waits=[sev.now()], incs=[vev.inc()])
        for h in range(2):
            dma(sy, lambda e, h=h: e.dma_start(
                out=wcstage, in_=dwC[:, h * D:(h + 1) * D]), sw,
                [sev.now(), vev.now()])
            a(lambda e, h=h: e.activation(out=wch[:, h * D:(h + 1) * D],
                                          in_=wcstage, func=AF.Copy),
              waits=[sw.now()], incs=[sev.inc()])
            v(lambda e, h=h: e.tensor_sub(wcl[:, h * D:(h + 1) * D], wcstage,
                                          wch[:, h * D:(h + 1) * D]),
              waits=[sev.now()], incs=[vev.inc()])
        for k in range(KC):
            wts = [sev.now()] if k > 0 else []
            dma(sy, lambda e, k=k: e.dma_start(
                out=wqi8[:], in_=dwq[k * 128:(k + 1) * 128, :]), sw, wts)
            a(lambda e, k=k: e.activation(
                out=wq_b[:, k * D:(k + 1) * D], in_=wqi8[:], func=AF.Copy),
              waits=[sw.now()], incs=[sev.inc()])

        def load_split_big(chunks, wait_first=(), stages=None,
                           copy_first=()):
            if stages is None:
                stages = [wstage[:, 0:D], wstage2[:, 0:D],
                          ZS[0]["scrA"][:, 0:D], ZS[1]["scrA"][:, 0:D]]
            marks = [list(wait_first) for _ in stages]
            i = 0
            pieces = []
            for (off0, width0, src0) in chunks:
                nh = (width0 + D - 1) // D
                for h in range(nh):
                    pieces.append((off0 + h * D,
                                   min(D, width0 - h * D),
                                   src0[:, h * D:h * D + min(D, width0 - h * D)]))
            for (off, width, src) in pieces:
                stg = stages[i % len(stages)]
                swk = sw if i % 2 == 0 else sw2
                dma(sy, lambda e, off=off, width=width, src=src,
                    stg=stg: e.dma_start(
                    out=stg[:, :width] if False else stg[0:128, 0:width],
                    in_=src), swk, marks[i % len(stages)])
                a(lambda e, off=off, width=width, stg=stg: e.activation(
                    out=wxh[:, off:off + width], in_=stg[0:128, 0:width],
                    func=AF.Copy),
                  waits=[swk.now()] + (list(copy_first)
                                       if i < len(stages) else []),
                  incs=[sev.inc()])
                v(lambda e, off=off, width=width, stg=stg: e.tensor_sub(
                    wxl[:, off:off + width], stg[0:128, 0:width],
                    wxh[:, off:off + width]),
                  waits=[sev.now()], incs=[vev.inc()])
                marks[i % len(stages)] = [sev.now(), vev.now()]
                i += 1

        prep_done = [sev.now(), vev.now(), sw.now()]
        xnt2f = xnTh2[:].bitcast(F32)
        xnt2g = xnTl2[:].bitcast(F32)
        load_split_big([(k * DI, DI, dwin[k * 128:(k + 1) * 128, 0:DI])
                        for k in range(KC)],
                       wait_first=[sev.now(), vev.now()],
                       stages=[xnt2f[:, 0:D], xnt2f[:, D:2 * D],
                               xnt2g[:, 0:D], xnt2g[:, D:2 * D]])

        def vc(f, waits=()):
            v(f, waits=waits, incs=[vev.inc()])
            return vev.now()

        def ac(f, waits=()):
            a(f, waits=waits, incs=[sev.inc()])
            return sev.now()

        def run_gen(gn):
            for _ in gn:
                pass

        def duet(*gens_in):
            gens = list(gens_in)
            while gens:
                nxt = []
                for gn in gens:
                    try:
                        next(gn)
                        nxt.append(gn)
                    except StopIteration:
                        pass
                gens = nxt

        def gen_ln(xv, ntok, need_quant, waits0, Z, res):
            scrA = Z["scrA"]; stats = Z["stats"]
            w0 = list(waits0) + [Z["vtail"], Z["stail"]]
            sq_done = ac(lambda e: e.activation(out=sqS[0:ntok, :], in_=xv,
                                                func=AF.Square,
                                                accum_out=stats[0:ntok, 1:2]),
                         waits=w0 + [sq_guard[0]])
            sq_guard[0] = sq_done
            e1 = vc(lambda e: e.tensor_reduce(out=stats[0:ntok, 0:1], in_=xv,
                                              axis=AXX, op=ALU.add),
                    waits=w0)
            yield
            e2 = vc(lambda e: e.tensor_scalar(
                out=stats[0:ntok, 2:3], in0=stats[0:ntok, 0:1],
                scalar1=-1.0 / D, scalar2=None, op0=ALU.mult), waits=[e1])
            yield
            e3 = vc(lambda e: e.tensor_mul(stats[0:ntok, 3:4],
                                           stats[0:ntok, 2:3],
                                           stats[0:ntok, 2:3]), waits=[e2])
            yield
            e4 = vc(lambda e: e.tensor_scalar(
                out=stats[0:ntok, 4:5], in0=stats[0:ntok, 1:2],
                scalar1=1.0 / D, scalar2=stats[0:ntok, 3:4],
                op0=ALU.mult, op1=ALU.subtract), waits=[e3, sq_done])
            sq2 = ac(lambda e: e.activation(out=stats[0:ntok, 5:6],
                                            in_=stats[0:ntok, 4:5],
                                            func=AF.Sqrt,
                                            bias=epsln[0:ntok, :]),
                     waits=[e4])
            yield
            if need_quant:
                e7 = vc(lambda e: e.tensor_scalar(
                    out=scrA[0:ntok, :], in0=xv, scalar1=stats[0:ntok, 2:3],
                    scalar2=None, op0=ALU.add), waits=[e2])
                yield
            e6 = vc(lambda e: e.reciprocal(out=stats[0:ntok, 6:7],
                                           in_=stats[0:ntok, 5:6]),
                    waits=[sq2])
            yield
            if not need_quant:
                Z["vtail"] = e6
                Z["stail"] = sq2
                res["ln"] = e6
                return
            e8 = vc(lambda e: e.tensor_reduce(out=stats[0:ntok, 7:8],
                                              in_=scrA[0:ntok, :], axis=AXX,
                                              op=ALU.max,
                                              apply_absolute_value=True),
                    waits=[e7])
            yield
            e9 = vc(lambda e: e.tensor_mul(stats[0:ntok, 8:9],
                                           stats[0:ntok, 7:8],
                                           stats[0:ntok, 6:7]),
                    waits=[e8, e6])
            yield
            e10 = vc(lambda e: e.tensor_scalar(
                out=stats[0:ntok, 9:10], in0=stats[0:ntok, 8:9],
                scalar1=EPS_Q, scalar2=1.0 / 127.0,
                op0=ALU.max, op1=ALU.mult), waits=[e9])
            yield
            e13 = vc(lambda e: e.tensor_scalar(
                out=stats[0:ntok, 13:14], in0=stats[0:ntok, 9:10],
                scalar1=0.5, scalar2=None, op0=ALU.mult), waits=[e10])
            yield
            e11 = vc(lambda e: e.reciprocal(out=stats[0:ntok, 11:12],
                                            in_=stats[0:ntok, 9:10]),
                     waits=[e10])
            yield
            e12 = vc(lambda e: e.tensor_mul(stats[0:ntok, 12:13],
                                            stats[0:ntok, 6:7],
                                            stats[0:ntok, 11:12]),
                     waits=[e11])
            Z["vtail"] = e12
            Z["stail"] = sq2
            res["ln"] = e12

        def emit_ln(xv, ntok, need_quant, waits0, Z):
            res = {}
            run_gen(gen_ln(xv, ntok, need_quant, waits0, Z, res))
            return res["ln"]

        def gen_quant(ntok, Z, R, lnq, res):
            scrA = Z["scrA"]; scrB = Z["scrB"]
            scrGE = Z["scrGE"]; scrLE = Z["scrLE"]
            qbf = Z["qbf"]; stats = Z["stats"]
            q1 = ac(lambda e: e.activation(
                out=scrB[0:ntok, :], in_=scrA[0:ntok, :], func=AF.Copy,
                scale=stats[0:ntok, 12:13]), waits=[lnq])
            q2 = vc(lambda e: e.tensor_copy(out=scrA[0:ntok, :].bitcast(I32),
                                            in_=scrB[0:ntok, :]),
                    waits=[q1])
            yield
            q3 = vc(lambda e: e.tensor_copy(out=qbf[0:ntok, :],
                                            in_=scrA[0:ntok, :].bitcast(I32)),
                    waits=[q2, R["qbf"]])
            yield
            q4 = vc(lambda e: e.tensor_sub(scrB[0:ntok, :], scrB[0:ntok, :],
                                           qbf[0:ntok, :]), waits=[q3])
            yield
            qGE = vc(lambda e: e.tensor_scalar(
                out=scrGE[0:ntok, :], in0=scrB[0:ntok, :], scalar1=0.5,
                scalar2=None, op0=ALU.is_ge), waits=[q4, R["xn2"]])
            yield
            qLE = vc(lambda e: e.tensor_scalar(
                out=scrLE[0:ntok, :], in0=scrB[0:ntok, :], scalar1=-0.5,
                scalar2=None, op0=ALU.is_le), waits=[q4])
            yield
            q7 = vc(lambda e: e.tensor_sub(scrB[0:ntok, :], scrGE[0:ntok, :],
                                           scrLE[0:ntok, :]),
                    waits=[qGE, qLE])
            yield
            q8 = vc(lambda e: e.tensor_add(qbf[0:ntok, :], qbf[0:ntok, :],
                                           scrB[0:ntok, :]),
                    waits=[q7])
            Z["vtail"] = q8
            res["qd"] = q8

        bank_free = {i: ("vv", 0) for i in range(8)}
        RS = [{"xn2": ("tt", 0), "qbf": ("tt", 0), "xqT": ("tt", 0)}
              for _ in range(2)]

        def gen_ffn_part1(xbuf, ntok, x_wait, Z, R, res):
            r = {}
            yield from gen_ln(xbuf[0:ntok, :], ntok, True, x_wait, Z, r)
            yield from gen_quant(ntok, Z, R, r["ln"], res)

        def emit_ffn_part1(xbuf, ntok, x_wait, Z, R):
            res = {}
            run_gen(gen_ffn_part1(xbuf, ntok, x_wait, Z, R, res))
            return res["qd"]

        def emit_ffn_part2a(xbuf, ntok, qdone, Z, R, b_q=0, b_h=(1, 2)):
            qbf = Z["qbf"]; xqT = Z["xqT"]; stats = Z["stats"]
            scrB = Z["scrB"]
            pbq = pb[b_q][:].bitcast(BF16)
            stq = ntok if ntok % 2 == 0 else ntok + 1
            for c in range(KC):
                wts = [qdone, bank_free[b_q]] if c == 0 else []
                t(lambda e, c=c, ntok=ntok, stq=stq, pbq=pbq,
                  qbf=qbf: e.matmul(
                    pbq[:, c * stq:c * stq + ntok],
                    qbf[0:ntok, c * 128:(c + 1) * 128],
                    identb[0:ntok, 0:ntok],
                    is_transpose=True, start=(c == 0), stop=(c == KC - 1)),
                  waits=wts, incs=([tmm.inc()] if c == KC - 1 else []))
            trq_done = tmm.now()
            R["qbf"] = trq_done
            if stq == ntok:
                a(lambda e, ntok=ntok, pbq=pbq, xqT=xqT: e.activation(
                    out=xqT[:, 0:KC * ntok], in_=pbq[:, 0:KC * ntok],
                    func=AF.Copy),
                  waits=[trq_done, R["xqT"]], incs=[sev.inc()])
            else:
                a(lambda e, ntok=ntok, stq=stq, pbq=pbq,
                  xqT=xqT: e.activation(
                    out=xqT[:, 0:KC * ntok].rearrange(
                        "p (c q) -> p c q", q=ntok),
                    in_=pbq[:, 0:KC * stq].rearrange(
                        "p (c q) -> p c q", q=stq)[:, :, 0:ntok],
                    func=AF.Copy),
                  waits=[trq_done, R["xqT"]], incs=[sev.inc()])
            bank_free[b_q] = sev.now()
            xqt_done = sev.now()
            for k in range(KC):
                for half in range(2):
                    wts = []
                    if k == 0:
                        wts = [bank_free[b_h[half]]]
                        if half == 0:
                            wts.append(xqt_done)
                    t(lambda e, half=half, k=k, ntok=ntok, xqT=xqT: e.matmul(
                        pb[b_h[half]][0:ntok, :],
                        xqT[:, k * ntok:(k + 1) * ntok],
                        wq_b[:, k * D + half * 512: k * D + half * 512 + 512],
                        start=(k == 0), stop=(k == KC - 1)),
                      waits=wts, incs=([tmm.inc()]
                                       if (k == KC - 1 and half == 1) else []))
            ffn_done = tmm.now()
            R["xqT"] = ffn_done
            for half in range(2):
                a(lambda e, half=half, ntok=ntok, scrB=scrB,
                  stats=stats: e.activation(
                    out=scrB[0:ntok, half * 512:half * 512 + 512],
                    in_=pb[b_h[half]][0:ntok, :], func=AF.Copy,
                    scale=stats[0:ntok, 13:14]),
                  waits=([ffn_done] if half == 0 else []), incs=[sev.inc()])
                bank_free[b_h[half]] = sev.now()
            Z["stail"] = sev.now()
            return sev.now()

        def gen_ffn_part2b(xbuf, x2buf, ntok, evac_done, Z, x2_wait, res):
            scrB = Z["scrB"]
            p1 = vc(lambda e, ntok=ntok, scrB=scrB: e.tensor_mul(
                scrB[0:ntok, :], scrB[0:ntok, :], wsb[0:ntok, :]),
                waits=[evac_done, Z["vtail"]])
            yield
            p2 = vc(lambda e, ntok=ntok, scrB=scrB: e.tensor_add(
                scrB[0:ntok, :], scrB[0:ntok, :], xbuf[0:ntok, :]),
                waits=[p1])
            yield
            p3 = vc(lambda e, ntok=ntok, scrB=scrB: e.scalar_tensor_tensor(
                out=x2buf[0:ntok, :], in0=biasb[0:ntok, :], scalar=0.5,
                in1=scrB[0:ntok, :], op0=ALU.mult, op1=ALU.add),
                waits=[p2] + list(x2_wait))
            Z["vtail"] = p3
            res["x2"] = p3

        def emit_ffn_part2b(xbuf, x2buf, ntok, evac_done, Z, x2_wait=()):
            res = {}
            run_gen(gen_ffn_part2b(xbuf, x2buf, ntok, evac_done, Z,
                                   x2_wait, res))
            return res["x2"]

        def emit_ffn_part2(xbuf, x2buf, ntok, qdone, Z, R, b_q=0,
                           b_h=(1, 2), x2_wait=()):
            ed = emit_ffn_part2a(xbuf, ntok, qdone, Z, R, b_q, b_h)
            return emit_ffn_part2b(xbuf, x2buf, ntok, ed, Z, x2_wait)

        def emit_ffn_block(xbuf, x2buf, ntok, x_wait, Z, R, b_q=0,
                           b_h=(1, 2), x2_wait=()):
            qdone = emit_ffn_part1(xbuf, ntok, x_wait, Z, R)
            return emit_ffn_part2(xbuf, x2buf, ntok, qdone, Z, R, b_q, b_h,
                                  x2_wait)

        def gen_ln2_part1(x2buf, ntok, x2_wait, Z, R, res):
            scrA = Z["scrA"]; scrGE = Z["scrGE"]; scrLE = Z["scrLE"]
            stats = Z["stats"]
            r = {}
            yield from gen_ln(x2buf[0:ntok, :], ntok, False, x2_wait, Z, r)
            xn2_done = vc(
                lambda e, ntok=ntok, scrA=scrA, stats=stats: e.tensor_scalar(
                    out=scrA[0:ntok, :], in0=x2buf[0:ntok, :],
                    scalar1=stats[0:ntok, 2:3], scalar2=stats[0:ntok, 6:7],
                    op0=ALU.add, op1=ALU.mult),
                waits=[r["ln"]])
            yield
            res["xn2"] = xn2_done
            sp1 = ac(lambda e, ntok=ntok, scrGE=scrGE, scrA=scrA: e.activation(
                out=scrGE[0:ntok, :], in_=scrA[0:ntok, :], func=AF.Copy),
                waits=[xn2_done, R["xn2"]])
            sp2 = vc(lambda e, ntok=ntok, scrLE=scrLE, scrA=scrA,
                     scrGE=scrGE: e.tensor_sub(scrLE[0:ntok, :],
                                               scrA[0:ntok, :],
                                               scrGE[0:ntok, :]),
                     waits=[sp1])
            Z["vtail"] = sp2
            res["sd"] = sp2

        def emit_ln2_part1(x2buf, ntok, x2_wait, Z, R):
            res = {}
            run_gen(gen_ln2_part1(x2buf, ntok, x2_wait, Z, R, res))
            return res["xn2"], res["sd"]

        def emit_ln2_part2(ntok, col_off, split_done, xnh, xnl, Z, R,
                           banks=(3, 4), first_wait=()):
            scrGE = Z["scrGE"]; scrLE = Z["scrLE"]
            """bf16 transposes of scrGE/scrLE into xnh/xnl columns."""
            stq = ntok if ntok % 2 == 0 else ntok + 1
            for grp, (srcb, dstb) in enumerate(((scrGE, xnh), (scrLE, xnl))):
                bank = banks[grp]
                pbv = pb[bank][:].bitcast(BF16)
                for c in range(KC):
                    wts = ([split_done, bank_free[bank]] + list(first_wait)) \
                        if c == 0 else []
                    t(lambda e, c=c, ntok=ntok, stq=stq, pbv=pbv,
                      srcb=srcb: e.matmul(
                        pbv[:, c * stq:c * stq + ntok],
                        srcb[0:ntok, c * 128:(c + 1) * 128],
                        identb[0:ntok, 0:ntok],
                        is_transpose=True, start=(c == 0), stop=(c == KC - 1)),
                      waits=wts, incs=([tmm.inc()] if c == KC - 1 else []))
                tr_done = tmm.now()
                dst_ap = dstb[:].rearrange(
                    "p (c q) -> p c q", q=T_tile)[:, :,
                                                  col_off:col_off + ntok]
                src_ap = pbv[:, 0:KC * stq].rearrange(
                    "p (c q) -> p c q", q=stq)[:, :, 0:ntok]
                a(lambda e, dst_ap=dst_ap, src_ap=src_ap: e.activation(
                    out=dst_ap, in_=src_ap, func=AF.Copy),
                  waits=[tr_done], incs=[sev.inc()])
                bank_free[bank] = sev.now()
            R["xn2"] = tmm.now()

        def emit_ln2_transpose(x2buf, ntok, col_off, x2_wait, Z, R,
                               banks=(3, 4), xnh=None, xnl=None,
                               first_wait=()):
            if xnh is None:
                xnh, xnl = xnTh, xnTl
            xn2_done, split_done = emit_ln2_part1(x2buf, ntok, x2_wait, Z, R)
            emit_ln2_part2(ntok, col_off, split_done, xnh, xnl, Z, R, banks,
                           first_wait)
            return xn2_done


        def emit_win_chunk_mm(cchunk, ntok, bank, ready_wait,
                              xnh=None, xnl=None):
            if xnh is None:
                xnh, xnl = xnTh, xnTl
            n0 = cchunk * 128
            first = True
            for k in range(KC):
                for (wbuf, xbuf_) in ((wxh, xnh), (wxh, xnl), (wxl, xnh)):
                    is_first = first
                    first = False
                    is_last = (k == KC - 1 and wbuf is wxl)
                    wts = (list(ready_wait) + [bank_free[bank]]) \
                        if is_first else []
                    t(lambda e, k=k, wbuf=wbuf, xbuf_=xbuf_, n0=n0,
                      is_first=is_first, is_last=is_last, ntok=ntok: e.matmul(
                          pb[bank][:, 0:ntok],
                          wbuf[:, k * DI + n0:k * DI + n0 + 128],
                          xbuf_[:, k * T_tile:k * T_tile + ntok],
                          start=is_first, stop=is_last),
                      waits=wts, incs=([tmm.inc()] if is_last else []))
            return tmm.now()

        if upto >= 1:
            dma(sy, lambda e: e.dma_start(out=bufx[0][0:3, :], in_=dxh[:]),
                sxp[0])
            halo_wait = [sxp[0].now()] + prep_done
            halo_x2 = emit_ffn_block(bufx[0], bufx2[0], 3, halo_wait,
                                     ZS[0], RS[0])
            halo_xn2 = emit_ln2_transpose(bufx2[0], 3, 0, [halo_x2],
                                          ZS[0], RS[0])
            halo_xn = [vev.now(), sev.now()]
            hbuf_mark = [None] * CC
            for c in range(CC):
                bank = 5 + (c % 2)
                mm_done = emit_win_chunk_mm(c, 3, bank, halo_xn if c < 2 else [])
                hbuf_mark[c] = vc(lambda e, c=c, bank=bank: e.tensor_mul(
                    halobuf[:, c * 3:(c + 1) * 3], pb[bank][:, 0:3], halom[:]),
                    waits=[mm_done])
                bank_free[bank] = hbuf_mark[c]
        else:
            halo_x2 = ("vv", 0)
            halo_xn2 = ("vv", 0)
            hbuf_mark = [("vv", 0)] * CC
        x_free = [halo_x2, ("vv", 0)]
        x2_free = [[halo_xn2, ("dsp0", 0)], [("vv", 0), ("dsp1", 0)]]
        xn_used_halo = tmm.now()
        xp_free = [[("vv", 0)], [("vv", 0)]]
        xc_free = [("tt", 0)] * 2
        scan_done = ("vv", 0)
        hl_free = [("dhs", 0), ("dhs", 0)]

        xnhb = [xnTh, xnTh2]
        xnlb = [xnTl, xnTl2]
        xn_used2 = [xn_used_halo, ("tt", 0)]
        xn_ready2 = [[("vv", 0)], [("vv", 0)]]
        cbuf_free = {"xcf": [("ss", 0)], "xcg": ("ss", 0), "g": ("vv", 0)}

        def gen_subA_p1(j, s, res):
            gsub = j * n_sub + s
            par = gsub % 2
            trow = j * T_tile + s * 128
            dma(sy, lambda e, trow=trow, par=par: e.dma_start(
                out=bufx[par][:], in_=dx[trow:trow + 128, :]),
                sxp[par], [x_free[par]])
            yield from gen_ffn_part1(bufx[par], 128, [sxp[par].now()],
                                     ZS[s % 2], RS[s % 2], res)

        def subA_p1(j, s):
            res = {}
            run_gen(gen_subA_p1(j, s, res))
            return res["qd"]

        def subA_p2a(j, s, qd):
            return emit_ffn_part2a(bufx[(j * n_sub + s) % 2], 128, qd,
                                   ZS[s % 2], RS[s % 2])

        def gen_subA_p2b(j, s, ed, res):
            gsub = j * n_sub + s
            par = gsub % 2
            trow = j * T_tile + s * 128
            r = {}
            yield from gen_ffn_part2b(bufx[par], bufx2[par], 128, ed,
                                      ZS[s % 2], x2_free[par], r)
            x2done = r["x2"]
            x_free[par] = x2done
            dma(g, lambda e, trow=trow, par=par: e.dma_start(
                out=dx2[trow:trow + 128, :], in_=bufx2[par][:]),
                sspp[par], [x2done])
            r2 = {}
            yield from gen_ln2_part1(bufx2[par], 128, [x2done],
                                     ZS[s % 2], RS[s % 2], r2)
            x2_free[par] = [r2["xn2"], sspp[par].now()]
            res["sd"] = r2["sd"]

        def subA_p2b(j, s, ed):
            res = {}
            run_gen(gen_subA_p2b(j, s, ed, res))
            return res["sd"]

        def subA_p3(j, s, sd):
            jp = j % 2
            fw = [xn_used2[jp]] if s == 0 else []
            emit_ln2_part2(128, s * 128, sd, xnhb[jp], xnlb[jp],
                           ZS[s % 2], RS[s % 2], first_wait=fw)
            if s == n_sub - 1:
                xn_ready2[jp] = [vev.now(), sev.now()]

        def emit_u_mm(centry):
            c, par, xch_done, xcl_done = centry
            for ui, (wb_, xc_) in enumerate(((wbh, xch), (wbh, xcl),
                                             (wbl, xch))):
                is_first_g = (c == 0 and ui == 0)
                is_last_g = (c == CC - 1 and ui == 2)
                wts = []
                if ui == 0:
                    wts = [xch_done, xcl_done]
                    if is_first_g:
                        wts.append(bank_free[7])
                t(lambda e, c=c, wb_=wb_, xc_=xc_, par=par,
                  is_first_g=is_first_g, is_last_g=is_last_g: e.matmul(
                      pu[:, 0:T_tile],
                      wb_[:, c * DS:(c + 1) * DS],
                      xc_[par][:, :],
                      start=is_first_g, stop=is_last_g),
                  waits=wts, incs=[tmm.inc()] if ui == 2 else [])
            xc_free[par] = tmm.now()

        def chunkA_stage1(j, c):
            jp = j % 2
            bank = 3 + (c % 4)
            par = c % 2
            mm_done = emit_win_chunk_mm(c, T_tile, bank,
                                        xn_ready2[jp] if c < 2 else [],
                                        xnhb[jp], xnlb[jp])
            halo_ld = ac(lambda e, c=c, par=par: e.activation(
                out=xp[par][:, 0:3], in_=halobuf[:, c * 3:(c + 1) * 3],
                func=AF.Copy),
                waits=list(xp_free[par]) + [hbuf_mark[c]])
            evx = ac(lambda e, par=par, bank=bank: e.activation(
                out=xp[par][:, 3:3 + T_tile], in_=pb[bank][:, 0:T_tile],
                func=AF.Copy),
                waits=[mm_done])
            bank_free[bank] = evx
            hbuf_mark[c] = ac(lambda e, c=c, par=par: e.activation(
                out=halobuf[:, c * 3:(c + 1) * 3],
                in_=xp[par][:, T_tile:T_tile + 3], func=AF.Copy),
                waits=[evx])
            return dict(evx=evx, halo_ld=halo_ld, par=par)

        def chunkA_stage2(j, c, st, pump=lambda: None):
            par = st["par"]
            par4 = c % 2
            xp_ready = [st["evx"], st["halo_ld"]]
            ct0 = vc(lambda e, c=c, par=par: e.tensor_scalar(
                out=xcf[:, :], in0=xp[par][:, 0:T_tile],
                scalar1=convw[:, 4 * c:4 * c + 1], scalar2=None,
                op0=ALU.mult),
                waits=list(xp_ready) + list(cbuf_free["xcf"]))
            pump()
            ct1 = vc(lambda e, c=c, par=par: e.scalar_tensor_tensor(
                out=xcg[:, :], in0=xp[par][:, 1:1 + T_tile],
                scalar=convw[:, 4 * c + 1:4 * c + 2], in1=xcf[:, :],
                op0=ALU.mult, op1=ALU.add),
                waits=[ct0, cbuf_free["xcg"]])
            pump()
            ct2 = vc(lambda e, c=c, par=par: e.scalar_tensor_tensor(
                out=xcf[:, :], in0=xp[par][:, 2:2 + T_tile],
                scalar=convw[:, 4 * c + 2:4 * c + 3], in1=xcg[:, :],
                op0=ALU.mult, op1=ALU.add),
                waits=[ct1])
            pump()
            conv_done = vc(lambda e, c=c, par=par: e.scalar_tensor_tensor(
                out=xcg[:, :], in0=xp[par][:, 3:3 + T_tile],
                scalar=convw[:, 4 * c + 3:4 * c + 4], in1=xcf[:, :],
                op0=ALU.mult, op1=ALU.add),
                waits=[ct2])
            xp_free[par] = [conv_done]
            sl = ac(lambda e, c=c: e.activation(
                out=xcf[:, :], in_=xcg[:, :], func=AF.Silu,
                bias=convb[:, c:c + 1]),
                waits=[conv_done])
            cbuf_free["xcg"] = sl
            pump()
            xch_done = ac(lambda e, par4=par4: e.activation(
                out=xch[par4][:, :], in_=xcf[:, :], func=AF.Copy),
                waits=[sl, xc_free[par4]])
            pump()
            v(lambda e, par4=par4: e.tensor_sub(
                xcl[par4][:, :], xcf[:, :], xch[par4][:, :]),
              waits=[xch_done], incs=[vev.inc()])
            xcl_done = vev.now()
            pump()
            cbuf_free["xcf"] = [xch_done, xcl_done]
            u_pend.append((c, par4, xch_done, xcl_done))

        if upto >= 2:
            u_pend = []
            subA_p1_q = {}
            subA_p2_sd = {}
            subA_all_first = True
            for s0 in range(0, n_sub, 2):
                if s0 + 1 < n_sub:
                    r0, r1 = {}, {}
                    duet(gen_subA_p1(0, s0, r0),
                         gen_subA_p1(0, s0 + 1, r1))
                    ed0 = subA_p2a(0, s0, r0["qd"])
                    ed1 = subA_p2a(0, s0 + 1, r1["qd"])
                    h0, h1 = {}, {}
                    duet(gen_subA_p2b(0, s0, ed0, h0),
                         gen_subA_p2b(0, s0 + 1, ed1, h1))
                    subA_p3(0, s0, h0["sd"])
                    subA_p3(0, s0 + 1, h1["sd"])
                else:
                    qd = subA_p1(0, s0)
                    ed = subA_p2a(0, s0, qd)
                    sd = subA_p2b(0, s0, ed)
                    subA_p3(0, s0, sd)
            if n_sub == 2:
                events = [(0, "p1", 0), (0, "p1", 1), (9, "p2a", 0),
                          (10, "p2a", 1), (10, "p2b", 0), (10, "p2b", 1),
                          (14, "p3", 0), (15, "p3", 1)]
            elif n_sub == 4:
                events = [(0, "p1", 0), (0, "p1", 1),
                          (4, "p2a", 0), (4, "p2b", 0),
                          (5, "p2a", 1), (5, "p2b", 1),
                          (8, "p3", 0), (8, "p1", 2),
                          (9, "p3", 1), (9, "p1", 3),
                          (11, "p2a", 2), (11, "p2b", 2),
                          (12, "p2a", 3), (12, "p2b", 3),
                          (14, "p3", 2), (14, "p3", 3)]
            else:
                events = []
                for s in range(n_sub):
                    base = (CC // n_sub) * s
                    events += [(base, "p1", s), (base + 1, "p2a", s),
                               (base + 2, "p2b", s), (base + 3, "p3", s)]
            active = []
            rr = [0]

            def pumpA():
                done = 0
                while active and done < 2:
                    i = rr[0] % len(active)
                    try:
                        next(active[i])
                        rr[0] = i + 1
                        done += 1
                    except StopIteration:
                        active.pop(i)

            def exhaust(gn):
                try:
                    while True:
                        next(gn)
                except StopIteration:
                    pass
                if gn in active:
                    active.remove(gn)

            st_g = {}
            st_res = {}
            st_ed = {}
            for j in range(n_tiles):
                jp = j % 2
                t0 = j * T_tile
                st1 = None
                for c in range(CC):
                    if st1 is None:
                        st1 = chunkA_stage1(j, 0)
                    st1_next = (chunkA_stage1(j, c + 1)
                                if c + 1 < CC else None)
                    while len(u_pend) >= 2:
                        emit_u_mm(u_pend.pop(0))
                    chunkA_stage2(j, c, st1, pumpA)
                    st1 = st1_next
                    if j + 1 < n_tiles:
                        for (cc_, stage, s) in events:
                            if cc_ != c:
                                continue
                            if stage == "p1":
                                st_res[s] = {}
                                st_g[s] = gen_subA_p1(j + 1, s, st_res[s])
                                active.append(st_g[s])
                            elif stage == "p2a":
                                exhaust(st_g[s])
                                st_ed[s] = subA_p2a(j + 1, s,
                                                    st_res[s]["qd"])
                            elif stage == "p2b":
                                st_res[s] = {}
                                st_g[s] = gen_subA_p2b(j + 1, s, st_ed[s],
                                                       st_res[s])
                                active.append(st_g[s])
                            elif stage == "p3":
                                exhaust(st_g[s])
                                subA_p3(j + 1, s, st_res[s]["sd"])
                while u_pend:
                    emit_u_mm(u_pend.pop(0))
                xn_used2[jp] = tmm.now()
                u_done = tmm.now()
                u_ev = vc(lambda e: e.tensor_scalar(out=u_t[:, :],
                                                    in0=pu[:, 0:T_tile],
                                                    scalar1=0.1, scalar2=None,
                                                    op0=ALU.mult),
                          waits=[u_done, scan_done])
                bank_free[7] = u_ev
                hlp = j % 2
                if j == 0:
                    scan_done = vc(lambda e: e.tensor_tensor_scan(
                        out=hl_tmp[0][:, :], data0=d09[:, :],
                        data1=u_t[:, :], initial=0.0,
                        op0=ALU.mult, op1=ALU.add),
                        waits=[u_ev])
                else:
                    scan_done = vc(lambda e, hlp=hlp: e.tensor_tensor_scan(
                        out=hl_tmp[hlp][:, :], data0=d09[:, :],
                        data1=u_t[:, :],
                        initial=hl_tmp[1 - hlp][:, T_tile - 1:T_tile],
                        op0=ALU.mult, op1=ALU.add),
                        waits=[u_ev, hl_free[hlp]])
                dma(g, lambda e, t0=t0, hlp=hlp: e.dma_start(
                    out=dhlc[:, t0:t0 + T_tile], in_=hl_tmp[hlp][:, :]),
                    shs, [scan_done])
                hl_free[hlp] = shs.now()


        a_last_mm = tmm.now()

        if upto >= 3:
            dma(g, lambda e: e.dma_start(out=dcci[:, :],
                                         in_=hl_tmp[(n_tiles - 1) % 2][
                                             :, T_tile - 1:T_tile]),
                scc, [scan_done])
            g(lambda e: e.collective_compute(
                "AllGather", ALU.bypass,
                replica_groups=[list(range(n_cores))],
                ins=[dcci.ap().opt()], outs=[dcco.ap().opt()]),
              waits=[scc.now()], incs=[gev.inc()])
            dma(g, lambda e: e.dma_start(out=gbuf[0:DS * n_cores, :],
                                         in_=dcco.ap()),
                scc, [gev.now()])
            gh1 = vc(lambda e: e.tensor_copy(out=ghl[:, 0:1], in_=gbuf[:, :]),
                     waits=[scc.now()])
            ghl_done = vc(lambda e: e.tensor_sub(ghl[:, 1:2], gbuf[:, :],
                                                 ghl[:, 0:1]),
                          waits=[gh1])
            t(lambda e: e.matmul(pb[0][0:DS, 0:1], selb[:, :], ghl[:, 0:1],
                                 start=True, stop=False),
              waits=[ghl_done, bank_free[0]])
            t(lambda e: e.matmul(pb[0][0:DS, 0:1], selb[:, :], ghl[:, 1:2],
                                 start=False, stop=True),
              incs=[tmm.inc()])
            v(lambda e: e.tensor_copy(out=hin[:, :], in_=pb[0][0:DS, 0:1]),
              waits=[tmm.now()], incs=[vev.inc()])
            bank_free[0] = vev.now()
            hin_done = vev.now()
            decay_used = ("vv", 0)

        if upto >= 4:
            load_split_big([(k * DI, DI,
                             dwin[k * 128:(k + 1) * 128, DI:2 * DI])
                            for k in range(KC)], wait_first=[a_last_mm],
                           stages=[xnt2f[:, 0:D], xnt2f[:, D:2 * D],
                                   xnt2g[:, 0:D], xnt2g[:, D:2 * D]])
            b_w_done = [sev.now(), vev.now()]
            hh_used = ("tt", 0)
            decay_used = scan_done
            spill_a_done = [sspp[0].now(), sspp[1].now()]
            gk_free = [xp_free[0][0], xp_free[1][0]]
            ykh_free = [("dyh0", 0), ("dyh1", 0)]
            ykl_free = [("dyl0", 0), ("dyl1", 0)]
            yk_used = [("ss", 0)]

            def gen_subB_p1(j, s, res):
                gsub = j * n_sub + s
                par = gsub % 2
                trow = j * T_tile + s * 128
                dma(sy, lambda e, trow=trow, par=par: e.dma_start(
                    out=bufx2[par][:], in_=dx2[trow:trow + 128, :]),
                    sxp[par], list(spill_a_done) + list(x2_free[par]))
                extra = []
                r2 = {}
                yield from gen_ln2_part1(
                    bufx2[par], 128, [sxp[par].now()] + list(extra),
                    ZS[s % 2], RS[s % 2], r2)
                x2_free[par] = [r2["xn2"]]
                res["sd"] = r2["sd"]

            def subB_p1(j, s):
                res = {}
                run_gen(gen_subB_p1(j, s, res))
                return res["sd"]

            def subB_p3(j, s, sd):
                jp = j % 2
                fw = [xn_used2[jp]] if s == 0 else []
                emit_ln2_part2(128, s * 128, sd, xnhb[jp], xnlb[jp],
                               ZS[s % 2], RS[s % 2], first_wait=fw)
                if s == n_sub - 1:
                    xn_ready2[jp] = [vev.now(), sev.now()]

            def chunkB(j, c, hh_done):
                jp = j % 2
                bank = 5 + (c % 2)
                ybank = 1 + (c % 2)
                par = c % 2
                t0 = j * T_tile
                zw = list(xn_ready2[jp]) if c < 2 else []
                if j == 0 and c < 2:
                    zw = zw + b_w_done
                z_done = emit_win_chunk_mm(c, T_tile, bank, zw,
                                           xnhb[jp], xnlb[jp])
                for ui, (wc_, hb_) in enumerate(((wch, hh_t), (wch, hl_t),
                                                 (wcl, hh_t))):
                    wts = []
                    if ui == 0:
                        wts = [bank_free[ybank]]
                        if c < 2:
                            wts.append(hh_done)
                    t(lambda e, c=c, wc_=wc_, hb_=hb_, ybank=ybank,
                      ui=ui: e.matmul(
                          pb[ybank][:, 0:T_tile],
                          wc_[0:DS, c * 128:(c + 1) * 128],
                          hb_[0:DS, 0:T_tile],
                          start=(ui == 0), stop=(ui == 2)),
                      waits=wts, incs=([tmm.inc()] if ui == 2 else []))
                yp_done = tmm.now()
                a(lambda e, bank=bank, par=par: e.activation(
                    out=gk[par][:, :], in_=pb[bank][:, 0:T_tile],
                    func=AF.Silu),
                  waits=[z_done, gk_free[par]], incs=[sev.inc()])
                bank_free[bank] = sev.now()
                g_done = sev.now()
                v(lambda e, ybank=ybank, par=par: e.tensor_mul(
                    yk[:, :], pb[ybank][:, 0:T_tile], gk[par][:, :]),
                  waits=[yp_done, g_done] + list(yk_used), incs=[vev.inc()])
                bank_free[ybank] = vev.now()
                gk_free[par] = vev.now()
                yk_done = vev.now()
                a(lambda e, par=par: e.activation(
                    out=ykh[par][:, :], in_=yk[:, :], func=AF.Copy),
                  waits=[yk_done, ykh_free[par]], incs=[sev.inc()])
                ykh_done = sev.now()
                v(lambda e, par=par: e.tensor_sub(
                    ykl[par][:, :], yk[:, :], ykh[par][:, :]),
                  waits=[ykh_done, ykl_free[par]], incs=[vev.inc()])
                ykl_done = vev.now()
                yk_used[:] = [sev.now(), ykl_done]
                gs0 = t0 // 128
                dma(g, lambda e, c=c, gs0=gs0, par=par: e.dma_start(
                    out=dysh[gs0:gs0 + n_sub, :, c, :].rearrange(
                        "b p n -> p b n"),
                    in_=ykh[par][:, :].rearrange("p (b n) -> p b n",
                                                 b=n_sub)),
                    syph[par], [ykh_done])
                ykh_free[par] = syph[par].now()
                dma(g, lambda e, c=c, gs0=gs0, par=par: e.dma_start(
                    out=dysl[gs0:gs0 + n_sub, :, c, :].rearrange(
                        "b p n -> p b n"),
                    in_=ykl[par][:, :].rearrange("p (b n) -> p b n",
                                                 b=n_sub)),
                    sypl[par], [ykl_done])
                ykl_free[par] = sypl[par].now()

            subB_sd = {}
            for s0 in range(0, n_sub, 2):
                if s0 + 1 < n_sub:
                    r0, r1 = {}, {}
                    duet(gen_subB_p1(0, s0, r0),
                         gen_subB_p1(0, s0 + 1, r1))
                    subB_p3(0, s0, r0["sd"])
                    subB_p3(0, s0 + 1, r1["sd"])
                else:
                    sd = subB_p1(0, s0)
                    subB_p3(0, s0, sd)
            if n_sub == 4:
                p1_events = {1: (0, 1), 7: (2, 3)}
                p3_sched = {4: 0, 5: 1, 11: 2, 12: 3}
            else:
                p1_events = {1: tuple(range(n_sub))}
                p3_sched = ({6: 0, 8: 1} if n_sub == 2 else {6: 0})
            hstage = hl_tmp[0]
            hstage_used = hl_free[0] if n_tiles % 2 == 0 else hl_free[1]
            for j in range(n_tiles):
                jp = j % 2
                t0 = j * T_tile
                dma(sy, lambda e, t0=t0: e.dma_start(
                    out=decay_t[:, :], in_=ddecay[:, t0:t0 + T_tile]),
                    sdec, [decay_used])
                dma(sy, lambda e, t0=t0: e.dma_start(
                    out=hstage[:, :], in_=dhlc[:, t0:t0 + T_tile]),
                    shb, [hstage_used, hl_free[0], hl_free[1]])
                decay_used = vc(lambda e, t0=t0: e.scalar_tensor_tensor(
                    out=htmp[:, :], in0=decay_t[:, :], scalar=hin[:, :],
                    in1=hstage[:, :], op0=ALU.mult, op1=ALU.add),
                    waits=[sdec.now(), shb.now(), hin_done, hh_used])
                hstage_used = decay_used
                hcp = vc(lambda e: e.tensor_copy(out=hh_t[:, :],
                                                 in_=htmp[:, :]),
                         waits=[decay_used])
                hh_done = vc(lambda e: e.tensor_sub(hl_t[:, :], htmp[:, :],
                                                    hh_t[:, :]),
                             waits=[hcp])
                for c in range(CC):
                    chunkB(j, c, hh_done)
                    if j + 1 < n_tiles:
                        if c in p1_events:
                            ss = p1_events[c]
                            ress = {s: {} for s in ss}
                            duet(*[gen_subB_p1(j + 1, s, ress[s])
                                   for s in ss])
                            for s in ss:
                                subB_sd[s] = ress[s]["sd"]
                        if c in p3_sched:
                            s = p3_sched[c]
                            subB_p3(j + 1, s, subB_sd[s])
                xn_used2[jp] = tmm.now()
                hh_used = tmm.now()

            b_last_mm = tmm.now()
            spill_b_done = [syph[0].now(), syph[1].now(),
                            sypl[0].now(), sypl[1].now()]

        if upto >= 5:
            xntf = xnTh[:].bitcast(F32)
            xntg = xnTl[:].bitcast(F32)
            load_split_big([(k * D, D, dwout[k * 128:(k + 1) * 128, :])
                            for k in range(CC)],
                           wait_first=[xn_used2[0]],
                           stages=[xntf[:, 0:D], xntf[:, D:2 * D],
                                   xntg[:, 0:D], xntg[:, D:2 * D]],
                           copy_first=[b_last_mm])
            c_w_done = [sev.now(), vev.now()]
            ybh = [xnTh[:, 0:CC * 128], ybh1[:, 0:CC * 128]]
            ybl = [xnTl[:, 0:CC * 128], ybl1[:, 0:CC * 128]]
            HC = CC // 2
            yb_free_f = [b_last_mm, ("vv", 0)]
            yb_free_b = [b_last_mm, ("vv", 0)]
            out_free = [("dot0", 0), ("dot1", 0)]

            def emitC_yb_loads(gsub):
                par = gsub % 2
                trow = gsub * 128
                gsi = trow // 128
                cw = c_w_done if gsub < 2 else []
                dma(sy, lambda e, gsi=gsi, par=par: e.dma_start(
                    out=ybh[par][:, 0:HC * 128],
                    in_=dysh[gsi, :, 0:HC, :].rearrange("p c n -> p (c n)")),
                    sybp[par], list(spill_b_done) + [yb_free_f[par]] + cw)
                dma(g, lambda e, gsi=gsi, par=par: e.dma_start(
                    out=ybl[par][:, 0:HC * 128],
                    in_=dysl[gsi, :, 0:HC, :].rearrange("p c n -> p (c n)")),
                    sybl2[par], list(spill_b_done) + [yb_free_f[par]] + cw)
                ywf = [sybp[par].now(), sybl2[par].now()]
                dma(sy, lambda e, gsi=gsi, par=par: e.dma_start(
                    out=ybh[par][:, HC * 128:CC * 128],
                    in_=dysh[gsi, :, HC:CC, :].rearrange("p c n -> p (c n)")),
                    sybp[par], [yb_free_b[par]] + cw)
                dma(g, lambda e, gsi=gsi, par=par: e.dma_start(
                    out=ybl[par][:, HC * 128:CC * 128],
                    in_=dysl[gsi, :, HC:CC, :].rearrange("p c n -> p (c n)")),
                    sybl2[par], [yb_free_b[par]] + cw)
                ywb = [sybp[par].now(), sybl2[par].now()]
                return dict(gsub=gsub, par=par, trow=trow, ywf=ywf, ywb=ywb)

            def emitC_x2load(pd):
                par = pd["par"]
                trow = pd["trow"]
                dma(sy, lambda e, trow=trow, par=par: e.dma_start(
                    out=bufx2[par][:], in_=dx2[trow:trow + 128, :]),
                    sxp[par], list(x2_free[par]))
                pd["x2w"] = sxp[par].now()

            def emitC_wout(pd, first_sub):
                gsub = pd["gsub"]
                par = pd["par"]
                ywf = pd["ywf"]
                ywb = pd["ywb"]
                bp = (1, 2) if gsub % 2 == 0 else (5, 6)
                started = [False, False]
                back_waited = False
                for c in range(CC):
                    groups = (((ybh, wxh), (ybh, wxl)), ((ybl, wxh),))
                    for grp in groups:
                        for (yb_, wv) in grp:
                            for half in range(2):
                                is_first = not started[half]
                                started[half] = True
                                is_last = (c == CC - 1 and yb_ is ybl)
                                is_fend = (c == HC - 1 and yb_ is ybl)
                                wts = []
                                if is_first:
                                    wts = list(ywf) + [bank_free[bp[half]]]
                                    if first_sub and half == 0:
                                        wts += c_w_done
                                if c == HC and not back_waited:
                                    wts = wts + list(ywb)
                                    back_waited = True
                                t(lambda e, c=c, yb_=yb_, wv=wv, par=par,
                                  half=half, bp=bp, is_first=is_first,
                                  is_last=is_last: e.matmul(
                                      pb[bp[half]][:, :],
                                      yb_[par][:, c * 128:(c + 1) * 128],
                                      wv[:, c * D + half * 512:
                                         c * D + half * 512 + 512],
                                      start=is_first, stop=is_last),
                                  waits=wts,
                                  incs=([tmm.inc()]
                                        if ((is_last or is_fend)
                                            and half == 1) else []))
                    if c == HC - 1:
                        yb_free_f[par] = tmm.now()
                wo_done = tmm.now()
                yb_free_b[par] = wo_done
                pd["wo_done"] = wo_done
                pd["bp"] = bp

            def gen_x3_p1(pd):
                par = pd["par"]
                bp = pd["bp"]
                for half in range(2):
                    v(lambda e, half=half, par=par, bp=bp: e.tensor_add(
                        bufx[par][:, half * 512:half * 512 + 512],
                        pb[bp[half]][:, :],
                        bufx2[par][:, half * 512:half * 512 + 512]),
                      waits=([pd["wo_done"], pd["x2w"], x_free[par]]
                             if half == 0 else []),
                      incs=[vev.inc()])
                    bank_free[bp[half]] = vev.now()
                    if half == 0:
                        yield
                x3_done = vev.now()
                x2_free[par] = [x3_done]
                yield
                r = {}
                yield from gen_ffn_part1(bufx[par], 128, [x3_done],
                                         ZS[par], RS[par], r)
                pd["qd"] = r["qd"]

            def emitC_part2a(pd):
                pd["ed"] = emit_ffn_part2a(bufx[pd["par"]], 128, pd["qd"],
                                           ZS[pd["par"]], RS[pd["par"]],
                                           b_q=0, b_h=(3, 4))

            def gen_part2b_store(pd):
                par = pd["par"]
                trow = pd["trow"]
                r = {}
                yield from gen_ffn_part2b(bufx[par], bufx[par], 128,
                                          pd["ed"], ZS[par], (), r)
                x4_done = r["x2"]
                dma_nc(sy, lambda e, trow=trow, par=par: e.dma_start(
                    out=dout[trow:trow + 128, :], in_=bufx[par][:]),
                    sotp[par], [x4_done])
                x_free[par] = sotp[par].now()

            prev = None
            first_sub = True
            for gp in range(T_core // 256):
                if prev is not None:
                    duet(gen_x3_p1(prev[0]), gen_x3_p1(prev[1]))
                pds = []
                for gsub in (2 * gp, 2 * gp + 1):
                    pds.append(emitC_yb_loads(gsub))
                for pd in pds:
                    emitC_x2load(pd)
                for pd in pds:
                    emitC_wout(pd, first_sub)
                    first_sub = False
                if prev is not None:
                    emitC_part2a(prev[0])
                    emitC_part2a(prev[1])
                    duet(gen_part2b_store(prev[0]), gen_part2b_store(prev[1]))
                prev = pds
            duet(gen_x3_p1(prev[0]), gen_x3_p1(prev[1]))
            emitC_part2a(prev[0])
            emitC_part2a(prev[1])
            duet(gen_part2b_store(prev[0]), gen_part2b_store(prev[1]))

        if upto >= 5:
            g(None, waits=[sotp[0].now(), sotp[1].now()])
        else:
            g(None, waits=[("vv", P.streams["vv"].n), ("ss", P.streams["ss"].n),
                           ("tt", P.streams["tt"].n), ("gg", P.streams["gg"].n),
                           ("dsp0", P.streams["dsp0"].n), ("dsp1", P.streams["dsp1"].n),
                           ("dyh0", P.streams["dyh0"].n), ("dyh1", P.streams["dyh1"].n),
                           ("dyl0", P.streams["dyl0"].n), ("dyl1", P.streams["dyl1"].n),
                           ("dcc", P.streams["dcc"].n)])

        with nc.Block() as block:
            @block.sync
            def _(eng):
                P.run("sync", eng, sems)

            @block.tensor
            def _(eng):
                P.run("tensor", eng, sems)

            @block.vector
            def _(eng):
                P.run("vector", eng, sems)

            @block.scalar
            def _(eng):
                P.run("scalar", eng, sems)

            @block.gpsimd
            def _(eng):
                P.run("gpsimd", eng, sems)

    return nc


def make_core_inputs(x_flat, n_cores, T_core, weights):
    (w_in, conv_w, conv_b, w_B, w_C, w_out, wq, wsc, bias) = weights
    halves_per_seq = 2
    wq_t = np.ascontiguousarray(wq.T)
    conv_w_p = np.ascontiguousarray(
        conv_w[:, 0, :].reshape(CC, 128, 4).transpose(1, 0, 2).reshape(128, CC * 4))
    conv_b_p = np.ascontiguousarray(conv_b.reshape(CC, 128).T)
    wsb = np.ascontiguousarray(np.broadcast_to(wsc[None, :], (128, D)))
    bias_b = np.ascontiguousarray(np.broadcast_to(bias[None, :], (128, D)))
    identf = np.eye(128, dtype=np.float32)
    identb = np.eye(128, dtype=ml_dtypes.bfloat16)
    tt = np.arange(1, T_core + 1, dtype=np.float64)
    decay = np.broadcast_to((0.9 ** tt).astype(np.float32)[None, :],
                            (DS, T_core)).copy()
    in_maps = []
    for i in range(n_cores):
        r0 = i * T_core
        xc = np.ascontiguousarray(x_flat[r0:r0 + T_core])
        is_mid = (i % halves_per_seq) != 0
        if is_mid:
            xhalo = np.ascontiguousarray(x_flat[r0 - 3:r0])
            hm = np.ones((128, 3), dtype=np.float32)
            sel = np.zeros((128, DS), dtype=ml_dtypes.bfloat16)
            prev = i - 1
            for q in range(DS):
                sel[prev * DS + q, q] = 1.0
        else:
            xhalo = np.zeros((3, D), dtype=np.float32)
            hm = np.zeros((128, 3), dtype=np.float32)
            sel = np.zeros((128, DS), dtype=ml_dtypes.bfloat16)
        in_maps.append({
            "x": xc, "xhalo": xhalo, "w_in": w_in,
            "conv_w_p": conv_w_p, "conv_b_p": conv_b_p,
            "w_B": w_B, "w_C": w_C, "w_out": w_out, "wq_t": wq_t,
            "w_scale_b": wsb, "bias_b": bias_b, "halo_mask": hm,
            "sel_b": sel, "decay": decay, "ident_f": identf,
            "ident_b": identb,
        })
    return in_maps


def kernel(L0_attn_x, L0_attn_w_in, L0_attn_conv_w, L0_attn_conv_b,
           L0_attn_w_B, L0_attn_w_C, L0_attn_w_out, L0_ffn_weight_q,
           L0_ffn_weight_scale, L0_ffn_bias):
    n_cores = 8
    T_core = 4096
    T_tile = 512
    B, S, _ = L0_attn_x.shape
    x_flat = np.ascontiguousarray(
        np.asarray(L0_attn_x, dtype=np.float32).reshape(B * S, D))
    weights = (
        np.ascontiguousarray(np.asarray(L0_attn_w_in, dtype=np.float32)),
        np.ascontiguousarray(np.asarray(L0_attn_conv_w, dtype=np.float32)),
        np.ascontiguousarray(np.asarray(L0_attn_conv_b, dtype=np.float32)),
        np.ascontiguousarray(np.asarray(L0_attn_w_B, dtype=np.float32)),
        np.ascontiguousarray(np.asarray(L0_attn_w_C, dtype=np.float32)),
        np.ascontiguousarray(np.asarray(L0_attn_w_out, dtype=np.float32)),
        np.ascontiguousarray(np.asarray(L0_ffn_weight_q, dtype=np.int8)),
        np.ascontiguousarray(np.asarray(L0_ffn_weight_scale, dtype=np.float32)),
        np.ascontiguousarray(np.asarray(L0_ffn_bias, dtype=np.float32)),
    )
    nc = build_program(n_cores, T_core, T_tile)
    in_maps = make_core_inputs(x_flat, n_cores, T_core, weights)
    res = run_bass_kernel_spmd(nc, in_maps, core_ids=list(range(n_cores)))
    out = np.concatenate([np.asarray(r["out"]) for r in res.results], axis=0)
    return out.reshape(B, S, D).astype(np.float32)
